# version 03
# baseline (speedup 1.0000x reference)
import os
import numpy as np
import ml_dtypes
from contextlib import ExitStack
import concourse.bass as bass
import concourse.mybir as mybir
from concourse.bass_utils import run_bass_kernel_spmd

F32 = mybir.dt.float32
BF16 = mybir.dt.bfloat16
I32 = mybir.dt.int32
AF = mybir.ActivationFunctionType
ALU = mybir.AluOpType
AX = mybir.AxisListType

B, S, D, NH, HD, R = 4, 4096, 1024, 16, 64, 16
T = 2048
NT = 16
DC = 8
HID = 3072
NE, MH = 8, 1024
CAP = 2560.0
EPS_LN = 1e-5
EPS_QK = 1e-6
INV127 = 1.0 / 127.0
NCORES = 8
XN_ELEMS = 128 * DC * T


class Prog:
    ENGINES = ("tensor", "vector", "scalar", "gpsimd", "sync")

    def __init__(self):
        self.ops = []
        self.cnt = {}
        self.pending_barrier = {}

    def sem(self, name):
        self.cnt[name] = 0
        return name

    SELF_SERIAL = {"vector": "ve", "scalar": "ac"}

    def op(self, engine, fn, waits=(), inc=(), inc_amt=1):
        if isinstance(inc, str):
            inc = (inc,)
        w = list(self.pending_barrier.pop(engine, ())) + list(waits)
        ss = self.SELF_SERIAL.get(engine)
        if ss is not None and self.cnt[ss] > 0:
            w.append((ss, self.cnt[ss]))
        incs = []
        for snm in inc:
            self.cnt[snm] += inc_amt
            incs.append((snm, inc_amt))
        self.ops.append((engine, fn, tuple(w), tuple(incs)))
        return self.cnt[inc[0]] if inc else None

    def barrier(self, sems):
        snap = [(snm, self.cnt[snm]) for snm in sems if self.cnt[snm] > 0]
        for e in self.ENGINES:
            cur = self.pending_barrier.get(e, [])
            self.pending_barrier[e] = cur + snap

    def emit(self, block, handles):
        per = {e: [] for e in self.ENGINES}
        for rec in self.ops:
            per[rec[0]].append(rec)

        def mk(engine_name):
            def body(eng):
                for (_, fn, waits, incs) in per[engine_name]:
                    seen = {}
                    for (snm, v) in waits:
                        if seen.get(snm, -1) < v:
                            seen[snm] = v
                    for snm, v in seen.items():
                        eng.wait_ge(handles[snm], v)
                    inst = fn(eng)
                    for (snm, a) in incs:
                        inst.then_inc(handles[snm], a)
            return body

        block.tensor(mk("tensor"))
        block.vector(mk("vector"))
        block.scalar(mk("scalar"))
        block.gpsimd(mk("gpsimd"))
        block.sync(mk("sync"))


class Ctx:
    def __init__(self, nc, mode):
        self.nc = nc
        self.mode = mode
        self.p = Prog()
        for snm in ("mm", "ve", "ac", "ld", "gd", "cc"):
            self.p.sem(snm)
        self.dr = {}
        self.sb = {}

    def mm(self, out, lhsT, rhs, start=True, stop=True, waits=(), tp=None):
        def fn(e, out=out, lhsT=lhsT, rhs=rhs, start=start, stop=stop, tp=tp):
            return e.matmul(out, lhsT, rhs, start=start, stop=stop, tile_position=tp)
        return self.p.op("tensor", fn, waits=waits, inc="mm")

    def trans(self, out, in_, ident, waits=()):
        return self.p.op("tensor",
                         lambda e, o=out, i=in_, d=ident: e.transpose(o, i, d),
                         waits=waits, inc="mm")

    def act(self, out, in_, func, bias=0.0, scale=1.0, waits=()):
        def fn(e, out=out, in_=in_, func=func, bias=bias, scale=scale):
            return e.activation(out, in_, func, bias=bias, scale=scale)
        return self.p.op("scalar", fn, waits=waits, inc="ac")

    def copy(self, out, in_, waits=()):
        return self.p.op("vector", lambda e, o=out, i=in_: e.tensor_copy(o, i),
                         waits=waits, inc="ve")

    def tt(self, out, in0, in1, op, waits=()):
        return self.p.op("vector",
                         lambda e, o=out, a=in0, b=in1, op=op: e.tensor_tensor(o, a, b, op),
                         waits=waits, inc="ve")

    def ts(self, out, in0, s1, s2=None, op0=ALU.mult, op1=ALU.add, waits=()):
        if s2 is None:
            op1 = ALU.bypass
        def fn(e, o=out, a=in0, s1=s1, s2=s2, op0=op0, op1=op1):
            return e.tensor_scalar(o, a, s1, s2, op0, op1)
        return self.p.op("vector", fn, waits=waits, inc="ve")

    def stt(self, out, in0, scalar, in1, op0, op1, waits=()):
        def fn(e, o=out, a=in0, s=scalar, b=in1, op0=op0, op1=op1):
            return e.scalar_tensor_tensor(o, a, s, b, op0, op1)
        return self.p.op("vector", fn, waits=waits, inc="ve")

    def red(self, out, in_, op, negate=False, absv=False, waits=()):
        def fn(e, o=out, i=in_, op=op, neg=negate, ab=absv):
            return e.tensor_reduce(o, i, AX.X, op, apply_absolute_value=ab, negate=neg)
        return self.p.op("vector", fn, waits=waits, inc="ve")

    def recip(self, out, in_, waits=()):
        return self.p.op("vector", lambda e, o=out, i=in_: e.reciprocal(o, i),
                         waits=waits, inc="ve")

    def max8(self, out, in_, waits=()):
        return self.p.op("vector", lambda e, o=out, i=in_: e.max(out=o, in_=i),
                         waits=waits, inc="ve")

    def memset(self, ap, val, waits=()):
        return self.p.op("vector", lambda e, a=ap, v=val: e.memset(a, v),
                         waits=waits, inc="ve")

    def dma(self, dst, src, waits=(), eng="sync"):
        if len(src.shape) == 3 and len(dst.shape) == 2:
            dst = dst.rearrange("p (c w) -> p c w", c=src.shape[1])
        elif len(dst.shape) == 3 and len(src.shape) == 2:
            src = src.rearrange("p (c w) -> p c w", c=dst.shape[1])
        snm = "ld" if eng == "sync" else "gd"
        return self.p.op(eng, lambda e, d=dst, s=src: e.dma_start(out=d, in_=s),
                         waits=waits, inc=snm, inc_amt=16)

    def dma_fn(self, fn, waits=(), eng="gpsimd"):
        snm = "ld" if eng == "sync" else "gd"
        return self.p.op(eng, fn, waits=waits, inc=snm, inc_amt=16)

    def coll(self, kind, op, ins, outs, waits=()):
        def fn(e, kind=kind, op=op, ins=ins, outs=outs):
            return e.collective_compute(kind, op, replica_groups=[list(range(NCORES))],
                                        ins=ins, outs=outs)
        return self.p.op("gpsimd", fn, waits=waits, inc="cc", inc_amt=1)

    def barrier(self):
        self.p.barrier(("mm", "ve", "ac", "ld", "gd", "cc"))

    def now(self, snm):
        return self.p.cnt[snm]

    def bankf(self, i, lo=0, width=512, p0=0, p1=128):
        return self.sb["PS"][p0:p1, i * 512 + lo: i * 512 + lo + width]

    def alloc(self, name, shape, dtype, psum=False):
        fn = self.nc.psum_tensor if psum else self.nc.sbuf_tensor
        self._nalloc = getattr(self, "_nalloc", 0) + 1
        return self.scopes[-1].enter_context(fn(f"sb{self._nalloc}_{name}", shape, dtype))

    def push_scope(self):
        es = ExitStack()
        self.scopes.append(es)
        return es

    def pop_scope(self):
        self.scopes.pop().close()


def _bf(x):
    return np.ascontiguousarray(np.asarray(x), dtype=ml_dtypes.bfloat16)


def _f32(x):
    return np.ascontiguousarray(np.asarray(x), dtype=np.float32)


def feat8(v):
    return _f32(np.asarray(v, np.float32).reshape(8, 128).T)


def host_constants():
    c = {}
    c["identf"] = _f32(np.eye(128, dtype=np.float32))
    c["identb"] = _bf(np.eye(128, dtype=np.float32))
    Bm = np.eye(64, dtype=np.float32) - 1.0 / 64.0
    Z = np.zeros((64, 64), np.float32)
    c["bpair"] = _bf(np.block([[Bm, Z], [Z, Bm]]))
    J = np.ones((64, 64), np.float32)
    c["ones64"] = _bf(np.block([[J, Z], [Z, J]]))
    c["onesrow"] = _bf(np.ones((1, 128), np.float32))
    c["onesrowf"] = _f32(np.ones((1, 128), np.float32))
    c["onescol"] = _bf(np.ones((128, 1), np.float32))
    return c


def core_constants(core):
    p = core % 2
    c = {}
    c["rkall"] = np.array([[core, (core // 2) * 2, (core // 2) * 2 + 1, core ^ 1]],
                          dtype=np.int32)
    row = np.arange(128)[:, None]
    col = np.arange(256)[None, :]
    masks = []
    for i in range(4):
        valid = (i * 128 + row) <= (2 * (col // 128) * 128 + p * 128 + col % 128)
        masks.append(valid.astype(np.float32))
    c["cmask"] = _bf(np.stack(masks))
    c["flagS"] = _f32(np.full((8, 2), 1.0 if core == 0 else 0.0, np.float32))
    return c


def host_weights(inputs):
    w = {}
    w["Wq"] = _f32(inputs["L0_attn_Wq"]); w["Wk"] = _f32(inputs["L0_attn_Wk"])
    w["Wv"] = _f32(inputs["L0_attn_Wv"]); w["Wo"] = _f32(inputs["L0_attn_Wo"])
    w["bq"] = feat8(inputs["L0_attn_bq"]); w["bk"] = feat8(inputs["L0_attn_bk"])
    w["bv"] = feat8(inputs["L0_attn_bv"]); w["bo"] = feat8(inputs["L0_attn_bo"])
    for nm in ("q", "k", "v"):
        sc = np.asarray(inputs[f"L0_attn_{nm}_scale"], np.float32)
        bi = np.asarray(inputs[f"L0_attn_{nm}_bias"], np.float32)
        w[f"{nm}s128"] = _f32(np.tile(sc, 2)[:, None])
        w[f"{nm}b128"] = _f32(np.tile(bi, 2)[:, None])
        w[f"{nm}srow"] = _f32(sc[None, :])
        w[f"{nm}brow"] = _f32(bi[None, :])
    w["Wg_t"] = np.ascontiguousarray(np.asarray(inputs["L0_ffn_w_gate"], np.int32).T)
    w["Wu_t"] = np.ascontiguousarray(np.asarray(inputs["L0_ffn_w_up"], np.int32).T)
    w["Wd_t"] = np.ascontiguousarray(np.asarray(inputs["L0_ffn_w_down"], np.int32).T)
    w["Wqkv"] = _f32(inputs["L1_attn_W_qkv"])
    w["Wo1"] = _f32(inputs["L1_attn_W_o"])
    wq_lsr = np.asarray(inputs["L1_attn_W_q_lsr"], np.float32)
    wk_lsr = np.asarray(inputs["L1_attn_W_k_lsr"], np.float32)
    w["Wqlsr"] = _f32(np.transpose(wq_lsr, (1, 0, 2)).reshape(64, 256))
    w["Wklsr"] = _f32(np.transpose(wk_lsr, (1, 0, 2)).reshape(64, 256))
    w["coreT"] = _f32(np.asarray(inputs["L1_attn_core"], np.float32).T)
    w["RW"] = _f32(inputs["L1_ffn_router_w"])
    w["Rb"] = _f32(np.asarray(inputs["L1_ffn_router_b"], np.float32)[:, None])
    w["w1"] = _f32(inputs["L1_ffn_w1"])
    w["w2"] = _f32(inputs["L1_ffn_w2"])
    b1 = np.asarray(inputs["L1_ffn_b1"], np.float32)
    w["b1t"] = _f32(np.stack([b1[e].reshape(8, 128).T for e in range(NE)]))
    w["b2e"] = _f32(inputs["L1_ffn_b2"])
    return w


def own_rows(core):
    p = core % 2
    return np.concatenate([np.arange(128) + (2 * j + p) * 128 for j in range(NT)])


def declare_dram(ctx):
    nc, mode = ctx.nc, ctx.mode
    t = ctx.dr

    def inp(name, shape, dt=F32):
        t[name] = nc.dram_tensor(name, shape, dt, kind="ExternalInput")

    inp("identf", [128, 128]); inp("identb", [128, 128], BF16)
    inp("bpair", [128, 128], BF16); inp("ones64", [128, 128], BF16)
    inp("onesrow", [1, 128], BF16); inp("onesrowf", [1, 128])
    inp("onescol", [128, 1], BF16)
    inp("rkall", [1, 4], I32)
    inp("cmask", [4, 128, 256], BF16); inp("flagS", [8, 2])

    if mode in ("A", "B", "full"):
        inp("x0", [D, T])
    if mode in ("B", "full"):
        inp("Wq", [D, D]); inp("Wk", [D, D]); inp("Wv", [D, D]); inp("Wo", [D, D])
        inp("bq", [128, 8]); inp("bk", [128, 8]); inp("bv", [128, 8]); inp("bo", [128, 8])
        for nm in ("q", "k", "v"):
            inp(f"{nm}s128", [128, 1]); inp(f"{nm}b128", [128, 1])
            inp(f"{nm}srow", [1, 64]); inp(f"{nm}brow", [1, 64])
        inp("Wg_t", [D, HID], I32); inp("Wu_t", [D, HID], I32); inp("Wd_t", [HID, D], I32)
    if mode in ("B", "C", "full"):
        inp("RW", [D, 8]); inp("Rb", [8, 1])
    if mode in ("C", "full"):
        inp("Wqkv", [D, 3 * D]); inp("Wo1", [D, D])
        inp("Wqlsr", [64, 256]); inp("Wklsr", [64, 256]); inp("coreT", [16, 16])
        inp("w1", [NE, D, MH]); inp("w2", [NE, MH, D])
        inp("b1t", [NE, 128, 8]); inp("b2e", [NE, D])
    if mode == "B":
        inp("xn_own_in", [128, DC * T], BF16)
        inp("kf0_in", [128, DC * 2 * T], BF16)
    if mode == "C":
        inp("x1", [D, T])
        if os.environ.get("KDBG"):
            t["dbg_qlr"] = nc.dram_tensor("dbg_qlr", [128, 8 * T], BF16, kind="ExternalOutput")
            t["dbg_klr"] = nc.dram_tensor("dbg_klr", [128, 2 * 4096], BF16, kind="ExternalOutput")
            t["dbg_vt4"] = nc.dram_tensor("dbg_vt4", [128, 32 * 4 * 65], BF16, kind="ExternalOutput")
            t["dbg_nb1"] = nc.dram_tensor("dbg_nb1", [128, 18], F32, kind="ExternalOutput")
            t["dbg_oa1"] = nc.dram_tensor("dbg_oa1", [128, 2 * T], BF16, kind="ExternalOutput")
        inp("xn2own", [128, DC * T], BF16)
        inp("kf1_in", [128, DC * 2 * T], BF16)
        inp("Sall", [8, 2 * NCORES])

    if mode == "A":
        t["xn_out"] = nc.dram_tensor("xn_out", [128, DC * T], BF16, kind="ExternalOutput")
    if mode == "B":
        t["x1_out"] = nc.dram_tensor("x1_out", [D, T], F32, kind="ExternalOutput")
        t["xn2_out"] = nc.dram_tensor("xn2_out", [128, DC * T], BF16, kind="ExternalOutput")
        t["Sp_out"] = nc.dram_tensor("Sp_out", [8, 2], F32, kind="ExternalOutput")
        if os.environ.get("KDBG"):
            t["dbg_qh"] = nc.dram_tensor("dbg_qh", [128, T], BF16, kind="ExternalOutput")
            t["dbg_kr"] = nc.dram_tensor("dbg_kr", [128, 2 * T], BF16, kind="ExternalOutput")
            t["dbg_vt"] = nc.dram_tensor("dbg_vt", [128, 32 * 2 * 65], BF16, kind="ExternalOutput")
            t["dbg_nb"] = nc.dram_tensor("dbg_nb", [128, 1], F32, kind="ExternalOutput")
            t["dbg_oa"] = nc.dram_tensor("dbg_oa", [128, 2 * T], BF16, kind="ExternalOutput")
            t["dbg_wc"] = nc.dram_tensor("dbg_wc", [128, 2 * 1024], BF16, kind="ExternalOutput")
    if mode in ("C", "full"):
        t["y"] = nc.dram_tensor("y", [D, T], F32, kind="ExternalOutput")

    if mode == "full":
        if os.environ.get("KDBG"):
            t["dbg_sm"] = nc.dram_tensor("dbg_sm", [128, 64], F32, kind="ExternalOutput")
            t["dbg_mt"] = nc.dram_tensor("dbg_mt", [128, 144], F32, kind="ExternalOutput")
        t["cc_xn_in"] = nc.dram_tensor("cc_xn_in", [XN_ELEMS], BF16)
        t["cc_xn0_out"] = nc.dram_tensor("cc_xn0_out", [NCORES * XN_ELEMS], BF16,
                                         addr_space="Shared")
        t["cc_xn1_out"] = nc.dram_tensor("cc_xn1_out", [NCORES * XN_ELEMS], BF16,
                                         addr_space="Shared")
        t["cc_S_in"] = nc.dram_tensor("cc_S_in", [16], F32)
        t["cc_S_out"] = nc.dram_tensor("cc_S_out", [16], F32, addr_space="Shared")
        t["kf1_dram"] = nc.dram_tensor("kf1_dram", [128, DC * 2 * T], BF16)


def declare_sbuf_persistent(ctx):
    s = ctx.sb
    A = ctx.alloc
    s["x"] = A("x", [128, DC * T], F32)
    s["consts"] = A("consts", [128, 384], BF16)
    s["identf"] = A("identf", [128, 128], F32)
    s["onesrow"] = A("onesrow", [1, 128], BF16)
    s["onesrowf"] = A("onesrowf", [1, 128], F32)
    s["onescol"] = A("onescol", [128, 1], BF16)
    s["bias8"] = A("bias8", [128, 40], F32)
    s["qk128"] = A("qk128", [128, 12], F32)
    s["rows64"] = A("rows64", [1, 384], F32)
    s["rowsA"] = A("rowsA", [1, 512], F32)
    s["smalls"] = A("smalls", [128, 64], F32)
    s["negB"] = A("negB", [128, 18], F32)
    s["ints"] = A("ints", [1, 4], I32)
    s["flagS"] = A("flagS", [8, 2], F32)
    s["cmask"] = A("cmask", [128, 4 * 256], BF16)
    s["coreT"] = A("coreT", [16, 16], F32)
    s["cscaled"] = A("cscaled", [16, 16], F32)
    s["lsr"] = A("lsr", [128, 512], BF16)
    s["rbc"] = A("rbc", [128, 512], F32)
    s["rstdb"] = A("rstdb", [128, 512], BF16)
    s["bbc"] = A("bbc", [128, 512], BF16)
    s["masktok"] = A("masktok", [128, 16 + 128], F32)
    s["epsc"] = A("epsc", [128, 2], F32)
    s["biasc"] = A("biasc", [128, 24], F32)
    s["sqsl"] = A("sqsl", [128, 1024], BF16)
    s["PS"] = A("PS", [128, 3584], F32, psum=True)
    s["PSB"] = A("PSB", [128, 1024], BF16, psum=True)


def ph_load(ctx):
    s, d, mode = ctx.sb, ctx.dr, ctx.mode
    ctx.dma(s["identf"][:, :], d["identf"].ap())
    ctx.dma(s["consts"][:, 0:128], d["identb"].ap(), eng="gpsimd")
    ctx.dma(s["consts"][:, 128:256], d["bpair"].ap(), eng="gpsimd")
    ctx.dma(s["consts"][:, 256:384], d["ones64"].ap(), eng="gpsimd")
    ctx.dma(s["onesrow"][:, :], d["onesrow"].ap(), eng="gpsimd")
    ctx.dma(s["onesrowf"][:, :], d["onesrowf"].ap())
    ctx.dma(s["onescol"][:, :], d["onescol"].ap(), eng="gpsimd")
    ctx.dma(s["ints"][:, :], d["rkall"].ap())
    ctx.dma(s["cmask"].ap().rearrange("p (m w) -> p m w", m=4),
            d["cmask"].ap().rearrange("m p w -> p m w"))
    ctx.dma(s["flagS"][:, :], d["flagS"].ap())
    ctx.memset(s["epsc"][:, 0:1], EPS_LN)
    ctx.memset(s["epsc"][:, 1:2], EPS_QK)
    if mode in ("A", "B", "full"):
        ctx.dma(s["x"][:, :], d["x0"].ap().rearrange("(c p) t -> p c t", p=128))
    if mode == "C":
        ctx.dma(s["x"][:, :], d["x1"].ap().rearrange("(c p) t -> p c t", p=128))
        ctx.dma(s["bias8"][0:8, 0:16], d["Sall"].ap())
    if mode in ("B", "full"):
        for i, nm in enumerate(("bq", "bk", "bv", "bo")):
            ctx.dma(s["bias8"][:, 8 * i:8 * i + 8], d[nm].ap())
        for i, nm in enumerate(("qs128", "qb128", "ks128", "kb128", "vs128", "vb128")):
            ctx.dma(s["qk128"][:, i:i + 1], d[nm].ap())
        for i, nm in enumerate(("qsrow", "qbrow", "ksrow", "kbrow", "vsrow", "vbrow")):
            ctx.dma(s["rows64"][:, 64 * i:64 * i + 64], d[nm].ap())
    if mode in ("B", "C", "full"):
        ctx.dma(s["bias8"][0:8, 32:33], d["Rb"].ap())
    if mode in ("C", "full"):
        ctx.dma(s["coreT"][:, :], d["coreT"].ap())
        ctx.dma(s["lsr"][0:64, 0:256], d["Wqlsr"].ap(), eng="gpsimd")
        ctx.dma(s["lsr"][0:64, 256:512], d["Wklsr"].ap(), eng="gpsimd")
        ctx.dma(s["lsr"][64:128, 0:256], d["Wqlsr"].ap(), eng="gpsimd")
        ctx.dma(s["lsr"][64:128, 256:512], d["Wklsr"].ap(), eng="gpsimd")
    ctx.barrier()


def emit_ln(ctx):
    s = ctx.sb
    ctx.push_scope()
    rex = ctx.alloc("lnrows", [1, 1536], F32)
    xb = s["xn"]
    casts = []
    for dc in range(DC):
        casts.append(ctx.copy(xb[:, dc * T:(dc + 1) * T], s["x"][:, dc * T:(dc + 1) * T]))
    sq_mm = [0, 0]
    tt_last = 0
    for q in range(4):
        q0 = q * 512
        sums = ctx.bankf(0, p1=1)
        ssq = ctx.bankf(1, p1=1)
        mm_s = mm_q = 0
        for dc in range(DC):
            xc = xb[:, dc * T + q0: dc * T + q0 + 512]
            sq = s["sqsl"][:, (dc % 2) * 512:(dc % 2) * 512 + 512]
            aw = [("ve", casts[dc])]
            if sq_mm[dc % 2]:
                aw.append(("mm", sq_mm[dc % 2]))
            a1 = ctx.act(sq, xc, AF.Square, waits=aw)
            mm_s = ctx.mm(sums, s["onescol"][:, :], xc, start=(dc == 0), stop=(dc == DC - 1),
                          waits=(("ve", casts[dc]), ("ve", tt_last)))
            mm_q = ctx.mm(ssq, s["onescol"][:, :], sq, start=(dc == 0), stop=(dc == DC - 1),
                          waits=(("ac", a1),))
            sq_mm[dc % 2] = mm_q
        rA = s["rowsA"][0:1, 0:512]
        rB = rex[0:1, 0:512]
        rC = rex[0:1, 512:1024]
        rD = rex[0:1, 1024:1536]
        v1 = ctx.ts(rA, sums, 1.0 / D, None, ALU.mult, waits=(("mm", mm_s),))
        a1 = ctx.act(rB, rA, AF.Square, waits=(("ve", v1),))
        v2 = ctx.stt(rB, ssq, 1.0 / D, rB, ALU.mult, ALU.subtract,
                     waits=(("mm", mm_q), ("ac", a1)))
        a2 = ctx.act(rB, rB, AF.Sqrt, bias=s["epsc"][0:1, 0:1], waits=(("ve", v2),))
        v3 = ctx.recip(rC, rB, waits=(("ac", a2),))
        v4 = ctx.tt(rD, rA, rC, ALU.mult)
        bcA = ctx.bankf(2)
        bcB = ctx.bankf(3)
        m1 = ctx.mm(bcA, s["onesrowf"][:, :], rC, waits=(("ve", v3), ("ac", ctx.now("ac"))))
        m2 = ctx.mm(bcB, s["onesrowf"][:, :], rD, waits=(("ve", v4),))
        e1 = ctx.act(s["rstdb"][:, :], bcA, AF.Copy, waits=(("mm", m1), ("ve", tt_last)))
        e2 = ctx.act(s["bbc"][:, :], bcB, AF.Copy, waits=(("mm", m2),))
        for dc in range(DC):
            xc = xb[:, dc * T + q0: dc * T + q0 + 512]
            ctx.tt(xc, xc, s["rstdb"][:, :], ALU.mult, waits=(("ac", e2), ("mm", mm_q)))
            tt_last = ctx.tt(xc, xc, s["bbc"][:, :], ALU.subtract)
    ctx.barrier()
    ctx.pop_scope()


def emit_gather_issue(ctx, layer):
    s, d = ctx.sb, ctx.dr
    if ctx.mode != "full":
        return
    ccin = d["cc_xn_in"].ap().rearrange("(p n) -> p n", p=128)
    ccout = d["cc_xn0_out"] if layer == 0 else d["cc_xn1_out"]
    g1 = ctx.dma(ccin, s["xn"][:, :], eng="gpsimd")
    ctx.cc_xn = ctx.coll("AllGather", ALU.bypass, [d["cc_xn_in"].ap()], [ccout.ap()],
                         waits=(("gd", g1),))


def ph_assemble0(ctx):
    s, d = ctx.sb, ctx.dr
    if ctx.mode == "B":
        ctx.dma(s["kfull"][:, :], d["kf0_in"].ap())
        ctx.barrier()
        return
    agv = d["cc_xn0_out"].ap().rearrange("(r p c w) -> r p c w", r=NCORES, p=128, c=DC)
    kv = s["kfull"].ap().rearrange("p (c g w) -> p c g w", c=DC, g=2)
    cw = ctx.cc_xn

    def fetch(slot, off):
        def fn(e, slot=slot, off=off):
            reg = e.alloc_register()
            e.reg_load(reg, ctx.sb["ints"][0:1, slot:slot + 1])
            rv = e.snap(reg, min_val=0, max_val=7)
            return e.dma_start(out=kv[:, :, off, :], in_=agv[rv, :, :, :])
        return fn

    ctx.p.op("gpsimd", fetch(0, 0), waits=(("cc", cw),), inc="gd", inc_amt=16)
    ctx.p.op("gpsimd", fetch(3, 1), inc="gd", inc_amt=16)
    ctx.barrier()


def ph_assemble1(ctx):
    s, d = ctx.sb, ctx.dr
    if ctx.mode != "full":
        return
    agv = d["cc_xn1_out"].ap().rearrange("(r p c j w) -> r p c j w",
                                         r=NCORES, p=128, c=DC, j=NT)
    kv = d["kf1_dram"].ap().rearrange("p (c g w) -> p c g w", c=DC, g=2 * NT)
    cw = ctx.cc_xn

    _rv = {}

    def fetch(slot, off, dc):
        def fn(e, slot=slot, off=off, dc=dc):
            if slot not in _rv:
                reg = e.alloc_register()
                e.reg_load(reg, ctx.sb["ints"][0:1, slot:slot + 1])
                _rv[slot] = e.snap(reg, min_val=0, max_val=7)
            return e.dma_start(out=kv[:, dc, off:2 * NT:2, :], in_=agv[_rv[slot], :, dc, :, :])
        return fn

    for dc in range(DC):
        ctx.p.op("gpsimd", fetch(1, 0, dc), waits=(("cc", cw),) if dc == 0 else (),
                 inc="gd", inc_amt=16)
        ctx.p.op("gpsimd", fetch(2, 1, dc), inc="gd", inc_amt=16)
    ctx.barrier()


def emit_ffn0(ctx):
    s, d = ctx.sb, ctx.dr
    ctx.push_scope()
    H = ctx.alloc("ffnH", [128, 24 * 1024], BF16)
    wA = ctx.alloc("ffnWA", [128, 8 * 1024], BF16)
    wB = ctx.alloc("ffnWB", [128, 8 * 1024], BF16)
    wC = ctx.alloc("ffnWC", [128, 6 * 1024], BF16)
    sg = ctx.alloc("ffnSG", [128, 1024], BF16)
    wfree = [0, 0]
    dfree = [0, 0]
    h_done = 0
    for th in range(2):
        t0 = th * 1024
        for blk in range(6):
            src_g = d["Wg_t"].ap()[:, blk * 512:(blk + 1) * 512].rearrange(
                "(c p) h -> p c h", p=128)
            src_u = d["Wu_t"].ap()[:, blk * 512:(blk + 1) * 512].rearrange(
                "(c p) h -> p c h", p=128)
            half = (blk % 2) * 4096
            w = (("mm", wfree[blk % 2]),) if wfree[blk % 2] else ()
            gdg = ctx.dma(wA[:, half:half + 4096], src_g, waits=w, eng="gpsimd")
            gdu = ctx.dma(wB[:, half:half + 4096], src_u, waits=w, eng="gpsimd")
            for hq in range(4):
                hc = blk * 4 + hq
                mmg = mmu = 0
                for sub in range(2):
                    gb = ctx.bankf(sub)
                    for dc in range(DC):
                        lhs_g = wA[:, half + dc * 512 + hq * 128:
                                   half + dc * 512 + hq * 128 + 128]
                        rhs = s["xn"][:, dc * T + t0 + sub * 512:
                                      dc * T + t0 + sub * 512 + 512]
                        wt = []
                        if dc == 0 and sub == 0 and hq == 0:
                            wt.append(("gd", gdg))
                        if dc == 0:
                            wt.append(("ac", ctx.now("ac")))
                        mmg = ctx.mm(gb, lhs_g, rhs, start=(dc == 0), stop=(dc == DC - 1),
                                     waits=wt)
                for sub in range(2):
                    ub = ctx.bankf(2 + sub)
                    for dc in range(DC):
                        lhs_u = wB[:, half + dc * 512 + hq * 128:
                                   half + dc * 512 + hq * 128 + 128]
                        rhs = s["xn"][:, dc * T + t0 + sub * 512:
                                      dc * T + t0 + sub * 512 + 512]
                        wt = []
                        if dc == 0 and sub == 0 and hq == 0:
                            wt.append(("gd", gdu))
                        if dc == 0:
                            wt.append(("ve", ctx.now("ve")))
                        mmu = ctx.mm(ub, lhs_u, rhs, start=(dc == 0), stop=(dc == DC - 1),
                                     waits=wt)
                a1 = ctx.act(sg[:, :], ctx.sb["PS"][:, 0:1024], AF.Silu, scale=INV127,
                             waits=(("mm", mmg),))
                h_done = ctx.stt(H[:, hc * 1024:(hc + 1) * 1024],
                                 ctx.sb["PS"][:, 1024:2048], INV127, sg[:, :],
                                 ALU.mult, ALU.mult, waits=(("mm", mmu), ("ac", a1)))
            wfree[blk % 2] = ctx.now("mm")
        dbank_rd = getattr(ctx, "_ffn_dbank", [0, 0])
        for dco in range(DC):
            src = d["Wd_t"].ap()[:, dco * 128:(dco + 1) * 128].rearrange(
                "(c p) o -> p c o", p=128)
            half = (dco % 2) * 3072
            w = (("mm", dfree[dco % 2]),) if dfree[dco % 2] else ()
            gdd = ctx.dma(wC[:, half:half + 3072], src, waits=w, eng="gpsimd")
            for sub in range(2):
                zb = ctx.bankf(4 + sub)
                mmd = 0
                for hc in range(24):
                    lhs = wC[:, half + hc * 128: half + hc * 128 + 128]
                    rhs = H[:, hc * 1024 + sub * 512: hc * 1024 + sub * 512 + 512]
                    wt = []
                    if hc == 0:
                        wt = [("gd", gdd), ("ve", h_done)]
                        if dbank_rd[sub]:
                            wt.append(("ve", dbank_rd[sub]))
                    mmd = ctx.mm(zb, lhs, rhs, start=(hc == 0), stop=(hc == 23), waits=wt)
                xc = s["x"][:, dco * T + t0 + sub * 512: dco * T + t0 + sub * 512 + 512]
                dbank_rd[sub] = ctx.stt(xc, zb, INV127, xc, ALU.mult, ALU.add,
                                        waits=(("mm", mmd),))
            dfree[dco % 2] = ctx.now("mm")
        ctx._ffn_dbank = dbank_rd
    ctx.barrier()
    ctx.pop_scope()


def emit_negB_l0(ctx):
    s = ctx.sb
    r = s["rowsA"]
    outs = []
    for i, (so, bo) in enumerate(((0, 64), (128, 192))):
        srow = s["rows64"][0:1, so:so + 64]
        brow = s["rows64"][0:1, bo:bo + 64]
        ms = r[0:1, 0:1]
        nb = r[0:1, 1:2]
        v1 = ctx.red(ms, srow, ALU.max, absv=True)
        ctx.tt(r[0:1, 4:68], brow, brow, ALU.mult)
        v3 = ctx.red(nb, r[0:1, 4:68], ALU.add)
        a1 = ctx.act(nb, nb, AF.Sqrt, waits=(("ve", v3),))
        out = r[0:1, 2 + i:3 + i]
        outs.append(ctx.stt(out, ms, 8.0, nb, ALU.mult, ALU.add,
                            waits=(("ac", a1), ("ve", v1))))
    ctx.tt(r[0:1, 0:1], r[0:1, 2:3], r[0:1, 3:4], ALU.mult, waits=(("ve", outs[1]),))
    vB = ctx.ts(r[0:1, 0:1], r[0:1, 0:1], -0.125, None, ALU.mult)
    m1 = ctx.mm(ctx.bankf(6, width=1), s["onesrowf"][:, :], r[0:1, 0:1], waits=(("ve", vB),))
    ctx.copy(s["negB"][:, 0:1], ctx.bankf(6, width=1), waits=(("mm", m1),))
    ctx.ts(s["smalls"][:, 0:1], s["qk128"][:, 0:1], 0.125, None, ALU.mult)
    ctx.ts(s["smalls"][:, 1:2], s["qk128"][:, 1:2], 0.125, None, ALU.mult)
    vb0 = ctx.copy(s["sqsl"][:, 0:24], s["bias8"][:, 0:24])
    mb0 = ctx.mm(ctx.bankf(5, width=24), s["consts"][:, 256:384], s["sqsl"][:, 0:24],
                 waits=(("ve", vb0),))
    ctx.stt(s["biasc"][:, 0:24], ctx.bankf(5, width=24), -1.0 / 64.0, s["bias8"][:, 0:24],
            ALU.mult, ALU.add, waits=(("mm", mb0),))
    ctx.barrier()


def _flush_outproj(ctx, wo_dma, wC, oall, npairs=2):
    s = ctx.sb
    oready = ctx.now("ve")
    bank_rd = [0, 0]
    for dco in range(DC):
        for sub in range(4):
            zb = ctx.bankf(sub % 2)
            mmz = 0
            for i in range(npairs):
                lhs = wC[:, i * 1024 + dco * 128: i * 1024 + dco * 128 + 128]
                rhs = oall[:, i * T + sub * 512: i * T + sub * 512 + 512]
                wt = []
                if i == 0:
                    if bank_rd[sub % 2]:
                        wt.append(("ve", bank_rd[sub % 2]))
                    if dco == 0 and sub == 0:
                        wt += [("ve", oready), ("gd", wo_dma), ("ac", ctx.now("ac"))]
                mmz = ctx.mm(zb, lhs, rhs, start=(i == 0), stop=(i == npairs - 1),
                             waits=wt)
            xc = s["x"][:, dco * T + sub * 512: dco * T + sub * 512 + 512]
            bank_rd[sub % 2] = ctx.stt(xc, zb, 1.0, xc, ALU.mult, ALU.add,
                                       waits=(("mm", mmz),))
    ctx.barrier()


def emit_l0_attn(ctx):
    s, d = ctx.sb, ctx.dr
    ctx.push_scope()
    kfull = ctx.alloc("kfull", [128, DC * 2 * T], BF16)
    s["kfull"] = kfull
    ph_assemble0(ctx)
    kraw = ctx.alloc("kraw", [128, 2 * T], BF16)
    vhat = ctx.alloc("vhat", [128, 2 * T], BF16)
    vtm = ctx.alloc("vtm", [128, 32 * 2 * 65], BF16)
    qhat = ctx.alloc("qhat", [128, T], BF16)
    oall = ctx.alloc("oall", [128, 2 * T], BF16)
    wA = ctx.alloc("attnWA", [128, 3 * 1024], BF16)
    wC = ctx.alloc("attnWC", [128, 2 * 1024], BF16)
    psb = ctx.alloc("psb", [128, 2 * 1024], BF16)

    negb = s["negB"][:, 0:1]
    bconst = s["consts"][:, 128:256]
    oconst = s["consts"][:, 256:384]
    evac_tok = [0, 0]
    sqw = [0, 0]
    exp_done = [0, 0]
    pv_done = [0, 0]
    o_norm = 0
    wo_dma = 0
    wo_free = 0

    def pipe(base, raw, dst, n_tok, bcol, scol, b2col, srcf, gwait):
        nonlocal evac_tok, sqw
        last_ve = 0
        prev_vc = 0
        prev_a3 = 0
        for ss in range(n_tok // 512):
            pb = ctx.bankf(ss % 2)
            mmp = 0
            for dc in range(DC):
                lhs = wA[:, base + dc * 128: base + dc * 128 + 128]
                wt = []
                if dc == 0 and ss == 0:
                    wt.append(("gd", gwait))
                    wt.append(("ac", ctx.now("ac")))
                if dc == 0 and evac_tok[ss % 2]:
                    wt.append(("ac", evac_tok[ss % 2]))
                mmp = ctx.mm(pb, lhs, srcf(dc, ss), start=(dc == 0), stop=(dc == DC - 1),
                             waits=wt)
            a1 = ctx.act(raw[:, ss * 512:(ss + 1) * 512], pb, AF.Copy,
                         waits=(("mm", mmp), ("ve", last_ve)))
            evac_tok[ss % 2] = a1
            cb = ctx.bankf(2)
            mc = ctx.mm(cb, bconst, raw[:, ss * 512:(ss + 1) * 512],
                        waits=[("ac", a1)] + ([("ve", prev_vc)] if prev_vc else []))
            vc = ctx.ts(raw[:, ss * 512:(ss + 1) * 512], cb, bcol, None, ALU.add,
                        waits=(("mm", mc),))
            prev_vc = vc
            sq = s["sqsl"][:, (ss % 2) * 512:(ss % 2) * 512 + 512]
            a2 = ctx.act(sq, raw[:, ss * 512:(ss + 1) * 512], AF.Square,
                         waits=[("ve", vc)] + ([("mm", sqw[ss % 2])] if sqw[ss % 2] else []))
            vb = ctx.bankf(3)
            mv = ctx.mm(vb, oconst, sq,
                        waits=[("ac", a2)] + ([("ac", prev_a3)] if prev_a3 else []))
            sqw[ss % 2] = mv
            a3 = ctx.act(s["rbc"][:, :], vb, AF.Sqrt, scale=1.0 / 64.0,
                         bias=s["epsc"][:, 1:2], waits=(("mm", mv), ("ve", last_ve)))
            prev_a3 = a3
            ctx.recip(s["rstdb"][:, :], s["rbc"][:, :], waits=(("ac", a3),))
            ctx.tt(raw[:, ss * 512:(ss + 1) * 512], raw[:, ss * 512:(ss + 1) * 512],
                   s["rstdb"][:, :], ALU.mult)
            last_ve = ctx.ts(dst[:, ss * 512:(ss + 1) * 512],
                             raw[:, ss * 512:(ss + 1) * 512],
                             scol, b2col, ALU.mult, ALU.add)
        return last_ve

    _nhp = int(os.environ.get("KHP", str(NH // 2)))
    for hp in range(_nhp):
        wq = d["Wq"].ap()[:, hp * 128:(hp + 1) * 128].rearrange("(c p) o -> p c o", p=128)
        wk = d["Wk"].ap()[:, hp * 128:(hp + 1) * 128].rearrange("(c p) o -> p c o", p=128)
        wv = d["Wv"].ap()[:, hp * 128:(hp + 1) * 128].rearrange("(c p) o -> p c o", p=128)
        wo = d["Wo"].ap()[hp * 128:(hp + 1) * 128, :]
        wfree = ctx.now("mm")
        g1 = ctx.dma(wA[:, 0:1024], wq, waits=(("mm", wfree),), eng="gpsimd")
        g2 = ctx.dma(wA[:, 1024:2048], wk, eng="gpsimd")
        g3 = ctx.dma(wA[:, 2048:3072], wv, eng="gpsimd")
        wo_w = (("mm", wo_free),) if (hp % 2 == 0 and hp > 0) else ()
        g4 = ctx.dma(wC[:, (hp % 2) * 1024:(hp % 2) * 1024 + 1024], wo,
                     waits=wo_w, eng="gpsimd")
        if hp % 2 == 1:
            wo_dma = g4

        def ksrc(dc, ss):
            return kfull[:, dc * 2 * T + ss * 512: dc * 2 * T + ss * 512 + 512]

        kv_done = pipe(1024, kraw, kraw, 2 * T, s["biasc"][:, 8 + hp:9 + hp],
                       s["qk128"][:, 2:3], s["qk128"][:, 3:4], ksrc, g2)
        q_done = pipe(0, qhat, qhat, T, s["biasc"][:, 0 + hp:1 + hp],
                      s["smalls"][:, 0:1], s["smalls"][:, 1:2], ksrc, g1)
        v_done = pipe(2048, vhat, vhat, 2 * T, s["biasc"][:, 16 + hp:17 + hp],
                      s["qk128"][:, 4:5], s["qk128"][:, 5:6], ksrc, g3)
        ms1 = ctx.memset(vtm[:, :], 1.0, waits=(("mm", ctx.now("mm")),))
        pv8 = s["PSB"].ap().rearrange("p (j w) -> p j w", j=8)
        vv = vtm.ap().rearrange("p (c h w) -> p c h w", c=32, h=2)
        tr_ev = 0
        for kb in range(8):
            tr = 0
            for j in range(4):
                kc = kb * 4 + j
                wt = []
                if kb == 0 and j == 0:
                    wt.append(("ve", v_done))
                if kb >= 1 and j == 0:
                    wt.append(("ve", tr_ev))
                tr = ctx.trans(s["PSB"][:, j * 128:(j + 1) * 128],
                               vhat[:, kc * 128:(kc + 1) * 128],
                               s["consts"][:, 0:128], waits=wt)
            w0 = (("mm", tr), ("ve", ms1))
            ctx.copy(vv[:, kb * 4:kb * 4 + 4, 0, 0:64], pv8[:, 0:4, 0:64], waits=w0)
            tr_ev = ctx.copy(vv[:, kb * 4:kb * 4 + 4, 1, 0:64], pv8[:, 0:4, 64:128])
        vtm_done = tr_ev

        for qg in range(4):
            q0 = qg * 512
            ob = [ctx.bankf(4, p1=65), ctx.bankf(5, p1=65)]
            for kc in range(32):
                pb = kc % 2
                wt = []
                if kc == 0 and qg == 0:
                    wt += [("ve", q_done), ("ve", kv_done)]
                if exp_done[pb]:
                    wt.append(("ac", exp_done[pb]))
                ctx.mm(ctx.bankf(2 * pb), kraw[0:64, kc * 128:(kc + 1) * 128],
                       qhat[0:64, q0:q0 + 512], tp=(0, 0), waits=wt)
                m1 = ctx.mm(ctx.bankf(2 * pb + 1), kraw[64:128, kc * 128:(kc + 1) * 128],
                            qhat[64:128, q0:q0 + 512], tp=(64, 0))
                wexp = [("mm", m1)]
                if pv_done[pb]:
                    wexp.append(("mm", pv_done[pb]))
                ea = ctx.act(psb[:, pb * 1024:(pb + 1) * 1024],
                             ctx.sb["PS"][:, pb * 1024:(pb + 1) * 1024],
                             AF.Exp, bias=negb, waits=wexp)
                exp_done[pb] = ea
                wv0 = [("ac", ea)]
                if kc == 0:
                    wv0.append(("ve", vtm_done))
                    if o_norm:
                        wv0.append(("ve", o_norm))
                ctx.mm(ob[0], vtm[:, (kc * 2) * 65:(kc * 2) * 65 + 65],
                       psb[:, pb * 1024:pb * 1024 + 512],
                       start=(kc == 0), stop=(kc == 31), waits=wv0)
                pv_done[pb] = ctx.mm(ob[1], vtm[:, (kc * 2 + 1) * 65:(kc * 2 + 1) * 65 + 65],
                                     psb[:, pb * 1024 + 512:pb * 1024 + 1024],
                                     start=(kc == 0), stop=(kc == 31))
            for h in range(2):
                zrow = ctx.sb["PS"][64:65, (4 + h) * 512:(4 + h) * 512 + 512]
                rr = s["rowsA"][0:1, 0:512]
                ctx.ts(rr, zrow, 1e-38, None, ALU.add, waits=(("mm", pv_done[1]),))
                v2 = ctx.recip(rr, rr)
                bc = ctx.bankf(6, p1=64)
                m1 = ctx.mm(bc, s["onesrowf"][0:1, 0:64], rr, waits=(("ve", v2),))
                v3 = ctx.copy(s["rbc"][0:64, 0:512], bc, waits=(("mm", m1),))
                dst = oall[h * 64:h * 64 + 64, (hp % 2) * T + q0:(hp % 2) * T + q0 + 512]
                o_norm = ctx.tt(dst, ctx.sb["PS"][0:64, (4 + h) * 512:(4 + h) * 512 + 512],
                                s["rbc"][0:64, 0:512], ALU.mult,
                                waits=(("ve", v3), ("mm", getattr(ctx, "_flush_mm", 0))))
        if os.environ.get("KDBG") and ctx.mode == "B" and hp == _nhp - 1:
            ctx.barrier()
            ctx.dma(d["dbg_qh"].ap(), qhat[:, :])
            ctx.dma(d["dbg_kr"].ap(), kraw[:, :])
            ctx.dma(d["dbg_vt"].ap(), vtm[:, :])
            ctx.dma(d["dbg_nb"].ap(), s["negB"][:, 0:1])
            ctx.dma(d["dbg_oa"].ap(), oall[:, :])
            ctx.dma(d["dbg_wc"].ap(), wC[:, :])
            ctx.barrier()
        if hp % 2 == 1:
            _flush_outproj(ctx, wo_dma, wC, oall, npairs=2)
            exp_done = [0, 0]
            pv_done = [0, 0]
            o_norm = 0
            evac_tok = [0, 0]
            sqw = [0, 0]
        wo_free = ctx.now("mm")
    for dco in range(DC):
        xc = s["x"][:, dco * T:(dco + 1) * T]
        ctx.ts(xc, xc, 1.0, s["bias8"][:, 24 + dco:25 + dco], ALU.mult, ALU.add)
    ctx.barrier()
    ctx.pop_scope()


def emit_router_S(ctx):
    s, d = ctx.sb, ctx.dr
    ctx.push_scope()
    wR = ctx.alloc("wR", [128, 64], BF16)
    t12 = ctx.alloc("t12", [128, 16], BF16)
    g1 = ctx.dma(wR[:, :], d["RW"].ap().rearrange("(c p) e -> p c e", p=128),
                 eng="gpsimd")
    lg_done = 0
    for sub in range(4):
        lb = ctx.bankf(0, p1=8)
        mml = 0
        for dc in range(DC):
            lhs = wR[:, dc * 8:(dc + 1) * 8]
            rhs = s["xn"][:, dc * T + sub * 512: dc * T + sub * 512 + 512]
            wt = []
            if dc == 0 and sub == 0:
                wt.append(("gd", g1))
            if dc == 0 and sub >= 1:
                wt.append(("ve", lg_done))
            mml = ctx.mm(lb, lhs, rhs, start=(dc == 0), stop=(dc == DC - 1), waits=wt)
        lg_done = ctx.ts(s["maskfm"][0:8, sub * 512:(sub + 1) * 512], lb,
                         s["bias8"][0:8, 32:33], None, ALU.add, waits=(("mm", mml),))
    s0_done = s1_done = 0
    prev_ev = 0
    for ti in range(NT):
        tb = ctx.sb["PS"][0:128, 512:512 + 8]
        wt = [("ve", lg_done)] if ti == 0 else [("ve", prev_ev)]
        mt = ctx.trans(tb, s["maskfm"][0:8, ti * 128:(ti + 1) * 128],
                       s["identf"][0:8, 0:8], waits=wt)
        lt = s["masktok"][:, 0:8]
        ctx.copy(lt, tb, waits=(("mm", mt),))
        m8 = s["masktok"][:, 8:16]
        ctx.max8(m8, lt)
        oh1 = s["smalls"][:, 8:16]
        ctx.ts(oh1, lt, m8[:, 0:1], None, ALU.is_ge)
        l2 = s["smalls"][:, 16:24]
        ctx.stt(l2, oh1, -1e9, lt, ALU.mult, ALU.add)
        oh2 = s["smalls"][:, 24:32]
        ctx.ts(oh2, l2, m8[:, 1:2], None, ALU.is_ge)
        dd = s["smalls"][:, 32:33]
        v5 = ctx.tt(dd, m8[:, 1:2], m8[:, 0:1], ALU.subtract)
        e2 = s["smalls"][:, 33:34]
        a1 = ctx.act(e2, dd, AF.Exp, waits=(("ve", v5),))
        p1 = s["smalls"][:, 34:35]
        ctx.ts(p1, e2, 1.0, None, ALU.add, waits=(("ac", a1),))
        ctx.recip(p1, p1)
        p2 = s["smalls"][:, 35:36]
        ctx.ts(p2, p1, -1.0, 1.0, ALU.mult, ALU.add)
        t1 = s["smalls"][:, 40:48]
        t2 = s["smalls"][:, 48:56]
        ctx.ts(t1, oh1, p1[:, 0:1], None, ALU.mult)
        ctx.ts(t2, oh2, p2[:, 0:1], None, ALU.mult)
        mrow = s["masktok"][:, 16 + ti * 8: 16 + ti * 8 + 8]
        ctx.tt(mrow, t1, t2, ALU.add)
        wb = (("mm", s1_done),) if ti > 0 else ()
        v12 = ctx.copy(t12[:, 0:8], t1, waits=wb)
        prev_ev = ctx.copy(t12[:, 8:16], t2)
        s0_done = ctx.mm(ctx.sb["PS"][0:8, 2 * 512:2 * 512 + 1], t12[:, 0:8],
                         s["onescol"][:, :], start=(ti == 0), stop=(ti == NT - 1),
                         waits=(("ve", v12),))
        s1_done = ctx.mm(ctx.sb["PS"][0:8, 3 * 512:3 * 512 + 1], t12[:, 8:16],
                         s["onescol"][:, :], start=(ti == 0), stop=(ti == NT - 1),
                         waits=(("ve", prev_ev),))
    ctx.copy(s["smalls"][0:8, 60:61], ctx.sb["PS"][0:8, 2 * 512:2 * 512 + 1],
             waits=(("mm", s1_done),))
    vS1 = ctx.copy(s["smalls"][0:8, 61:62], ctx.sb["PS"][0:8, 3 * 512:3 * 512 + 1])
    if ctx.mode == "full":
        g2 = ctx.dma(d["cc_S_in"].ap().rearrange("(e k) -> e k", e=8),
                     s["smalls"][0:8, 60:62], waits=(("ve", vS1),), eng="gpsimd")
        ctx.cc_S = ctx.coll("AllReduce", ALU.add, [d["cc_S_in"].ap()],
                            [d["cc_S_out"].ap()], waits=(("gd", g2),))
    if ctx.mode == "B":
        ctx.dma(d["Sp_out"].ap(), s["smalls"][0:8, 60:62], waits=(("ve", vS1),))
    ctx.barrier()
    ctx.pop_scope()


def emit_mask_final(ctx):
    s, d = ctx.sb, ctx.dr
    if ctx.mode == "full":
        gl = ctx.dma(s["smalls"][0:8, 62:64],
                     d["cc_S_out"].ap().rearrange("(e k) -> e k", e=8),
                     waits=(("cc", ctx.cc_S),))
        wS = (("ld", gl),)
    else:
        ctx.red(s["smalls"][0:8, 62:63], s["bias8"][0:8, 0:8], ALU.add)
        v2 = ctx.red(s["smalls"][0:8, 63:64], s["bias8"][0:8, 8:16], ALU.add)
        wS = (("ve", v2),)
    vf = ctx.tt(s["smalls"][0:8, 62:64], s["smalls"][0:8, 62:64], s["flagS"][:, :],
                ALU.mult, waits=wS)
    va = ctx.tt(s["masktok"][0:8, 16:18], s["masktok"][0:8, 16:18],
                s["smalls"][0:8, 62:64], ALU.add, waits=(("ve", vf),))
    vc = ctx.ts(s["masktok"][:, 16:24], s["masktok"][:, 16:24], CAP, None, ALU.min,
                waits=(("ve", va),))
    tr = vc
    for ti in range(NT):
        tb = ctx.sb["PS"][0:8, 512:512 + 128]
        mt = ctx.trans(tb, s["masktok"][:, 16 + ti * 8:16 + ti * 8 + 8],
                       s["identf"][:, :], waits=(("ve", tr),))
        tr = ctx.copy(s["maskfm"][0:8, ti * 128:(ti + 1) * 128], tb, waits=(("mm", mt),))
    ctx.copy(s["maskfmb"][0:8, :], s["maskfm"][0:8, :])
    ctx.barrier()


def emit_moe(ctx):
    s, d = ctx.sb, ctx.dr
    ctx.push_scope()
    Hm = ctx.alloc("moeH", [128, 8 * 1024], BF16)
    wA = ctx.alloc("moeWA", [128, 4 * 1024], BF16)
    wB = ctx.alloc("moeWB", [128, 4 * 1024], BF16)
    mbc = ctx.alloc("moeMBC", [128, 1024], F32)
    mbcb = ctx.alloc("moeMBCB", [128, 1024], BF16)
    ttmp = ctx.alloc("moeT", [128, 1024], BF16)
    b1sb = ctx.alloc("b1sb", [128, 8], F32)
    b2sb = ctx.alloc("b2sb", [8, D], BF16)
    mr0 = ctx.alloc("mrow0", [1, 1024], F32)
    mmlast = 0
    bcfree = 0
    bc_ve = 0
    b1_free = 0
    for ei in range(NE):
        gb1 = ctx.dma(b1sb[:, :], d["b1t"].ap()[ei, :, :],
                      waits=(("ac", b1_free),) if b1_free else (), eng="gpsimd")
        for th in range(2):
            t0 = th * 1024
            mb_done = 0
            gmr = ctx.dma(mr0[:, :], s["maskfm"][ei:ei + 1, t0:t0 + 1024],
                          waits=(("mm", bcfree),) if bcfree else (), eng="gpsimd")
            for sub in range(2):
                bc = ctx.bankf(6)
                wt = [("gd", gmr)]
                if bc_ve:
                    wt.append(("ve", bc_ve))
                m1 = ctx.mm(bc, s["onesrowf"][:, :],
                            mr0[0:1, sub * 512: sub * 512 + 512], waits=wt)
                bc_ve = ctx.copy(mbc[:, sub * 512:(sub + 1) * 512], bc, waits=(("mm", m1),))
                mb_done = ctx.copy(mbcb[:, sub * 512:(sub + 1) * 512],
                                   mbc[:, sub * 512:(sub + 1) * 512])
            bcfree = ctx.now("mm")
            for wh in range(2):
                w1s = d["w1"].ap()[ei, :, wh * 512:(wh + 1) * 512].rearrange(
                    "(c p) h -> p c h", p=128)
                wt = (("mm", mmlast),) if (ei > 0 or th > 0 or wh > 0) else ()
                gw1 = ctx.dma(wA[:, :], w1s, waits=wt, eng="gpsimd")
                for hq in range(4):
                    hc = wh * 4 + hq
                    gset = 2 * (hc % 2)
                    mmg = 0
                    for sub in range(2):
                        gb = ctx.bankf(gset + sub)
                        for dc in range(DC):
                            lhs = wA[:, dc * 512 + hq * 128: dc * 512 + hq * 128 + 128]
                            rhs = s["xn"][:, dc * T + t0 + sub * 512:
                                          dc * T + t0 + sub * 512 + 512]
                            wtm = []
                            if dc == 0 and sub == 0 and hq == 0:
                                wtm.append(("gd", gw1))
                            if dc == 0:
                                wtm.append(("ve", ctx.now("ve")))
                            mmg = ctx.mm(gb, lhs, rhs, start=(dc == 0),
                                         stop=(dc == DC - 1), waits=wtm)
                    vt = ctx.tt(ttmp[:, :], ctx.sb["PS"][:, gset * 512:gset * 512 + 1024],
                                mbc[:, :], ALU.mult, waits=(("mm", mmg), ("ve", mb_done)))
                    wg = [("ve", vt)]
                    if ei == 0 and th == 0 and hc == 0:
                        wg.append(("gd", gb1))
                    a1 = ctx.act(Hm[:, hc * 1024:(hc + 1) * 1024], ttmp[:, :], AF.Gelu,
                                 bias=b1sb[:, hc:hc + 1], waits=wg)
                    b1_free = a1
                    ctx.tt(Hm[:, hc * 1024:(hc + 1) * 1024],
                           Hm[:, hc * 1024:(hc + 1) * 1024], mbcb[:, :], ALU.mult,
                           waits=(("ac", a1),))
                mmlast = ctx.now("mm")
            h_done = ctx.now("ve")
            for zh in range(2):
                w2s = d["w2"].ap()[ei, :, zh * 512:(zh + 1) * 512].rearrange(
                    "(c p) o -> p c o", p=128)
                gw2 = ctx.dma(wB[:, :], w2s, waits=(("mm", mmlast),), eng="gpsimd")
                for dq in range(4):
                    dco = zh * 4 + dq
                    for sub in range(2):
                        zb = ctx.bankf(4 + (sub % 2))
                        mmz = 0
                        for hc in range(NE):
                            lhs = wB[:, hc * 512 + dq * 128: hc * 512 + dq * 128 + 128]
                            rhs = Hm[:, hc * 1024 + sub * 512: hc * 1024 + sub * 512 + 512]
                            wtm = []
                            if hc == 0:
                                wtm.append(("ve", ctx.now("ve")))
                            if hc == 0 and dq == 0 and sub == 0:
                                wtm += [("gd", gw2), ("ve", h_done)]
                            mmz = ctx.mm(zb, lhs, rhs, start=(hc == 0), stop=(hc == NE - 1),
                                         waits=wtm)
                        xc = s["x"][:, dco * T + t0 + sub * 512:
                                    dco * T + t0 + sub * 512 + 512]
                        ctx.stt(xc, zb, 1.0, xc, ALU.mult, ALU.add, waits=(("mm", mmz),))
                mmlast = ctx.now("mm")
    gb2 = ctx.dma(b2sb[:, :], d["b2e"].ap(), eng="gpsimd")
    bank_rd2 = [0, 0]
    for dco in range(DC):
        for sub in range(4):
            zb = ctx.bankf(sub % 2)
            wt = []
            if dco == 0 and sub == 0:
                wt = [("gd", gb2), ("mm", mmlast), ("ve", ctx.now("ve"))]
            if bank_rd2[sub % 2]:
                wt.append(("ve", bank_rd2[sub % 2]))
            mmz = ctx.mm(zb, b2sb[0:8, dco * 128:(dco + 1) * 128],
                         s["maskfmb"][0:8, sub * 512:(sub + 1) * 512], waits=wt)
            xc = s["x"][:, dco * T + sub * 512: dco * T + sub * 512 + 512]
            bank_rd2[sub % 2] = ctx.stt(xc, zb, 1.0, xc, ALU.mult, ALU.add,
                                        waits=(("mm", mmz),))
    ctx.barrier()
    ctx.pop_scope()


def emit_qlr_prep(ctx):
    s, d = ctx.sb, ctx.dr
    ctx.push_scope()
    qraw = ctx.alloc("qraw", [128, T], BF16)
    wA = ctx.alloc("qlrWA", [128, 1024], BF16)
    ctx.ts(s["cscaled"][0:16, :], s["coreT"][0:16, :], 0.25, None, ALU.mult)
    evac = [0, 0]
    for hp in range(NH // 2):
        wqc = d["Wqkv"].ap()[:, hp * 128:(hp + 1) * 128].rearrange("(c p) o -> p c o", p=128)
        g1 = ctx.dma(wA[:, :], wqc, waits=(("mm", ctx.now("mm")),), eng="gpsimd")
        ap_done = 0
        for ss in range(4):
            pb = ctx.bankf(ss % 2)
            mmp = 0
            for dc in range(DC):
                lhs = wA[:, dc * 128:(dc + 1) * 128]
                rhs = s["xn"][:, dc * T + ss * 512: dc * T + ss * 512 + 512]
                wt = []
                if dc == 0 and ss == 0:
                    wt.append(("gd", g1))
                if dc == 0 and evac[ss % 2]:
                    wt.append(("ac", evac[ss % 2]))
                mmp = ctx.mm(pb, lhs, rhs, start=(dc == 0), stop=(dc == DC - 1), waits=wt)
            ap_done = ctx.act(qraw[:, ss * 512:(ss + 1) * 512], pb, AF.Copy,
                              waits=(("mm", mmp), ("ve", ctx.now("ve"))))
            evac[ss % 2] = ap_done
        for h01 in range(2):
            h = hp * 2 + h01
            rb = h01 * 64
            for ss in range(4):
                lb = ctx.sb["PS"][0:16, 2 * 512: 2 * 512 + 512]
                mq = ctx.mm(lb, s["lsr"][rb:rb + 64, h * 16: h * 16 + 16],
                            qraw[rb:rb + 64, ss * 512:(ss + 1) * 512],
                            waits=(("ac", ap_done), ("ve", ctx.now("ve"))),
                            tp=(rb, 0) if rb else None)
                dst = s["qlr"][rb:rb + 16, hp * T + ss * 512: hp * T + ss * 512 + 512]
                ctx.ts(dst, lb, s["cscaled"][0:16, h:h + 1] if rb == 0 else
                       s["cscaled"][0:16, h:h + 1], None, ALU.mult, waits=(("mm", mq),))
    ctx.barrier()
    ctx.pop_scope()


def emit_l1_attn(ctx):
    s, d = ctx.sb, ctx.dr
    ctx.push_scope()
    slab = ctx.alloc("slab", [128, 2 * 4096], BF16)
    klr = ctx.alloc("klr", [128, 2 * 4096], BF16)
    kraw = ctx.alloc("kraw1", [128, 4096], BF16)
    vtm4 = ctx.alloc("vtm4", [128, 32 * 4 * 65], BF16)
    oall = ctx.alloc("oall1", [128, 2 * T], BF16)
    wAk = ctx.alloc("wAk", [128, 2 * 1024], BF16)
    wAv = ctx.alloc("wAv", [128, 2 * 1024], BF16)
    wC = ctx.alloc("l1WC", [128, 2 * 1024], BF16)
    psb = ctx.alloc("psb1", [128, 1024], BF16)

    if ctx.mode == "C":
        kf_src = d["kf1_in"].ap()
    else:
        kf_src = d["kf1_dram"].ap()
    kfv = kf_src.rearrange("p (c g w) -> p c g w", c=DC, g=2 * NT)

    exp_done = [0, 0]
    pv_done = [0, 0]
    o_norm = 0
    for quad in range(4):
        wv_src = d["Wqkv"].ap()[:, 2 * D + quad * 256: 2 * D + (quad + 1) * 256].rearrange(
            "(c p) o -> p c o", p=128)
        gv = ctx.dma(wAv[:, :], wv_src, waits=(("mm", ctx.now("mm")),), eng="gpsimd")
        for pr in range(2):
            hp = quad * 2 + pr
            wk_src = d["Wqkv"].ap()[:, D + hp * 128: D + hp * 128 + 128].rearrange(
                "(c p) o -> p c o", p=128)
            ctx.dma(wAk[:, pr * 1024:(pr + 1) * 1024], wk_src,
                    waits=(("mm", ctx.now("mm")),), eng="gpsimd")
        gwk = ctx.now("gd")
        ms1 = ctx.memset(vtm4[:, :], 1.0, waits=(("mm", ctx.now("mm")),))
        slab_free = [0, 0]
        kevac = [0, 0]
        klr_done = 0
        vdone = 0
        for ss in range(8):
            sl = slab[:, (ss % 2) * 4096:(ss % 2) * 4096 + 4096]
            wt = [("mm", slab_free[ss % 2])] if slab_free[ss % 2] else []
            gsl = ctx.dma(sl, kfv[:, :, 4 * ss:4 * ss + 4, :], waits=wt, eng="gpsimd")
            for pr in range(2):
                pb = ctx.bankf(pr)
                mmp = 0
                for dc in range(DC):
                    lhs = wAk[:, pr * 1024 + dc * 128: pr * 1024 + dc * 128 + 128]
                    rhs = sl[:, dc * 512:(dc + 1) * 512]
                    wtm = []
                    if dc == 0:
                        wtm.append(("gd", gsl))
                        if ss == 0:
                            wtm.append(("gd", gwk))
                            wtm.append(("ac", ctx.now("ac")))
                        if kevac[pr]:
                            wtm.append(("ac", kevac[pr]))
                    mmp = ctx.mm(pb, lhs, rhs, start=(dc == 0), stop=(dc == DC - 1),
                                 waits=wtm)
                ke = ctx.act(kraw[:, ss * 512:(ss + 1) * 512], pb, AF.Copy,
                             waits=(("mm", mmp), ("ve", klr_done)))
                kevac[pr] = ke
                for h01 in range(2):
                    rb = h01 * 64
                    lb = ctx.sb["PS"][0:16, 2 * 512: 2 * 512 + 512]
                    hh = (quad * 2 + pr) * 2 + h01
                    mq = ctx.mm(lb, s["lsr"][rb:rb + 64, 256 + hh * 16:
                                             256 + hh * 16 + 16],
                                kraw[rb:rb + 64, ss * 512:(ss + 1) * 512],
                                waits=(("ac", ke), ("ve", ctx.now("ve"))),
                                tp=(rb, 0) if rb else None)
                    dst = klr[rb:rb + 16, pr * 4096 + ss * 512: pr * 4096 + ss * 512 + 512]
                    klr_done = ctx.copy(dst, lb, waits=(("mm", mq),))
            for j in range(4):
                kc = ss * 4 + j
                vb = ctx.bankf(3, width=256)
                mmv = 0
                for dc in range(DC):
                    lhs = sl[:, dc * 512 + j * 128: dc * 512 + j * 128 + 128]
                    rhs = wAv[:, dc * 256:(dc + 1) * 256]
                    wtm = []
                    if dc == 0:
                        if kc == 0:
                            wtm += [("gd", gv), ("ve", ms1)]
                        if vdone:
                            wtm.append(("ve", vdone))
                    mmv = ctx.mm(vb, lhs, rhs, start=(dc == 0), stop=(dc == DC - 1),
                                 waits=wtm)
                pvv = ctx.sb["PS"].ap()[:, 3 * 512:3 * 512 + 256].rearrange(
                    "p (h w) -> p h w", h=4)
                vv = vtm4.ap().rearrange("p (c h w) -> p c h w", c=32, h=4)
                vdone = ctx.copy(vv[:, kc, :, 0:64], pvv, waits=(("mm", mmv),))
            slab_free[ss % 2] = ctx.now("mm")
        for h4 in range(4):
            h = quad * 4 + h4
            hp = quad * 2 + h4 // 2
            rb = (h4 % 2) * 64
            outs = []
            for idx, (buf, off, n_tok) in enumerate(
                    ((s["qlr"], (quad * 2 + h4 // 2) * T, T),
                     (klr, (h4 // 2) * 4096, 2 * T))):
                sqb = kraw[rb:rb + 16, 0:2048]
                nss = n_tok // 512
                mx = s["rowsA"][0:1, 64 + h4 * 2 + idx: 64 + h4 * 2 + idx + 1]
                first = True
                for half in range((n_tok + 2047) // 2048):
                    a1 = ctx.act(sqb, buf[rb:rb + 16, off + half * 2048: off + half * 2048 + 2048],
                                 AF.Square, waits=(("ve", ctx.now("ve")),))
                    for ssq in range(4):
                        bnk = ctx.bankf(2 + (ssq % 2), p1=1)
                        mmn = ctx.mm(bnk, s["onescol"][rb:rb + 16, :],
                                     sqb[:, ssq * 512:(ssq + 1) * 512],
                                     waits=(("ac", a1), ("ve", ctx.now("ve"))),
                                     tp=(rb, 0) if rb else None)
                        tmp = s["rowsA"][0:1, 60:61]
                        ctx.red(tmp, bnk, ALU.max, waits=(("mm", mmn),))
                        if first:
                            ctx.copy(mx, tmp)
                            first = False
                        else:
                            ctx.tt(mx, mx, tmp, ALU.max)
                outs.append(mx)
            pr_ = s["rowsA"][0:1, 61:62]
            v1 = ctx.tt(pr_, outs[0], outs[1], ALU.mult)
            a2 = ctx.act(pr_, pr_, AF.Sqrt, waits=(("ve", v1),))
            v2 = ctx.ts(pr_, pr_, -1.0, None, ALU.mult, waits=(("ac", a2),))
            m1 = ctx.mm(ctx.bankf(6, width=1), s["onesrowf"][:, :], pr_,
                        waits=(("ve", v2), ("ve", ctx.now("ve"))))
            ctx.copy(s["negB"][:, 2 + h:3 + h], ctx.bankf(6, width=1), waits=(("mm", m1),))
        wo1 = d["Wo1"].ap()[quad * 256:(quad + 1) * 256, :].rearrange(
            "(i p) o -> p i o", p=128)
        wo_dma = ctx.dma(wC[:, :], wo1, waits=(("mm", ctx.now("mm")),), eng="gpsimd")
        for h4 in range(4):
            h = quad * 4 + h4
            rb = (h4 % 2) * 64
            pr = h4 // 2
            negb1 = s["negB"][:, 2 + h:3 + h]
            for qg in range(8):
                q0 = qg * 256
                kmax = 4 * qg + 4
                ob = ctx.bankf(4, p1=65, width=256)
                for kc2 in range(0, kmax, 2):
                    g2i = (kc2 // 2) % 2
                    wt = []
                    if exp_done[g2i]:
                        wt.append(("ac", exp_done[g2i]))
                    if kc2 == 0 and qg == 0 and h4 == 0:
                        wt.append(("ve", vdone))
                        wt.append(("ve", ctx.now("ve")))
                    ctx.mm(ctx.bankf(g2i, lo=0, width=256),
                           klr[rb:rb + 16, pr * 4096 + kc2 * 128: pr * 4096 + kc2 * 128 + 128],
                           s["qlr"][rb:rb + 16, (quad * 2 + pr) * T + q0:
                                    (quad * 2 + pr) * T + q0 + 256],
                           tp=(rb, 0) if rb else None, waits=wt)
                    m1 = ctx.mm(ctx.bankf(g2i, lo=256, width=256),
                                klr[rb:rb + 16, pr * 4096 + kc2 * 128 + 128:
                                    pr * 4096 + kc2 * 128 + 256],
                                s["qlr"][rb:rb + 16, (quad * 2 + pr) * T + q0:
                                         (quad * 2 + pr) * T + q0 + 256],
                                tp=(rb, 0) if rb else None)
                    wexp = [("mm", m1)]
                    if pv_done[g2i]:
                        wexp.append(("mm", pv_done[g2i]))
                    psl = psb[:, g2i * 512:(g2i + 1) * 512]
                    ea = ctx.act(psl, ctx.bankf(g2i), AF.Exp, bias=negb1, waits=wexp)
                    exp_done[g2i] = ea
                    mask_done = 0
                    for j in range(2):
                        kc = kc2 + j
                        i = kc - (kmax - 4)
                        if i >= 0:
                            msl = psb[:, g2i * 512 + j * 256: g2i * 512 + j * 256 + 256]
                            mask_done = ctx.tt(msl, msl,
                                               s["cmask"][:, i * 256:(i + 1) * 256],
                                               ALU.mult, waits=(("ac", ea),))
                    for j in range(2):
                        kc = kc2 + j
                        wv0 = [("ac", ea)]
                        if mask_done:
                            wv0.append(("ve", mask_done))
                        if kc == 0 and o_norm:
                            wv0.append(("ve", o_norm))
                        pv_done[g2i] = ctx.mm(
                            ob, vtm4[:, (kc * 4 + h4) * 65:(kc * 4 + h4) * 65 + 65],
                            psb[:, g2i * 512 + j * 256: g2i * 512 + j * 256 + 256],
                            start=(kc == 0), stop=(kc == kmax - 1), waits=wv0)
                zrow = ctx.sb["PS"][64:65, 4 * 512: 4 * 512 + 256]
                rr = s["rowsA"][0:1, 0:256]
                ctx.ts(rr, zrow, 1e-38, None, ALU.add,
                       waits=(("mm", pv_done[0] or pv_done[1]),
                              ("mm", pv_done[1] or pv_done[0])))
                v2 = ctx.recip(rr, rr)
                bc = ctx.bankf(6, p1=64, width=256)
                m1 = ctx.mm(bc, s["onesrowf"][0:1, 0:64], rr, waits=(("ve", v2),))
                v3 = ctx.copy(s["rbc"][0:64, 0:256], bc, waits=(("mm", m1),))
                dst = oall[rb:rb + 64, pr * T + q0: pr * T + q0 + 256]
                o_norm = ctx.tt(dst, ctx.sb["PS"][0:64, 4 * 512:4 * 512 + 256],
                                s["rbc"][0:64, 0:256], ALU.mult,
                                waits=(("ve", v3), ("mm", getattr(ctx, "_flush_mm", 0))))
        if os.environ.get("KDBG") and ctx.mode == "C" and quad == int(os.environ.get("KDBGQ", "3")):
            ctx.barrier()
            ctx.dma(d["dbg_qlr"].ap(), s["qlr"][:, :])
            ctx.dma(d["dbg_klr"].ap(), klr[:, :])
            ctx.dma(d["dbg_vt4"].ap(), vtm4[:, :])
            ctx.dma(d["dbg_nb1"].ap(), s["negB"][:, :])
            ctx.dma(d["dbg_oa1"].ap(), oall[:, :])
            ctx.barrier()
        _flush_outproj(ctx, wo_dma, wC, oall, npairs=2)
        exp_done = [0, 0]
        pv_done = [0, 0]
        o_norm = 0
    ctx.barrier()
    ctx.pop_scope()


def ph_store(ctx):
    s, d = ctx.sb, ctx.dr
    if ctx.mode == "A":
        g = ctx.dma(d["xn_out"].ap(), s["xn"][:, :])
        ctx.p.op("sync", lambda e: e.nop(), waits=(("ld", g),))
        return
    if ctx.mode == "B":
        ctx.dma(d["x1_out"].ap().rearrange("(c p) t -> p c t", p=128), s["x"][:, :])
        g2 = ctx.dma(d["xn2_out"].ap(), s["xn"][:, :])
        ctx.p.op("sync", lambda e: e.nop(), waits=(("ld", g2),))
        return
    g = ctx.dma(d["y"].ap().rearrange("(c p) t -> p c t", p=128), s["x"][:, :])
    ctx.p.op("sync", lambda e: e.nop(), waits=(("ld", g),))


def emit_all(ctx):
    ph_load(ctx)
    if ctx.mode == "A":
        ctx.sb["xn"] = ctx.alloc("xn", [128, DC * T], BF16)
        emit_ln(ctx)
        ph_store(ctx)
        return
    if ctx.mode == "B":
        upto = int(os.environ.get("KUPTO", "9"))
        ctx.push_scope()
        ctx.sb["xn"] = ctx.alloc("xn", [128, DC * T], BF16)
        ctx.dma(ctx.sb["xn"][:, :], ctx.dr["xn_own_in"].ap())
        ctx.barrier()
        if upto >= 1:
            emit_negB_l0(ctx)
        if upto >= 2:
            emit_ffn0(ctx)
        ctx.pop_scope()
        if upto >= 3:
            emit_l0_attn(ctx)
        ctx.push_scope()
        ctx.sb["xn"] = ctx.alloc("xn", [128, DC * T], BF16)
        ctx.sb["maskfm"] = ctx.alloc("maskfm", [8, T], F32)
        if upto >= 4:
            emit_ln(ctx)
        if upto >= 5:
            emit_router_S(ctx)
        ph_store(ctx)
        ctx.pop_scope()
        return
    if ctx.mode == "C":
        ctx.push_scope()
        ctx.sb["qlr"] = ctx.alloc("qlr", [128, 8 * T], BF16)
        ctx.push_scope()
        ctx.sb["xn"] = ctx.alloc("xn", [128, DC * T], BF16)
        ctx.sb["maskfm"] = ctx.alloc("maskfm", [8, T], F32)
        ctx.sb["maskfmb"] = ctx.alloc("maskfmb", [8, T], BF16)
        ctx.dma(ctx.sb["xn"][:, :], ctx.dr["xn2own"].ap())
        ctx.barrier()
        emit_router_S(ctx)
        emit_mask_final(ctx)
        emit_qlr_prep(ctx)
        if os.environ.get("KC_MOE", "1") == "1":
            emit_moe(ctx)
        ctx.pop_scope()
        if os.environ.get("KC_ATTN", "1") == "1":
            emit_l1_attn(ctx)
        ctx.pop_scope()
        ph_store(ctx)
        return
    up = int(os.environ.get("KF_UP", "9"))
    ctx.push_scope()
    ctx.sb["xn"] = ctx.alloc("xn", [128, DC * T], BF16)
    emit_ln(ctx)
    emit_gather_issue(ctx, 0)
    emit_negB_l0(ctx)
    if up >= 1:
        emit_ffn0(ctx)
    ctx.pop_scope()
    if up >= 2:
        emit_l0_attn(ctx)
    if up >= 3:
        ctx.push_scope()
        ctx.sb["qlr"] = ctx.alloc("qlr", [128, 8 * T], BF16)
        ctx.push_scope()
        ctx.sb["xn"] = ctx.alloc("xn", [128, DC * T], BF16)
        ctx.sb["maskfm"] = ctx.alloc("maskfm", [8, T], F32)
        ctx.sb["maskfmb"] = ctx.alloc("maskfmb", [8, T], BF16)
        emit_ln(ctx)
        emit_router_S(ctx)
        emit_gather_issue(ctx, 1)
        emit_mask_final(ctx)
        if os.environ.get("KDBG"):
            ctx.dma(ctx.dr["dbg_sm"].ap(), ctx.sb["smalls"][:, :])
            ctx.dma(ctx.dr["dbg_mt"].ap(), ctx.sb["masktok"][:, :])
            ctx.barrier()
        emit_qlr_prep(ctx)
        if up >= 4:
            emit_moe(ctx)
        ph_assemble1(ctx)
        ctx.pop_scope()
        if up >= 5:
            emit_l1_attn(ctx)
        ctx.pop_scope()
    ph_store(ctx)


def build_program(mode):
    nc = bass.Bass("TRN2", target_bir_lowering=False)
    ctx = Ctx(nc, mode)
    declare_dram(ctx)
    root = ExitStack()
    ctx.scopes = [root]
    with root:
        declare_sbuf_persistent(ctx)
        sems = {}
        for snm in ("mm", "ve", "ac", "ld", "gd", "cc"):
            sems[snm] = root.enter_context(nc.semaphore(snm))
        block = root.enter_context(nc.Block())
        emit_all(ctx)
        with nc.allow_low_precision(reason="bf16 operands with fp32 psum accumulation by design"):
            ctx.p.emit(block, sems)
    return nc


def kernel(**inputs):
    w = host_weights(inputs)
    shared = host_constants()
    x0 = np.asarray(inputs["L0_attn_x"], np.float32)

    nc = build_program("full")
    maps = []
    for core in range(NCORES):
        dd = dict(shared)
        dd.update(core_constants(core))
        dd["x0"] = _f32(x0[core // 2][own_rows(core)].T)
        for knm in ("Wq", "Wk", "Wv", "Wo", "bq", "bk", "bv", "bo",
                    "Wg_t", "Wu_t", "Wd_t", "RW", "Rb",
                    "Wqkv", "Wo1", "Wqlsr", "Wklsr", "coreT",
                    "w1", "w2", "b1t", "b2e"):
            dd[knm] = w[knm]
        for nm in ("q", "k", "v"):
            for suf in ("s128", "b128", "srow", "brow"):
                dd[f"{nm}{suf}"] = w[f"{nm}{suf}"]
        maps.append(dd)
    res = run_bass_kernel_spmd(nc, maps, core_ids=list(range(NCORES)))

    out = np.empty((B, S, D), dtype=np.float32)
    for core in range(NCORES):
        b = core // 2
        p = core % 2
        y = np.asarray(res.results[core]["y"], np.float32).T
        for j in range(NT):
            out[b, (2 * j + p) * 128:(2 * j + p + 1) * 128, :] = y[j * 128:(j + 1) * 128]
    return out


# version 04
# speedup vs baseline: 1.2668x; 1.2668x over previous
import os
import numpy as np
import ml_dtypes
from contextlib import ExitStack
import concourse.bass as bass
import concourse.mybir as mybir
from concourse.bass_utils import run_bass_kernel_spmd

F32 = mybir.dt.float32
BF16 = mybir.dt.bfloat16
I32 = mybir.dt.int32
AF = mybir.ActivationFunctionType
ALU = mybir.AluOpType
AX = mybir.AxisListType

B, S, D, NH, HD, R = 4, 4096, 1024, 16, 64, 16
T = 2048
NT = 16
DC = 8
HID = 3072
NE, MH = 8, 1024
CAP = 2560.0
EPS_LN = 1e-5
EPS_QK = 1e-6
INV127 = 1.0 / 127.0
NCORES = 8
XN_ELEMS = 128 * DC * T


class Prog:
    ENGINES = ("tensor", "vector", "scalar", "gpsimd", "sync")

    def __init__(self):
        self.ops = []
        self.cnt = {}
        self.pending_barrier = {}

    def sem(self, name):
        self.cnt[name] = 0
        return name

    SELF_SERIAL = {"vector": "ve", "scalar": "ac"}

    def op(self, engine, fn, waits=(), inc=(), inc_amt=1):
        if isinstance(inc, str):
            inc = (inc,)
        w = list(self.pending_barrier.pop(engine, ())) + list(waits)
        ss = self.SELF_SERIAL.get(engine)
        if ss is not None and self.cnt[ss] > 0:
            w.append((ss, self.cnt[ss]))
        incs = []
        for snm in inc:
            self.cnt[snm] += inc_amt
            incs.append((snm, inc_amt))
        self.ops.append((engine, fn, tuple(w), tuple(incs)))
        return self.cnt[inc[0]] if inc else None

    def barrier(self, sems):
        snap = [(snm, self.cnt[snm]) for snm in sems if self.cnt[snm] > 0]
        for e in self.ENGINES:
            cur = self.pending_barrier.get(e, [])
            self.pending_barrier[e] = cur + snap

    def emit(self, block, handles):
        per = {e: [] for e in self.ENGINES}
        for rec in self.ops:
            per[rec[0]].append(rec)

        def mk(engine_name):
            def body(eng):
                for (_, fn, waits, incs) in per[engine_name]:
                    seen = {}
                    for (snm, v) in waits:
                        if seen.get(snm, -1) < v:
                            seen[snm] = v
                    for snm, v in seen.items():
                        eng.wait_ge(handles[snm], v)
                    inst = fn(eng)
                    for (snm, a) in incs:
                        inst.then_inc(handles[snm], a)
            return body

        block.tensor(mk("tensor"))
        block.vector(mk("vector"))
        block.scalar(mk("scalar"))
        block.gpsimd(mk("gpsimd"))
        block.sync(mk("sync"))


class Ctx:
    def __init__(self, nc, mode):
        self.nc = nc
        self.mode = mode
        self.p = Prog()
        for snm in ("mm", "ve", "ac", "ld", "gd", "cc"):
            self.p.sem(snm)
        self.dr = {}
        self.sb = {}

    def mm(self, out, lhsT, rhs, start=True, stop=True, waits=(), tp=None):
        def fn(e, out=out, lhsT=lhsT, rhs=rhs, start=start, stop=stop, tp=tp):
            return e.matmul(out, lhsT, rhs, start=start, stop=stop, tile_position=tp)
        return self.p.op("tensor", fn, waits=waits, inc="mm")

    def trans(self, out, in_, ident, waits=()):
        return self.p.op("tensor",
                         lambda e, o=out, i=in_, d=ident: e.transpose(o, i, d),
                         waits=waits, inc="mm")

    def act(self, out, in_, func, bias=0.0, scale=1.0, waits=()):
        def fn(e, out=out, in_=in_, func=func, bias=bias, scale=scale):
            return e.activation(out, in_, func, bias=bias, scale=scale)
        return self.p.op("scalar", fn, waits=waits, inc="ac")

    def copy(self, out, in_, waits=()):
        return self.p.op("vector", lambda e, o=out, i=in_: e.tensor_copy(o, i),
                         waits=waits, inc="ve")

    def tt(self, out, in0, in1, op, waits=()):
        return self.p.op("vector",
                         lambda e, o=out, a=in0, b=in1, op=op: e.tensor_tensor(o, a, b, op),
                         waits=waits, inc="ve")

    def ts(self, out, in0, s1, s2=None, op0=ALU.mult, op1=ALU.add, waits=()):
        if s2 is None:
            op1 = ALU.bypass
        def fn(e, o=out, a=in0, s1=s1, s2=s2, op0=op0, op1=op1):
            return e.tensor_scalar(o, a, s1, s2, op0, op1)
        return self.p.op("vector", fn, waits=waits, inc="ve")

    def stt(self, out, in0, scalar, in1, op0, op1, waits=()):
        def fn(e, o=out, a=in0, s=scalar, b=in1, op0=op0, op1=op1):
            return e.scalar_tensor_tensor(o, a, s, b, op0, op1)
        return self.p.op("vector", fn, waits=waits, inc="ve")

    def red(self, out, in_, op, negate=False, absv=False, waits=()):
        def fn(e, o=out, i=in_, op=op, neg=negate, ab=absv):
            return e.tensor_reduce(o, i, AX.X, op, apply_absolute_value=ab, negate=neg)
        return self.p.op("vector", fn, waits=waits, inc="ve")

    def recip(self, out, in_, waits=()):
        return self.p.op("vector", lambda e, o=out, i=in_: e.reciprocal(o, i),
                         waits=waits, inc="ve")

    def max8(self, out, in_, waits=()):
        return self.p.op("vector", lambda e, o=out, i=in_: e.max(out=o, in_=i),
                         waits=waits, inc="ve")

    def memset(self, ap, val, waits=()):
        return self.p.op("vector", lambda e, a=ap, v=val: e.memset(a, v),
                         waits=waits, inc="ve")

    def dma(self, dst, src, waits=(), eng="sync"):
        if len(src.shape) == 3 and len(dst.shape) == 2:
            dst = dst.rearrange("p (c w) -> p c w", c=src.shape[1])
        elif len(dst.shape) == 3 and len(src.shape) == 2:
            src = src.rearrange("p (c w) -> p c w", c=dst.shape[1])
        snm = "ld" if eng == "sync" else "gd"
        return self.p.op(eng, lambda e, d=dst, s=src: e.dma_start(out=d, in_=s),
                         waits=waits, inc=snm, inc_amt=16)

    def dma_fn(self, fn, waits=(), eng="gpsimd"):
        snm = "ld" if eng == "sync" else "gd"
        return self.p.op(eng, fn, waits=waits, inc=snm, inc_amt=16)

    def coll(self, kind, op, ins, outs, waits=()):
        def fn(e, kind=kind, op=op, ins=ins, outs=outs):
            return e.collective_compute(kind, op, replica_groups=[list(range(NCORES))],
                                        ins=ins, outs=outs)
        return self.p.op("gpsimd", fn, waits=waits, inc="cc", inc_amt=1)

    def barrier(self):
        self.p.barrier(("mm", "ve", "ac", "ld", "gd", "cc"))

    def now(self, snm):
        return self.p.cnt[snm]

    def bankf(self, i, lo=0, width=512, p0=0, p1=128):
        return self.sb["PS"][p0:p1, i * 512 + lo: i * 512 + lo + width]

    def alloc(self, name, shape, dtype, psum=False):
        fn = self.nc.psum_tensor if psum else self.nc.sbuf_tensor
        self._nalloc = getattr(self, "_nalloc", 0) + 1
        return self.scopes[-1].enter_context(fn(f"sb{self._nalloc}_{name}", shape, dtype))

    def push_scope(self):
        es = ExitStack()
        self.scopes.append(es)
        return es

    def pop_scope(self):
        self.scopes.pop().close()


def _bf(x):
    return np.ascontiguousarray(np.asarray(x), dtype=ml_dtypes.bfloat16)


def _f32(x):
    return np.ascontiguousarray(np.asarray(x), dtype=np.float32)


def feat8(v):
    return _f32(np.asarray(v, np.float32).reshape(8, 128).T)


def host_constants():
    c = {}
    c["identf"] = _f32(np.eye(128, dtype=np.float32))
    c["identb"] = _bf(np.eye(128, dtype=np.float32))
    Bm = np.eye(64, dtype=np.float32) - 1.0 / 64.0
    Z = np.zeros((64, 64), np.float32)
    c["bpair"] = _bf(np.block([[Bm, Z], [Z, Bm]]))
    J = np.ones((64, 64), np.float32)
    c["ones64"] = _bf(np.block([[J, Z], [Z, J]]))
    c["onesrow"] = _bf(np.ones((1, 128), np.float32))
    c["onesrowf"] = _f32(np.ones((1, 128), np.float32))
    c["onescol"] = _bf(np.ones((128, 1), np.float32))
    return c


def core_constants(core):
    p = core % 2
    c = {}
    c["rkall"] = np.array([[core, (core // 2) * 2, (core // 2) * 2 + 1, core ^ 1]],
                          dtype=np.int32)
    row = np.arange(128)[:, None]
    col = np.arange(256)[None, :]
    masks = []
    for i in range(4):
        valid = (i * 128 + row) <= (2 * (col // 128) * 128 + p * 128 + col % 128)
        masks.append(valid.astype(np.float32))
    c["cmask"] = _bf(np.stack(masks))
    c["flagS"] = _f32(np.full((8, 2), 1.0 if core == 0 else 0.0, np.float32))
    return c


def host_weights(inputs):
    w = {}
    w["Wq"] = _f32(inputs["L0_attn_Wq"]); w["Wk"] = _f32(inputs["L0_attn_Wk"])
    w["Wv"] = _f32(inputs["L0_attn_Wv"]); w["Wo"] = _f32(inputs["L0_attn_Wo"])
    w["bq"] = feat8(inputs["L0_attn_bq"]); w["bk"] = feat8(inputs["L0_attn_bk"])
    w["bv"] = feat8(inputs["L0_attn_bv"]); w["bo"] = feat8(inputs["L0_attn_bo"])
    for nm in ("q", "k", "v"):
        sc = np.asarray(inputs[f"L0_attn_{nm}_scale"], np.float32)
        bi = np.asarray(inputs[f"L0_attn_{nm}_bias"], np.float32)
        w[f"{nm}s128"] = _f32(np.tile(sc, 2)[:, None])
        w[f"{nm}b128"] = _f32(np.tile(bi, 2)[:, None])
        w[f"{nm}srow"] = _f32(sc[None, :])
        w[f"{nm}brow"] = _f32(bi[None, :])
    w["Wg_t"] = np.ascontiguousarray(np.asarray(inputs["L0_ffn_w_gate"], np.int32).T)
    w["Wu_t"] = np.ascontiguousarray(np.asarray(inputs["L0_ffn_w_up"], np.int32).T)
    w["Wd_t"] = np.ascontiguousarray(np.asarray(inputs["L0_ffn_w_down"], np.int32).T)
    w["Wqkv"] = _f32(inputs["L1_attn_W_qkv"])
    w["Wo1"] = _f32(inputs["L1_attn_W_o"])
    wq_lsr = np.asarray(inputs["L1_attn_W_q_lsr"], np.float32)
    wk_lsr = np.asarray(inputs["L1_attn_W_k_lsr"], np.float32)
    w["Wqlsr"] = _f32(np.transpose(wq_lsr, (1, 0, 2)).reshape(64, 256))
    w["Wklsr"] = _f32(np.transpose(wk_lsr, (1, 0, 2)).reshape(64, 256))
    w["coreT"] = _f32(np.asarray(inputs["L1_attn_core"], np.float32).T)
    w["RW"] = _f32(inputs["L1_ffn_router_w"])
    w["Rb"] = _f32(np.asarray(inputs["L1_ffn_router_b"], np.float32)[:, None])
    w["w1"] = _f32(inputs["L1_ffn_w1"])
    w["w2"] = _f32(inputs["L1_ffn_w2"])
    b1 = np.asarray(inputs["L1_ffn_b1"], np.float32)
    w["b1t"] = _f32(np.stack([b1[e].reshape(8, 128).T for e in range(NE)]))
    w["b2e"] = _f32(inputs["L1_ffn_b2"])
    return w


def own_rows(core):
    p = core % 2
    return np.concatenate([np.arange(128) + (2 * j + p) * 128 for j in range(NT)])


def declare_dram(ctx):
    nc, mode = ctx.nc, ctx.mode
    t = ctx.dr

    def inp(name, shape, dt=F32):
        t[name] = nc.dram_tensor(name, shape, dt, kind="ExternalInput")

    inp("identf", [128, 128]); inp("identb", [128, 128], BF16)
    inp("bpair", [128, 128], BF16); inp("ones64", [128, 128], BF16)
    inp("onesrow", [1, 128], BF16); inp("onesrowf", [1, 128])
    inp("onescol", [128, 1], BF16)
    inp("rkall", [1, 4], I32)
    inp("cmask", [4, 128, 256], BF16); inp("flagS", [8, 2])

    if mode in ("A", "B", "full"):
        inp("x0", [D, T])
    if mode in ("B", "full"):
        inp("Wq", [D, D]); inp("Wk", [D, D]); inp("Wv", [D, D]); inp("Wo", [D, D])
        inp("bq", [128, 8]); inp("bk", [128, 8]); inp("bv", [128, 8]); inp("bo", [128, 8])
        for nm in ("q", "k", "v"):
            inp(f"{nm}s128", [128, 1]); inp(f"{nm}b128", [128, 1])
            inp(f"{nm}srow", [1, 64]); inp(f"{nm}brow", [1, 64])
        inp("Wg_t", [D, HID], I32); inp("Wu_t", [D, HID], I32); inp("Wd_t", [HID, D], I32)
    if mode in ("B", "C", "full"):
        inp("RW", [D, 8]); inp("Rb", [8, 1])
    if mode in ("C", "full"):
        inp("Wqkv", [D, 3 * D]); inp("Wo1", [D, D])
        inp("Wqlsr", [64, 256]); inp("Wklsr", [64, 256]); inp("coreT", [16, 16])
        inp("w1", [NE, D, MH]); inp("w2", [NE, MH, D])
        inp("b1t", [NE, 128, 8]); inp("b2e", [NE, D])
    if mode == "B":
        inp("xn_own_in", [128, DC * T], BF16)
        inp("kf0_in", [128, DC * 2 * T], BF16)
    if mode == "C":
        inp("x1", [D, T])
        if os.environ.get("KDBG"):
            t["dbg_qlr"] = nc.dram_tensor("dbg_qlr", [128, 8 * T], BF16, kind="ExternalOutput")
            t["dbg_klr"] = nc.dram_tensor("dbg_klr", [128, 2 * 4096], BF16, kind="ExternalOutput")
            t["dbg_vt4"] = nc.dram_tensor("dbg_vt4", [128, 32 * 4 * 65], BF16, kind="ExternalOutput")
            t["dbg_nb1"] = nc.dram_tensor("dbg_nb1", [128, 18], F32, kind="ExternalOutput")
            t["dbg_oa1"] = nc.dram_tensor("dbg_oa1", [128, 2 * T], BF16, kind="ExternalOutput")
        inp("xn2own", [128, DC * T], BF16)
        inp("kf1_in", [128, DC * 2 * T], BF16)
        inp("Sall", [8, 2 * NCORES])

    if mode == "A":
        t["xn_out"] = nc.dram_tensor("xn_out", [128, DC * T], BF16, kind="ExternalOutput")
    if mode == "B":
        t["x1_out"] = nc.dram_tensor("x1_out", [D, T], F32, kind="ExternalOutput")
        t["xn2_out"] = nc.dram_tensor("xn2_out", [128, DC * T], BF16, kind="ExternalOutput")
        t["Sp_out"] = nc.dram_tensor("Sp_out", [8, 2], F32, kind="ExternalOutput")
        if os.environ.get("KDBG"):
            t["dbg_qh"] = nc.dram_tensor("dbg_qh", [128, T], BF16, kind="ExternalOutput")
            t["dbg_kr"] = nc.dram_tensor("dbg_kr", [128, 2 * T], BF16, kind="ExternalOutput")
            t["dbg_vt"] = nc.dram_tensor("dbg_vt", [128, 32 * 2 * 65], BF16, kind="ExternalOutput")
            t["dbg_nb"] = nc.dram_tensor("dbg_nb", [128, 1], F32, kind="ExternalOutput")
            t["dbg_oa"] = nc.dram_tensor("dbg_oa", [128, 2 * T], BF16, kind="ExternalOutput")
            t["dbg_wc"] = nc.dram_tensor("dbg_wc", [128, 2 * 1024], BF16, kind="ExternalOutput")
    if mode in ("C", "full"):
        t["y"] = nc.dram_tensor("y", [D, T], F32, kind="ExternalOutput")

    if mode == "full":
        if os.environ.get("KDBG"):
            t["dbg_sm"] = nc.dram_tensor("dbg_sm", [128, 64], F32, kind="ExternalOutput")
            t["dbg_mt"] = nc.dram_tensor("dbg_mt", [128, 144], F32, kind="ExternalOutput")
        t["cc_xn_in"] = nc.dram_tensor("cc_xn_in", [XN_ELEMS], BF16)
        t["cc_xn0_out"] = nc.dram_tensor("cc_xn0_out", [NCORES * XN_ELEMS], BF16,
                                         addr_space="Shared")
        t["cc_xn1_out"] = nc.dram_tensor("cc_xn1_out", [NCORES * XN_ELEMS], BF16,
                                         addr_space="Shared")
        t["cc_S_in"] = nc.dram_tensor("cc_S_in", [16], F32)
        t["cc_S_out"] = nc.dram_tensor("cc_S_out", [16], F32, addr_space="Shared")
        t["kf1_dram"] = nc.dram_tensor("kf1_dram", [128, DC * 2 * T], BF16)


def declare_sbuf_persistent(ctx):
    s = ctx.sb
    A = ctx.alloc
    s["x"] = A("x", [128, DC * T], F32)
    s["consts"] = A("consts", [128, 384], BF16)
    s["identf"] = A("identf", [128, 128], F32)
    s["onesrow"] = A("onesrow", [1, 128], BF16)
    s["onesrowf"] = A("onesrowf", [1, 128], F32)
    s["onescol"] = A("onescol", [128, 1], BF16)
    s["bias8"] = A("bias8", [128, 40], F32)
    s["qk128"] = A("qk128", [128, 12], F32)
    s["rows64"] = A("rows64", [1, 384], F32)
    s["rowsA"] = A("rowsA", [1, 512], F32)
    s["smalls"] = A("smalls", [128, 64], F32)
    s["negB"] = A("negB", [128, 18], F32)
    s["ints"] = A("ints", [1, 4], I32)
    s["flagS"] = A("flagS", [8, 2], F32)
    s["cmask"] = A("cmask", [128, 4 * 256], BF16)
    s["coreT"] = A("coreT", [16, 16], F32)
    s["cscaled"] = A("cscaled", [16, 16], F32)
    s["lsr"] = A("lsr", [128, 512], BF16)
    s["rbc"] = A("rbc", [128, 512], F32)
    s["rstdb"] = A("rstdb", [128, 512], BF16)
    s["bbc"] = A("bbc", [128, 512], BF16)
    s["masktok"] = A("masktok", [128, 16 + 128], F32)
    s["epsc"] = A("epsc", [128, 2], F32)
    s["biasc"] = A("biasc", [128, 24], F32)
    s["sqsl"] = A("sqsl", [128, 1024], BF16)
    s["PS"] = A("PS", [128, 3584], F32, psum=True)
    s["PSB"] = A("PSB", [128, 1024], BF16, psum=True)


def ph_load(ctx):
    s, d, mode = ctx.sb, ctx.dr, ctx.mode
    ctx.dma(s["identf"][:, :], d["identf"].ap())
    ctx.dma(s["consts"][:, 0:128], d["identb"].ap(), eng="gpsimd")
    ctx.dma(s["consts"][:, 128:256], d["bpair"].ap(), eng="gpsimd")
    ctx.dma(s["consts"][:, 256:384], d["ones64"].ap(), eng="gpsimd")
    ctx.dma(s["onesrow"][:, :], d["onesrow"].ap(), eng="gpsimd")
    ctx.dma(s["onesrowf"][:, :], d["onesrowf"].ap())
    ctx.dma(s["onescol"][:, :], d["onescol"].ap(), eng="gpsimd")
    ctx.dma(s["ints"][:, :], d["rkall"].ap())
    ctx.dma(s["cmask"].ap().rearrange("p (m w) -> p m w", m=4),
            d["cmask"].ap().rearrange("m p w -> p m w"))
    ctx.dma(s["flagS"][:, :], d["flagS"].ap())
    ctx.memset(s["epsc"][:, 0:1], EPS_LN)
    ctx.memset(s["epsc"][:, 1:2], EPS_QK)
    if mode in ("A", "B", "full"):
        ctx.dma(s["x"][:, :], d["x0"].ap().rearrange("(c p) t -> p c t", p=128))
    if mode == "C":
        ctx.dma(s["x"][:, :], d["x1"].ap().rearrange("(c p) t -> p c t", p=128))
        ctx.dma(s["bias8"][0:8, 0:16], d["Sall"].ap())
    if mode in ("B", "full"):
        for i, nm in enumerate(("bq", "bk", "bv", "bo")):
            ctx.dma(s["bias8"][:, 8 * i:8 * i + 8], d[nm].ap())
        for i, nm in enumerate(("qs128", "qb128", "ks128", "kb128", "vs128", "vb128")):
            ctx.dma(s["qk128"][:, i:i + 1], d[nm].ap())
        for i, nm in enumerate(("qsrow", "qbrow", "ksrow", "kbrow", "vsrow", "vbrow")):
            ctx.dma(s["rows64"][:, 64 * i:64 * i + 64], d[nm].ap())
    if mode in ("B", "C", "full"):
        ctx.dma(s["bias8"][0:8, 32:33], d["Rb"].ap())
    if mode in ("C", "full"):
        ctx.dma(s["coreT"][:, :], d["coreT"].ap())
        ctx.dma(s["lsr"][0:64, 0:256], d["Wqlsr"].ap(), eng="gpsimd")
        ctx.dma(s["lsr"][0:64, 256:512], d["Wklsr"].ap(), eng="gpsimd")
        ctx.dma(s["lsr"][64:128, 0:256], d["Wqlsr"].ap(), eng="gpsimd")
        ctx.dma(s["lsr"][64:128, 256:512], d["Wklsr"].ap(), eng="gpsimd")
    ctx.barrier()


def emit_ln(ctx):
    s = ctx.sb
    ctx.push_scope()
    rex = ctx.alloc("lnrows", [1, 1536], F32)
    xb = s["xn"]
    casts = []
    for dc in range(DC):
        casts.append(ctx.copy(xb[:, dc * T:(dc + 1) * T], s["x"][:, dc * T:(dc + 1) * T]))
    sq_mm = [0, 0]
    tt_last = 0
    for q in range(4):
        q0 = q * 512
        sums = ctx.bankf(0, p1=1)
        ssq = ctx.bankf(1, p1=1)
        mm_s = mm_q = 0
        for dc in range(DC):
            xc = xb[:, dc * T + q0: dc * T + q0 + 512]
            sq = s["sqsl"][:, (dc % 2) * 512:(dc % 2) * 512 + 512]
            aw = [("ve", casts[dc])]
            if sq_mm[dc % 2]:
                aw.append(("mm", sq_mm[dc % 2]))
            a1 = ctx.act(sq, xc, AF.Square, waits=aw)
            mm_s = ctx.mm(sums, s["onescol"][:, :], xc, start=(dc == 0), stop=(dc == DC - 1),
                          waits=(("ve", casts[dc]), ("ve", tt_last)))
            mm_q = ctx.mm(ssq, s["onescol"][:, :], sq, start=(dc == 0), stop=(dc == DC - 1),
                          waits=(("ac", a1),))
            sq_mm[dc % 2] = mm_q
        rA = s["rowsA"][0:1, 0:512]
        rB = rex[0:1, 0:512]
        rC = rex[0:1, 512:1024]
        rD = rex[0:1, 1024:1536]
        v1 = ctx.ts(rA, sums, 1.0 / D, None, ALU.mult, waits=(("mm", mm_s),))
        a1 = ctx.act(rB, rA, AF.Square, waits=(("ve", v1),))
        v2 = ctx.stt(rB, ssq, 1.0 / D, rB, ALU.mult, ALU.subtract,
                     waits=(("mm", mm_q), ("ac", a1)))
        a2 = ctx.act(rB, rB, AF.Sqrt, bias=s["epsc"][0:1, 0:1], waits=(("ve", v2),))
        v3 = ctx.recip(rC, rB, waits=(("ac", a2),))
        v4 = ctx.tt(rD, rA, rC, ALU.mult)
        bcA = ctx.bankf(2)
        bcB = ctx.bankf(3)
        m1 = ctx.mm(bcA, s["onesrowf"][:, :], rC, waits=(("ve", v3), ("ac", ctx.now("ac"))))
        m2 = ctx.mm(bcB, s["onesrowf"][:, :], rD, waits=(("ve", v4),))
        e1 = ctx.act(s["rstdb"][:, :], bcA, AF.Copy, waits=(("mm", m1), ("ve", tt_last)))
        e2 = ctx.act(s["bbc"][:, :], bcB, AF.Copy, waits=(("mm", m2),))
        for dc in range(DC):
            xc = xb[:, dc * T + q0: dc * T + q0 + 512]
            ctx.tt(xc, xc, s["rstdb"][:, :], ALU.mult, waits=(("ac", e2), ("mm", mm_q)))
            tt_last = ctx.tt(xc, xc, s["bbc"][:, :], ALU.subtract)
    ctx.barrier()
    ctx.pop_scope()


def emit_gather_issue(ctx, layer):
    s, d = ctx.sb, ctx.dr
    if ctx.mode != "full":
        return
    ccin = d["cc_xn_in"].ap().rearrange("(p n) -> p n", p=128)
    ccout = d["cc_xn0_out"] if layer == 0 else d["cc_xn1_out"]
    g1 = ctx.dma(ccin, s["xn"][:, :], eng="gpsimd")
    ctx.cc_xn = ctx.coll("AllGather", ALU.bypass, [d["cc_xn_in"].ap()], [ccout.ap()],
                         waits=(("gd", g1),))


def ph_assemble0(ctx):
    s, d = ctx.sb, ctx.dr
    if ctx.mode == "B":
        ctx.dma(s["kfull"][:, :], d["kf0_in"].ap())
        ctx.barrier()
        return
    agv = d["cc_xn0_out"].ap().rearrange("(r p c w) -> r p c w", r=NCORES, p=128, c=DC)
    kv = s["kfull"].ap().rearrange("p (c g w) -> p c g w", c=DC, g=2)
    cw = ctx.cc_xn

    def fetch(slot, off):
        def fn(e, slot=slot, off=off):
            reg = e.alloc_register()
            e.reg_load(reg, ctx.sb["ints"][0:1, slot:slot + 1])
            rv = e.snap(reg, min_val=0, max_val=7)
            return e.dma_start(out=kv[:, :, off, :], in_=agv[rv, :, :, :])
        return fn

    ctx.p.op("gpsimd", fetch(0, 0), waits=(("cc", cw),), inc="gd", inc_amt=16)
    ctx.p.op("gpsimd", fetch(3, 1), inc="gd", inc_amt=16)
    ctx.barrier()


def ph_assemble1(ctx):
    s, d = ctx.sb, ctx.dr
    if ctx.mode != "full":
        return
    agv = d["cc_xn1_out"].ap().rearrange("(r p c j w) -> r p c j w",
                                         r=NCORES, p=128, c=DC, j=NT)
    kv = d["kf1_dram"].ap().rearrange("p (c g w) -> p c g w", c=DC, g=2 * NT)
    cw = ctx.cc_xn

    _rv = {}

    def fetch(slot, off, dc):
        def fn(e, slot=slot, off=off, dc=dc):
            if slot not in _rv:
                reg = e.alloc_register()
                e.reg_load(reg, ctx.sb["ints"][0:1, slot:slot + 1])
                _rv[slot] = e.snap(reg, min_val=0, max_val=7)
            return e.dma_start(out=kv[:, dc, off:2 * NT:2, :], in_=agv[_rv[slot], :, dc, :, :])
        return fn

    for dc in range(DC):
        ctx.p.op("gpsimd", fetch(1, 0, dc), waits=(("cc", cw),) if dc == 0 else (),
                 inc="gd", inc_amt=16)
        ctx.p.op("gpsimd", fetch(2, 1, dc), inc="gd", inc_amt=16)
    ctx.barrier()


def emit_ffn0(ctx):
    s, d = ctx.sb, ctx.dr
    ctx.push_scope()
    H = ctx.alloc("ffnH", [128, 24 * 1024], BF16)
    wA = ctx.alloc("ffnWA", [128, 8 * 1024], BF16)
    wB = ctx.alloc("ffnWB", [128, 8 * 1024], BF16)
    wC = ctx.alloc("ffnWC", [128, 6 * 1024], BF16)
    sg = ctx.alloc("ffnSG", [128, 1024], BF16)
    wfree = [0, 0]
    dfree = [0, 0]
    h_done = 0
    for th in range(2):
        t0 = th * 1024
        for blk in range(6):
            src_g = d["Wg_t"].ap()[:, blk * 512:(blk + 1) * 512].rearrange(
                "(c p) h -> p c h", p=128)
            src_u = d["Wu_t"].ap()[:, blk * 512:(blk + 1) * 512].rearrange(
                "(c p) h -> p c h", p=128)
            half = (blk % 2) * 4096
            w = (("mm", wfree[blk % 2]),) if wfree[blk % 2] else ()
            gdg = ctx.dma(wA[:, half:half + 4096], src_g, waits=w, eng="gpsimd")
            gdu = ctx.dma(wB[:, half:half + 4096], src_u, waits=w, eng="gpsimd")
            for hq in range(4):
                hc = blk * 4 + hq
                mmg = mmu = 0
                for sub in range(2):
                    gb = ctx.bankf(sub)
                    for dc in range(DC):
                        lhs_g = wA[:, half + dc * 512 + hq * 128:
                                   half + dc * 512 + hq * 128 + 128]
                        rhs = s["xn"][:, dc * T + t0 + sub * 512:
                                      dc * T + t0 + sub * 512 + 512]
                        wt = []
                        if dc == 0 and sub == 0 and hq == 0:
                            wt.append(("gd", gdg))
                        if dc == 0:
                            wt.append(("ac", ctx.now("ac")))
                        mmg = ctx.mm(gb, lhs_g, rhs, start=(dc == 0), stop=(dc == DC - 1),
                                     waits=wt)
                for sub in range(2):
                    ub = ctx.bankf(2 + sub)
                    for dc in range(DC):
                        lhs_u = wB[:, half + dc * 512 + hq * 128:
                                   half + dc * 512 + hq * 128 + 128]
                        rhs = s["xn"][:, dc * T + t0 + sub * 512:
                                      dc * T + t0 + sub * 512 + 512]
                        wt = []
                        if dc == 0 and sub == 0 and hq == 0:
                            wt.append(("gd", gdu))
                        if dc == 0:
                            wt.append(("ve", ctx.now("ve")))
                        mmu = ctx.mm(ub, lhs_u, rhs, start=(dc == 0), stop=(dc == DC - 1),
                                     waits=wt)
                a1 = ctx.act(sg[:, :], ctx.sb["PS"][:, 0:1024], AF.Silu, scale=INV127,
                             waits=(("mm", mmg),))
                h_done = ctx.stt(H[:, hc * 1024:(hc + 1) * 1024],
                                 ctx.sb["PS"][:, 1024:2048], INV127, sg[:, :],
                                 ALU.mult, ALU.mult, waits=(("mm", mmu), ("ac", a1)))
            wfree[blk % 2] = ctx.now("mm")
        dbank_rd = getattr(ctx, "_ffn_dbank", [0, 0])
        for dco in range(DC):
            src = d["Wd_t"].ap()[:, dco * 128:(dco + 1) * 128].rearrange(
                "(c p) o -> p c o", p=128)
            half = (dco % 2) * 3072
            w = (("mm", dfree[dco % 2]),) if dfree[dco % 2] else ()
            gdd = ctx.dma(wC[:, half:half + 3072], src, waits=w, eng="gpsimd")
            for sub in range(2):
                zb = ctx.bankf(4 + sub)
                mmd = 0
                for hc in range(24):
                    lhs = wC[:, half + hc * 128: half + hc * 128 + 128]
                    rhs = H[:, hc * 1024 + sub * 512: hc * 1024 + sub * 512 + 512]
                    wt = []
                    if hc == 0:
                        wt = [("gd", gdd), ("ve", h_done)]
                        if dbank_rd[sub]:
                            wt.append(("ve", dbank_rd[sub]))
                    mmd = ctx.mm(zb, lhs, rhs, start=(hc == 0), stop=(hc == 23), waits=wt)
                xc = s["x"][:, dco * T + t0 + sub * 512: dco * T + t0 + sub * 512 + 512]
                dbank_rd[sub] = ctx.stt(xc, zb, INV127, xc, ALU.mult, ALU.add,
                                        waits=(("mm", mmd),))
            dfree[dco % 2] = ctx.now("mm")
        ctx._ffn_dbank = dbank_rd
    ctx.barrier()
    ctx.pop_scope()


def emit_negB_l0(ctx):
    s = ctx.sb
    r = s["rowsA"]
    outs = []
    for i, (so, bo) in enumerate(((0, 64), (128, 192))):
        srow = s["rows64"][0:1, so:so + 64]
        brow = s["rows64"][0:1, bo:bo + 64]
        ms = r[0:1, 0:1]
        nb = r[0:1, 1:2]
        v1 = ctx.red(ms, srow, ALU.max, absv=True)
        ctx.tt(r[0:1, 4:68], brow, brow, ALU.mult)
        v3 = ctx.red(nb, r[0:1, 4:68], ALU.add)
        a1 = ctx.act(nb, nb, AF.Sqrt, waits=(("ve", v3),))
        out = r[0:1, 2 + i:3 + i]
        outs.append(ctx.stt(out, ms, 8.0, nb, ALU.mult, ALU.add,
                            waits=(("ac", a1), ("ve", v1))))
    ctx.tt(r[0:1, 0:1], r[0:1, 2:3], r[0:1, 3:4], ALU.mult, waits=(("ve", outs[1]),))
    vB = ctx.ts(r[0:1, 0:1], r[0:1, 0:1], -0.125, None, ALU.mult)
    m1 = ctx.mm(ctx.bankf(6, width=1), s["onesrowf"][:, :], r[0:1, 0:1], waits=(("ve", vB),))
    ctx.copy(s["negB"][:, 0:1], ctx.bankf(6, width=1), waits=(("mm", m1),))
    ctx.ts(s["smalls"][:, 0:1], s["qk128"][:, 0:1], 0.125, None, ALU.mult)
    ctx.ts(s["smalls"][:, 1:2], s["qk128"][:, 1:2], 0.125, None, ALU.mult)
    vb0 = ctx.copy(s["sqsl"][:, 0:24], s["bias8"][:, 0:24])
    mb0 = ctx.mm(ctx.bankf(5, width=24), s["consts"][:, 256:384], s["sqsl"][:, 0:24],
                 waits=(("ve", vb0),))
    ctx.stt(s["biasc"][:, 0:24], ctx.bankf(5, width=24), -1.0 / 64.0, s["bias8"][:, 0:24],
            ALU.mult, ALU.add, waits=(("mm", mb0),))
    ctx.barrier()


def _flush_outproj(ctx, wo_dma, wC, oall, npairs=2):
    s = ctx.sb
    oready = ctx.now("ve")
    bank_rd = [0, 0]
    for dco in range(DC):
        for sub in range(4):
            zb = ctx.bankf(sub % 2)
            mmz = 0
            for i in range(npairs):
                lhs = wC[:, i * 1024 + dco * 128: i * 1024 + dco * 128 + 128]
                rhs = oall[:, i * T + sub * 512: i * T + sub * 512 + 512]
                wt = []
                if i == 0:
                    if bank_rd[sub % 2]:
                        wt.append(("ve", bank_rd[sub % 2]))
                    if dco == 0 and sub == 0:
                        wt += [("ve", oready), ("gd", wo_dma), ("ac", ctx.now("ac"))]
                mmz = ctx.mm(zb, lhs, rhs, start=(i == 0), stop=(i == npairs - 1),
                             waits=wt)
            xc = s["x"][:, dco * T + sub * 512: dco * T + sub * 512 + 512]
            bank_rd[sub % 2] = ctx.stt(xc, zb, 1.0, xc, ALU.mult, ALU.add,
                                       waits=(("mm", mmz),))
    ctx.barrier()


def emit_l0_attn(ctx):
    s, d = ctx.sb, ctx.dr
    ctx.push_scope()
    kfull = ctx.alloc("kfull", [128, DC * 2 * T], BF16)
    s["kfull"] = kfull
    ph_assemble0(ctx)
    kraw = ctx.alloc("kraw", [128, 2 * T], BF16)
    vhat = ctx.alloc("vhat", [128, 2 * T], BF16)
    vtm = ctx.alloc("vtm", [128, 32 * 2 * 65], BF16)
    qhat = ctx.alloc("qhat", [128, T], BF16)
    oall = ctx.alloc("oall", [128, 2 * T], BF16)
    wA = ctx.alloc("attnWA", [128, 3 * 1024], BF16)
    wC = ctx.alloc("attnWC", [128, 2 * 1024], BF16)
    psb = ctx.alloc("psb", [128, 2 * 1024], BF16)

    negb = s["negB"][:, 0:1]
    bconst = s["consts"][:, 128:256]
    oconst = s["consts"][:, 256:384]
    evac_tok = [0, 0]
    sqw = [0, 0]
    exp_done = [0, 0]
    pv_done = [0, 0]
    o_norm = 0
    wo_dma = 0
    wo_free = 0

    def pipe(base, raw, dst, n_tok, bcol, scol, b2col, srcf, gwait):
        nonlocal evac_tok, sqw
        last_ve = 0
        prev_vc = 0
        prev_a3 = 0
        for ss in range(n_tok // 512):
            pb = ctx.bankf(ss % 2)
            mmp = 0
            for dc in range(DC):
                lhs = wA[:, base + dc * 128: base + dc * 128 + 128]
                wt = []
                if dc == 0 and ss == 0:
                    wt.append(("gd", gwait))
                    wt.append(("ac", ctx.now("ac")))
                if dc == 0 and evac_tok[ss % 2]:
                    wt.append(("ac", evac_tok[ss % 2]))
                mmp = ctx.mm(pb, lhs, srcf(dc, ss), start=(dc == 0), stop=(dc == DC - 1),
                             waits=wt)
            a1 = ctx.act(raw[:, ss * 512:(ss + 1) * 512], pb, AF.Copy,
                         waits=(("mm", mmp), ("ve", last_ve)))
            evac_tok[ss % 2] = a1
            cb = ctx.bankf(2)
            mc = ctx.mm(cb, bconst, raw[:, ss * 512:(ss + 1) * 512],
                        waits=[("ac", a1)] + ([("ve", prev_vc)] if prev_vc else []))
            vc = ctx.ts(raw[:, ss * 512:(ss + 1) * 512], cb, bcol, None, ALU.add,
                        waits=(("mm", mc),))
            prev_vc = vc
            sq = s["sqsl"][:, (ss % 2) * 512:(ss % 2) * 512 + 512]
            a2 = ctx.act(sq, raw[:, ss * 512:(ss + 1) * 512], AF.Square,
                         waits=[("ve", vc)] + ([("mm", sqw[ss % 2])] if sqw[ss % 2] else []))
            vb = ctx.bankf(3)
            mv = ctx.mm(vb, oconst, sq,
                        waits=[("ac", a2)] + ([("ac", prev_a3)] if prev_a3 else []))
            sqw[ss % 2] = mv
            a3 = ctx.act(s["rbc"][:, :], vb, AF.Sqrt, scale=1.0 / 64.0,
                         bias=s["epsc"][:, 1:2], waits=(("mm", mv), ("ve", last_ve)))
            prev_a3 = a3
            ctx.recip(s["rstdb"][:, :], s["rbc"][:, :], waits=(("ac", a3),))
            ctx.tt(raw[:, ss * 512:(ss + 1) * 512], raw[:, ss * 512:(ss + 1) * 512],
                   s["rstdb"][:, :], ALU.mult)
            last_ve = ctx.ts(dst[:, ss * 512:(ss + 1) * 512],
                             raw[:, ss * 512:(ss + 1) * 512],
                             scol, b2col, ALU.mult, ALU.add)
        return last_ve

    _nhp = int(os.environ.get("KHP", str(NH // 2)))
    for hp in range(_nhp):
        wq = d["Wq"].ap()[:, hp * 128:(hp + 1) * 128].rearrange("(c p) o -> p c o", p=128)
        wk = d["Wk"].ap()[:, hp * 128:(hp + 1) * 128].rearrange("(c p) o -> p c o", p=128)
        wv = d["Wv"].ap()[:, hp * 128:(hp + 1) * 128].rearrange("(c p) o -> p c o", p=128)
        wo = d["Wo"].ap()[hp * 128:(hp + 1) * 128, :]
        wfree = ctx.now("mm")
        g1 = ctx.dma(wA[:, 0:1024], wq, waits=(("mm", wfree),), eng="gpsimd")
        g2 = ctx.dma(wA[:, 1024:2048], wk, eng="gpsimd")
        g3 = ctx.dma(wA[:, 2048:3072], wv, eng="gpsimd")
        wo_w = (("mm", wo_free),) if (hp % 2 == 0 and hp > 0) else ()
        g4 = ctx.dma(wC[:, (hp % 2) * 1024:(hp % 2) * 1024 + 1024], wo,
                     waits=wo_w, eng="gpsimd")
        if hp % 2 == 1:
            wo_dma = g4

        def ksrc(dc, ss):
            return kfull[:, dc * 2 * T + ss * 512: dc * 2 * T + ss * 512 + 512]

        kv_done = pipe(1024, kraw, kraw, 2 * T, s["biasc"][:, 8 + hp:9 + hp],
                       s["qk128"][:, 2:3], s["qk128"][:, 3:4], ksrc, g2)
        q_done = pipe(0, qhat, qhat, T, s["biasc"][:, 0 + hp:1 + hp],
                      s["smalls"][:, 0:1], s["smalls"][:, 1:2], ksrc, g1)
        v_done = pipe(2048, vhat, vhat, 2 * T, s["biasc"][:, 16 + hp:17 + hp],
                      s["qk128"][:, 4:5], s["qk128"][:, 5:6], ksrc, g3)
        ms1 = ctx.memset(vtm[:, :], 1.0, waits=(("mm", ctx.now("mm")),))
        pv8 = s["PSB"].ap().rearrange("p (j w) -> p j w", j=8)
        vv = vtm.ap().rearrange("p (c h w) -> p c h w", c=32, h=2)
        tr_ev = 0
        for kb in range(8):
            tr = 0
            for j in range(4):
                kc = kb * 4 + j
                wt = []
                if kb == 0 and j == 0:
                    wt.append(("ve", v_done))
                if kb >= 1 and j == 0:
                    wt.append(("ve", tr_ev))
                tr = ctx.trans(s["PSB"][:, j * 128:(j + 1) * 128],
                               vhat[:, kc * 128:(kc + 1) * 128],
                               s["consts"][:, 0:128], waits=wt)
            w0 = (("mm", tr), ("ve", ms1))
            ctx.copy(vv[:, kb * 4:kb * 4 + 4, 0, 0:64], pv8[:, 0:4, 0:64], waits=w0)
            tr_ev = ctx.copy(vv[:, kb * 4:kb * 4 + 4, 1, 0:64], pv8[:, 0:4, 64:128])
        vtm_done = tr_ev

        for qg in range(4):
            q0 = qg * 512
            ob = [ctx.bankf(4, p1=65), ctx.bankf(5, p1=65)]
            pend = None

            def emit_pv(kc, ea):
                nonlocal pv_done
                pb = kc % 2
                wv0 = [("ac", ea)]
                if kc == 0:
                    wv0.append(("ve", vtm_done))
                    if o_norm:
                        wv0.append(("ve", o_norm))
                ctx.mm(ob[0], vtm[:, (kc * 2) * 65:(kc * 2) * 65 + 65],
                       psb[:, pb * 1024:pb * 1024 + 512],
                       start=(kc == 0), stop=(kc == 31), waits=wv0)
                pv_done[pb] = ctx.mm(ob[1], vtm[:, (kc * 2 + 1) * 65:(kc * 2 + 1) * 65 + 65],
                                     psb[:, pb * 1024 + 512:pb * 1024 + 1024],
                                     start=(kc == 0), stop=(kc == 31))

            for kc in range(32):
                pb = kc % 2
                wt = []
                if kc == 0 and qg == 0:
                    wt += [("ve", q_done), ("ve", kv_done)]
                if exp_done[pb]:
                    wt.append(("ac", exp_done[pb]))
                ctx.mm(ctx.bankf(2 * pb), kraw[0:64, kc * 128:(kc + 1) * 128],
                       qhat[0:64, q0:q0 + 512], tp=(0, 0), waits=wt)
                m1 = ctx.mm(ctx.bankf(2 * pb + 1), kraw[64:128, kc * 128:(kc + 1) * 128],
                            qhat[64:128, q0:q0 + 512], tp=(64, 0))
                wexp = [("mm", m1)]
                if pv_done[pb]:
                    wexp.append(("mm", pv_done[pb]))
                ea = ctx.act(psb[:, pb * 1024:(pb + 1) * 1024],
                             ctx.sb["PS"][:, pb * 1024:(pb + 1) * 1024],
                             AF.Exp, bias=negb, waits=wexp)
                exp_done[pb] = ea
                if pend is not None:
                    emit_pv(*pend)
                pend = (kc, ea)
            emit_pv(*pend)
            for h in range(2):
                zrow = ctx.sb["PS"][64:65, (4 + h) * 512:(4 + h) * 512 + 512]
                rr = s["rowsA"][0:1, 0:512]
                ctx.ts(rr, zrow, 1e-38, None, ALU.add, waits=(("mm", pv_done[1]),))
                v2 = ctx.recip(rr, rr)
                bc = ctx.bankf(6, p1=64)
                m1 = ctx.mm(bc, s["onesrowf"][0:1, 0:64], rr, waits=(("ve", v2),))
                v3 = ctx.copy(s["rbc"][0:64, 0:512], bc, waits=(("mm", m1),))
                dst = oall[h * 64:h * 64 + 64, (hp % 2) * T + q0:(hp % 2) * T + q0 + 512]
                o_norm = ctx.tt(dst, ctx.sb["PS"][0:64, (4 + h) * 512:(4 + h) * 512 + 512],
                                s["rbc"][0:64, 0:512], ALU.mult,
                                waits=(("ve", v3), ("mm", getattr(ctx, "_flush_mm", 0))))
        if os.environ.get("KDBG") and ctx.mode == "B" and hp == _nhp - 1:
            ctx.barrier()
            ctx.dma(d["dbg_qh"].ap(), qhat[:, :])
            ctx.dma(d["dbg_kr"].ap(), kraw[:, :])
            ctx.dma(d["dbg_vt"].ap(), vtm[:, :])
            ctx.dma(d["dbg_nb"].ap(), s["negB"][:, 0:1])
            ctx.dma(d["dbg_oa"].ap(), oall[:, :])
            ctx.dma(d["dbg_wc"].ap(), wC[:, :])
            ctx.barrier()
        if hp % 2 == 1:
            _flush_outproj(ctx, wo_dma, wC, oall, npairs=2)
            exp_done = [0, 0]
            pv_done = [0, 0]
            o_norm = 0
            evac_tok = [0, 0]
            sqw = [0, 0]
        wo_free = ctx.now("mm")
    for dco in range(DC):
        xc = s["x"][:, dco * T:(dco + 1) * T]
        ctx.ts(xc, xc, 1.0, s["bias8"][:, 24 + dco:25 + dco], ALU.mult, ALU.add)
    ctx.barrier()
    ctx.pop_scope()


def emit_router_S(ctx):
    s, d = ctx.sb, ctx.dr
    ctx.push_scope()
    wR = ctx.alloc("wR", [128, 64], BF16)
    t12 = ctx.alloc("t12", [128, 16], BF16)
    g1 = ctx.dma(wR[:, :], d["RW"].ap().rearrange("(c p) e -> p c e", p=128),
                 eng="gpsimd")
    lg_done = 0
    for sub in range(4):
        lb = ctx.bankf(0, p1=8)
        mml = 0
        for dc in range(DC):
            lhs = wR[:, dc * 8:(dc + 1) * 8]
            rhs = s["xn"][:, dc * T + sub * 512: dc * T + sub * 512 + 512]
            wt = []
            if dc == 0 and sub == 0:
                wt.append(("gd", g1))
            if dc == 0 and sub >= 1:
                wt.append(("ve", lg_done))
            mml = ctx.mm(lb, lhs, rhs, start=(dc == 0), stop=(dc == DC - 1), waits=wt)
        lg_done = ctx.ts(s["maskfm"][0:8, sub * 512:(sub + 1) * 512], lb,
                         s["bias8"][0:8, 32:33], None, ALU.add, waits=(("mm", mml),))
    s0_done = s1_done = 0
    prev_ev = 0
    for ti in range(NT):
        tb = ctx.sb["PS"][0:128, 512:512 + 8]
        wt = [("ve", lg_done)] if ti == 0 else [("ve", prev_ev)]
        mt = ctx.trans(tb, s["maskfm"][0:8, ti * 128:(ti + 1) * 128],
                       s["identf"][0:8, 0:8], waits=wt)
        lt = s["masktok"][:, 0:8]
        ctx.copy(lt, tb, waits=(("mm", mt),))
        m8 = s["masktok"][:, 8:16]
        ctx.max8(m8, lt)
        oh1 = s["smalls"][:, 8:16]
        ctx.ts(oh1, lt, m8[:, 0:1], None, ALU.is_ge)
        l2 = s["smalls"][:, 16:24]
        ctx.stt(l2, oh1, -1e9, lt, ALU.mult, ALU.add)
        oh2 = s["smalls"][:, 24:32]
        ctx.ts(oh2, l2, m8[:, 1:2], None, ALU.is_ge)
        dd = s["smalls"][:, 32:33]
        v5 = ctx.tt(dd, m8[:, 1:2], m8[:, 0:1], ALU.subtract)
        e2 = s["smalls"][:, 33:34]
        a1 = ctx.act(e2, dd, AF.Exp, waits=(("ve", v5),))
        p1 = s["smalls"][:, 34:35]
        ctx.ts(p1, e2, 1.0, None, ALU.add, waits=(("ac", a1),))
        ctx.recip(p1, p1)
        p2 = s["smalls"][:, 35:36]
        ctx.ts(p2, p1, -1.0, 1.0, ALU.mult, ALU.add)
        t1 = s["smalls"][:, 40:48]
        t2 = s["smalls"][:, 48:56]
        ctx.ts(t1, oh1, p1[:, 0:1], None, ALU.mult)
        ctx.ts(t2, oh2, p2[:, 0:1], None, ALU.mult)
        mrow = s["masktok"][:, 16 + ti * 8: 16 + ti * 8 + 8]
        ctx.tt(mrow, t1, t2, ALU.add)
        wb = (("mm", s1_done),) if ti > 0 else ()
        v12 = ctx.copy(t12[:, 0:8], t1, waits=wb)
        prev_ev = ctx.copy(t12[:, 8:16], t2)
        s0_done = ctx.mm(ctx.sb["PS"][0:8, 2 * 512:2 * 512 + 1], t12[:, 0:8],
                         s["onescol"][:, :], start=(ti == 0), stop=(ti == NT - 1),
                         waits=(("ve", v12),))
        s1_done = ctx.mm(ctx.sb["PS"][0:8, 3 * 512:3 * 512 + 1], t12[:, 8:16],
                         s["onescol"][:, :], start=(ti == 0), stop=(ti == NT - 1),
                         waits=(("ve", prev_ev),))
    ctx.copy(s["smalls"][0:8, 60:61], ctx.sb["PS"][0:8, 2 * 512:2 * 512 + 1],
             waits=(("mm", s1_done),))
    vS1 = ctx.copy(s["smalls"][0:8, 61:62], ctx.sb["PS"][0:8, 3 * 512:3 * 512 + 1])
    if ctx.mode == "full":
        g2 = ctx.dma(d["cc_S_in"].ap().rearrange("(e k) -> e k", e=8),
                     s["smalls"][0:8, 60:62], waits=(("ve", vS1),), eng="gpsimd")
        ctx.cc_S = ctx.coll("AllReduce", ALU.add, [d["cc_S_in"].ap()],
                            [d["cc_S_out"].ap()], waits=(("gd", g2),))
    if ctx.mode == "B":
        ctx.dma(d["Sp_out"].ap(), s["smalls"][0:8, 60:62], waits=(("ve", vS1),))
    ctx.barrier()
    ctx.pop_scope()


def emit_mask_final(ctx):
    s, d = ctx.sb, ctx.dr
    if ctx.mode == "full":
        gl = ctx.dma(s["smalls"][0:8, 62:64],
                     d["cc_S_out"].ap().rearrange("(e k) -> e k", e=8),
                     waits=(("cc", ctx.cc_S),))
        wS = (("ld", gl),)
    else:
        ctx.red(s["smalls"][0:8, 62:63], s["bias8"][0:8, 0:8], ALU.add)
        v2 = ctx.red(s["smalls"][0:8, 63:64], s["bias8"][0:8, 8:16], ALU.add)
        wS = (("ve", v2),)
    vf = ctx.tt(s["smalls"][0:8, 62:64], s["smalls"][0:8, 62:64], s["flagS"][:, :],
                ALU.mult, waits=wS)
    va = ctx.tt(s["masktok"][0:8, 16:18], s["masktok"][0:8, 16:18],
                s["smalls"][0:8, 62:64], ALU.add, waits=(("ve", vf),))
    vc = ctx.ts(s["masktok"][:, 16:24], s["masktok"][:, 16:24], CAP, None, ALU.min,
                waits=(("ve", va),))
    tr = vc
    for ti in range(NT):
        tb = ctx.sb["PS"][0:8, 512:512 + 128]
        mt = ctx.trans(tb, s["masktok"][:, 16 + ti * 8:16 + ti * 8 + 8],
                       s["identf"][:, :], waits=(("ve", tr),))
        tr = ctx.copy(s["maskfm"][0:8, ti * 128:(ti + 1) * 128], tb, waits=(("mm", mt),))
    ctx.copy(s["maskfmb"][0:8, :], s["maskfm"][0:8, :])
    ctx.barrier()


def emit_moe(ctx):
    s, d = ctx.sb, ctx.dr
    ctx.push_scope()
    Hm = ctx.alloc("moeH", [128, 8 * 1024], BF16)
    wA = ctx.alloc("moeWA", [128, 4 * 1024], BF16)
    wB = ctx.alloc("moeWB", [128, 4 * 1024], BF16)
    mbc = ctx.alloc("moeMBC", [128, 1024], F32)
    mbcb = ctx.alloc("moeMBCB", [128, 1024], BF16)
    ttmp = ctx.alloc("moeT", [128, 1024], BF16)
    b1sb = ctx.alloc("b1sb", [128, 8], F32)
    b2sb = ctx.alloc("b2sb", [8, D], BF16)
    mr0 = ctx.alloc("mrow0", [1, 1024], F32)
    mmlast = 0
    bcfree = 0
    bc_ve = 0
    b1_free = 0
    for ei in range(NE):
        gb1 = ctx.dma(b1sb[:, :], d["b1t"].ap()[ei, :, :],
                      waits=(("ac", b1_free),) if b1_free else (), eng="gpsimd")
        for th in range(2):
            t0 = th * 1024
            mb_done = 0
            gmr = ctx.dma(mr0[:, :], s["maskfm"][ei:ei + 1, t0:t0 + 1024],
                          waits=(("mm", bcfree),) if bcfree else (), eng="gpsimd")
            for sub in range(2):
                bc = ctx.bankf(6)
                wt = [("gd", gmr)]
                if bc_ve:
                    wt.append(("ve", bc_ve))
                m1 = ctx.mm(bc, s["onesrowf"][:, :],
                            mr0[0:1, sub * 512: sub * 512 + 512], waits=wt)
                bc_ve = ctx.copy(mbc[:, sub * 512:(sub + 1) * 512], bc, waits=(("mm", m1),))
                mb_done = ctx.copy(mbcb[:, sub * 512:(sub + 1) * 512],
                                   mbc[:, sub * 512:(sub + 1) * 512])
            bcfree = ctx.now("mm")
            for wh in range(2):
                w1s = d["w1"].ap()[ei, :, wh * 512:(wh + 1) * 512].rearrange(
                    "(c p) h -> p c h", p=128)
                wt = (("mm", mmlast),) if (ei > 0 or th > 0 or wh > 0) else ()
                gw1 = ctx.dma(wA[:, :], w1s, waits=wt, eng="gpsimd")
                for hq in range(4):
                    hc = wh * 4 + hq
                    gset = 2 * (hc % 2)
                    mmg = 0
                    for sub in range(2):
                        gb = ctx.bankf(gset + sub)
                        for dc in range(DC):
                            lhs = wA[:, dc * 512 + hq * 128: dc * 512 + hq * 128 + 128]
                            rhs = s["xn"][:, dc * T + t0 + sub * 512:
                                          dc * T + t0 + sub * 512 + 512]
                            wtm = []
                            if dc == 0 and sub == 0 and hq == 0:
                                wtm.append(("gd", gw1))
                            if dc == 0:
                                wtm.append(("ve", ctx.now("ve")))
                            mmg = ctx.mm(gb, lhs, rhs, start=(dc == 0),
                                         stop=(dc == DC - 1), waits=wtm)
                    vt = ctx.tt(ttmp[:, :], ctx.sb["PS"][:, gset * 512:gset * 512 + 1024],
                                mbc[:, :], ALU.mult, waits=(("mm", mmg), ("ve", mb_done)))
                    wg = [("ve", vt)]
                    if ei == 0 and th == 0 and hc == 0:
                        wg.append(("gd", gb1))
                    a1 = ctx.act(Hm[:, hc * 1024:(hc + 1) * 1024], ttmp[:, :], AF.Gelu,
                                 bias=b1sb[:, hc:hc + 1], waits=wg)
                    b1_free = a1
                    ctx.tt(Hm[:, hc * 1024:(hc + 1) * 1024],
                           Hm[:, hc * 1024:(hc + 1) * 1024], mbcb[:, :], ALU.mult,
                           waits=(("ac", a1),))
                mmlast = ctx.now("mm")
            h_done = ctx.now("ve")
            for zh in range(2):
                w2s = d["w2"].ap()[ei, :, zh * 512:(zh + 1) * 512].rearrange(
                    "(c p) o -> p c o", p=128)
                gw2 = ctx.dma(wB[:, :], w2s, waits=(("mm", mmlast),), eng="gpsimd")
                for dq in range(4):
                    dco = zh * 4 + dq
                    for sub in range(2):
                        zb = ctx.bankf(4 + (sub % 2))
                        mmz = 0
                        for hc in range(NE):
                            lhs = wB[:, hc * 512 + dq * 128: hc * 512 + dq * 128 + 128]
                            rhs = Hm[:, hc * 1024 + sub * 512: hc * 1024 + sub * 512 + 512]
                            wtm = []
                            if hc == 0:
                                wtm.append(("ve", ctx.now("ve")))
                            if hc == 0 and dq == 0 and sub == 0:
                                wtm += [("gd", gw2), ("ve", h_done)]
                            mmz = ctx.mm(zb, lhs, rhs, start=(hc == 0), stop=(hc == NE - 1),
                                         waits=wtm)
                        xc = s["x"][:, dco * T + t0 + sub * 512:
                                    dco * T + t0 + sub * 512 + 512]
                        ctx.stt(xc, zb, 1.0, xc, ALU.mult, ALU.add, waits=(("mm", mmz),))
                mmlast = ctx.now("mm")
    gb2 = ctx.dma(b2sb[:, :], d["b2e"].ap(), eng="gpsimd")
    bank_rd2 = [0, 0]
    for dco in range(DC):
        for sub in range(4):
            zb = ctx.bankf(sub % 2)
            wt = []
            if dco == 0 and sub == 0:
                wt = [("gd", gb2), ("mm", mmlast), ("ve", ctx.now("ve"))]
            if bank_rd2[sub % 2]:
                wt.append(("ve", bank_rd2[sub % 2]))
            mmz = ctx.mm(zb, b2sb[0:8, dco * 128:(dco + 1) * 128],
                         s["maskfmb"][0:8, sub * 512:(sub + 1) * 512], waits=wt)
            xc = s["x"][:, dco * T + sub * 512: dco * T + sub * 512 + 512]
            bank_rd2[sub % 2] = ctx.stt(xc, zb, 1.0, xc, ALU.mult, ALU.add,
                                        waits=(("mm", mmz),))
    ctx.barrier()
    ctx.pop_scope()


def emit_qlr_prep(ctx):
    s, d = ctx.sb, ctx.dr
    ctx.push_scope()
    qraw = ctx.alloc("qraw", [128, T], BF16)
    wA = ctx.alloc("qlrWA", [128, 1024], BF16)
    ctx.ts(s["cscaled"][0:16, :], s["coreT"][0:16, :], 0.25, None, ALU.mult)
    evac = [0, 0]
    for hp in range(NH // 2):
        wqc = d["Wqkv"].ap()[:, hp * 128:(hp + 1) * 128].rearrange("(c p) o -> p c o", p=128)
        g1 = ctx.dma(wA[:, :], wqc, waits=(("mm", ctx.now("mm")),), eng="gpsimd")
        ap_done = 0
        for ss in range(4):
            pb = ctx.bankf(ss % 2)
            mmp = 0
            for dc in range(DC):
                lhs = wA[:, dc * 128:(dc + 1) * 128]
                rhs = s["xn"][:, dc * T + ss * 512: dc * T + ss * 512 + 512]
                wt = []
                if dc == 0 and ss == 0:
                    wt.append(("gd", g1))
                if dc == 0 and evac[ss % 2]:
                    wt.append(("ac", evac[ss % 2]))
                mmp = ctx.mm(pb, lhs, rhs, start=(dc == 0), stop=(dc == DC - 1), waits=wt)
            ap_done = ctx.act(qraw[:, ss * 512:(ss + 1) * 512], pb, AF.Copy,
                              waits=(("mm", mmp), ("ve", ctx.now("ve"))))
            evac[ss % 2] = ap_done
        for h01 in range(2):
            h = hp * 2 + h01
            rb = h01 * 64
            for ss in range(4):
                lb = ctx.sb["PS"][0:16, 2 * 512: 2 * 512 + 512]
                mq = ctx.mm(lb, s["lsr"][rb:rb + 64, h * 16: h * 16 + 16],
                            qraw[rb:rb + 64, ss * 512:(ss + 1) * 512],
                            waits=(("ac", ap_done), ("ve", ctx.now("ve"))),
                            tp=(rb, 0) if rb else None)
                dst = s["qlr"][rb:rb + 16, hp * T + ss * 512: hp * T + ss * 512 + 512]
                ctx.ts(dst, lb, s["cscaled"][0:16, h:h + 1] if rb == 0 else
                       s["cscaled"][0:16, h:h + 1], None, ALU.mult, waits=(("mm", mq),))
    ctx.barrier()
    ctx.pop_scope()


def emit_l1_attn(ctx):
    s, d = ctx.sb, ctx.dr
    ctx.push_scope()
    slab = ctx.alloc("slab", [128, 2 * 4096], BF16)
    klr = ctx.alloc("klr", [128, 2 * 4096], BF16)
    kraw = ctx.alloc("kraw1", [128, 4096], BF16)
    vtm4 = ctx.alloc("vtm4", [128, 32 * 4 * 65], BF16)
    oall = ctx.alloc("oall1", [128, 2 * T], BF16)
    wAk = ctx.alloc("wAk", [128, 2 * 1024], BF16)
    wAv = ctx.alloc("wAv", [128, 2 * 1024], BF16)
    wC = ctx.alloc("l1WC", [128, 2 * 1024], BF16)
    psb = ctx.alloc("psb1", [128, 1024], BF16)

    if ctx.mode == "C":
        kf_src = d["kf1_in"].ap()
    else:
        kf_src = d["kf1_dram"].ap()
    kfv = kf_src.rearrange("p (c g w) -> p c g w", c=DC, g=2 * NT)

    exp_done = [0, 0]
    pv_done = [0, 0]
    o_norm = 0
    for quad in range(4):
        wv_src = d["Wqkv"].ap()[:, 2 * D + quad * 256: 2 * D + (quad + 1) * 256].rearrange(
            "(c p) o -> p c o", p=128)
        gv = ctx.dma(wAv[:, :], wv_src, waits=(("mm", ctx.now("mm")),), eng="gpsimd")
        for pr in range(2):
            hp = quad * 2 + pr
            wk_src = d["Wqkv"].ap()[:, D + hp * 128: D + hp * 128 + 128].rearrange(
                "(c p) o -> p c o", p=128)
            ctx.dma(wAk[:, pr * 1024:(pr + 1) * 1024], wk_src,
                    waits=(("mm", ctx.now("mm")),), eng="gpsimd")
        gwk = ctx.now("gd")
        ms1 = ctx.memset(vtm4[:, :], 1.0, waits=(("mm", ctx.now("mm")),))
        slab_free = [0, 0]
        kevac = [0, 0]
        klr_done = 0
        vdone = 0
        for ss in range(8):
            sl = slab[:, (ss % 2) * 4096:(ss % 2) * 4096 + 4096]
            wt = [("mm", slab_free[ss % 2])] if slab_free[ss % 2] else []
            gsl = ctx.dma(sl, kfv[:, :, 4 * ss:4 * ss + 4, :], waits=wt, eng="gpsimd")
            for pr in range(2):
                pb = ctx.bankf(pr)
                mmp = 0
                for dc in range(DC):
                    lhs = wAk[:, pr * 1024 + dc * 128: pr * 1024 + dc * 128 + 128]
                    rhs = sl[:, dc * 512:(dc + 1) * 512]
                    wtm = []
                    if dc == 0:
                        wtm.append(("gd", gsl))
                        if ss == 0:
                            wtm.append(("gd", gwk))
                            wtm.append(("ac", ctx.now("ac")))
                        if kevac[pr]:
                            wtm.append(("ac", kevac[pr]))
                    mmp = ctx.mm(pb, lhs, rhs, start=(dc == 0), stop=(dc == DC - 1),
                                 waits=wtm)
                ke = ctx.act(kraw[:, ss * 512:(ss + 1) * 512], pb, AF.Copy,
                             waits=(("mm", mmp), ("ve", klr_done)))
                kevac[pr] = ke
                for h01 in range(2):
                    rb = h01 * 64
                    lb = ctx.sb["PS"][0:16, 2 * 512: 2 * 512 + 512]
                    hh = (quad * 2 + pr) * 2 + h01
                    mq = ctx.mm(lb, s["lsr"][rb:rb + 64, 256 + hh * 16:
                                             256 + hh * 16 + 16],
                                kraw[rb:rb + 64, ss * 512:(ss + 1) * 512],
                                waits=(("ac", ke), ("ve", ctx.now("ve"))),
                                tp=(rb, 0) if rb else None)
                    dst = klr[rb:rb + 16, pr * 4096 + ss * 512: pr * 4096 + ss * 512 + 512]
                    klr_done = ctx.copy(dst, lb, waits=(("mm", mq),))
            for j in range(4):
                kc = ss * 4 + j
                vb = ctx.bankf(3, width=256)
                mmv = 0
                for dc in range(DC):
                    lhs = sl[:, dc * 512 + j * 128: dc * 512 + j * 128 + 128]
                    rhs = wAv[:, dc * 256:(dc + 1) * 256]
                    wtm = []
                    if dc == 0:
                        if kc == 0:
                            wtm += [("gd", gv), ("ve", ms1)]
                        if vdone:
                            wtm.append(("ve", vdone))
                    mmv = ctx.mm(vb, lhs, rhs, start=(dc == 0), stop=(dc == DC - 1),
                                 waits=wtm)
                pvv = ctx.sb["PS"].ap()[:, 3 * 512:3 * 512 + 256].rearrange(
                    "p (h w) -> p h w", h=4)
                vv = vtm4.ap().rearrange("p (c h w) -> p c h w", c=32, h=4)
                vdone = ctx.copy(vv[:, kc, :, 0:64], pvv, waits=(("mm", mmv),))
            slab_free[ss % 2] = ctx.now("mm")
        for h4 in range(4):
            h = quad * 4 + h4
            hp = quad * 2 + h4 // 2
            rb = (h4 % 2) * 64
            outs = []
            for idx, (buf, off, n_tok) in enumerate(
                    ((s["qlr"], (quad * 2 + h4 // 2) * T, T),
                     (klr, (h4 // 2) * 4096, 2 * T))):
                sqb = kraw[rb:rb + 16, 0:2048]
                nss = n_tok // 512
                mx = s["rowsA"][0:1, 64 + h4 * 2 + idx: 64 + h4 * 2 + idx + 1]
                first = True
                for half in range((n_tok + 2047) // 2048):
                    a1 = ctx.act(sqb, buf[rb:rb + 16, off + half * 2048: off + half * 2048 + 2048],
                                 AF.Square, waits=(("ve", ctx.now("ve")),))
                    for ssq in range(4):
                        bnk = ctx.bankf(2 + (ssq % 2), p1=1)
                        mmn = ctx.mm(bnk, s["onescol"][rb:rb + 16, :],
                                     sqb[:, ssq * 512:(ssq + 1) * 512],
                                     waits=(("ac", a1), ("ve", ctx.now("ve"))),
                                     tp=(rb, 0) if rb else None)
                        tmp = s["rowsA"][0:1, 60:61]
                        ctx.red(tmp, bnk, ALU.max, waits=(("mm", mmn),))
                        if first:
                            ctx.copy(mx, tmp)
                            first = False
                        else:
                            ctx.tt(mx, mx, tmp, ALU.max)
                outs.append(mx)
            pr_ = s["rowsA"][0:1, 61:62]
            v1 = ctx.tt(pr_, outs[0], outs[1], ALU.mult)
            a2 = ctx.act(pr_, pr_, AF.Sqrt, waits=(("ve", v1),))
            v2 = ctx.ts(pr_, pr_, -1.0, None, ALU.mult, waits=(("ac", a2),))
            m1 = ctx.mm(ctx.bankf(6, width=1), s["onesrowf"][:, :], pr_,
                        waits=(("ve", v2), ("ve", ctx.now("ve"))))
            ctx.copy(s["negB"][:, 2 + h:3 + h], ctx.bankf(6, width=1), waits=(("mm", m1),))
        wo1 = d["Wo1"].ap()[quad * 256:(quad + 1) * 256, :].rearrange(
            "(i p) o -> p i o", p=128)
        wo_dma = ctx.dma(wC[:, :], wo1, waits=(("mm", ctx.now("mm")),), eng="gpsimd")
        for h4 in range(4):
            h = quad * 4 + h4
            rb = (h4 % 2) * 64
            pr = h4 // 2
            negb1 = s["negB"][:, 2 + h:3 + h]
            for qg in range(8):
                q0 = qg * 256
                kmax = 4 * qg + 4
                ob = ctx.bankf(4, p1=65, width=256)
                pend = None

                def emit_pv1(kc2, ea, mask_done):
                    nonlocal pv_done
                    g2i = (kc2 // 2) % 2
                    for j in range(2):
                        kc = kc2 + j
                        wv0 = [("ac", ea)]
                        if mask_done:
                            wv0.append(("ve", mask_done))
                        if kc == 0 and o_norm:
                            wv0.append(("ve", o_norm))
                        pv_done[g2i] = ctx.mm(
                            ob, vtm4[:, (kc * 4 + h4) * 65:(kc * 4 + h4) * 65 + 65],
                            psb[:, g2i * 512 + j * 256: g2i * 512 + j * 256 + 256],
                            start=(kc == 0), stop=(kc == kmax - 1), waits=wv0)

                for kc2 in range(0, kmax, 2):
                    g2i = (kc2 // 2) % 2
                    wt = []
                    if exp_done[g2i]:
                        wt.append(("ac", exp_done[g2i]))
                    if kc2 == 0 and qg == 0 and h4 == 0:
                        wt.append(("ve", vdone))
                        wt.append(("ve", ctx.now("ve")))
                    ctx.mm(ctx.bankf(g2i, lo=0, width=256),
                           klr[rb:rb + 16, pr * 4096 + kc2 * 128: pr * 4096 + kc2 * 128 + 128],
                           s["qlr"][rb:rb + 16, (quad * 2 + pr) * T + q0:
                                    (quad * 2 + pr) * T + q0 + 256],
                           tp=(rb, 0) if rb else None, waits=wt)
                    m1 = ctx.mm(ctx.bankf(g2i, lo=256, width=256),
                                klr[rb:rb + 16, pr * 4096 + kc2 * 128 + 128:
                                    pr * 4096 + kc2 * 128 + 256],
                                s["qlr"][rb:rb + 16, (quad * 2 + pr) * T + q0:
                                         (quad * 2 + pr) * T + q0 + 256],
                                tp=(rb, 0) if rb else None)
                    wexp = [("mm", m1)]
                    if pv_done[g2i]:
                        wexp.append(("mm", pv_done[g2i]))
                    psl = psb[:, g2i * 512:(g2i + 1) * 512]
                    ea = ctx.act(psl, ctx.bankf(g2i), AF.Exp, bias=negb1, waits=wexp)
                    exp_done[g2i] = ea
                    mask_done = 0
                    for j in range(2):
                        kc = kc2 + j
                        i = kc - (kmax - 4)
                        if i >= 0:
                            msl = psb[:, g2i * 512 + j * 256: g2i * 512 + j * 256 + 256]
                            mask_done = ctx.tt(msl, msl,
                                               s["cmask"][:, i * 256:(i + 1) * 256],
                                               ALU.mult, waits=(("ac", ea),))
                    if pend is not None:
                        emit_pv1(*pend)
                    pend = (kc2, ea, mask_done)
                emit_pv1(*pend)
                zrow = ctx.sb["PS"][64:65, 4 * 512: 4 * 512 + 256]
                rr = s["rowsA"][0:1, 0:256]
                ctx.ts(rr, zrow, 1e-38, None, ALU.add,
                       waits=(("mm", pv_done[0] or pv_done[1]),
                              ("mm", pv_done[1] or pv_done[0])))
                v2 = ctx.recip(rr, rr)
                bc = ctx.bankf(6, p1=64, width=256)
                m1 = ctx.mm(bc, s["onesrowf"][0:1, 0:64], rr, waits=(("ve", v2),))
                v3 = ctx.copy(s["rbc"][0:64, 0:256], bc, waits=(("mm", m1),))
                dst = oall[rb:rb + 64, pr * T + q0: pr * T + q0 + 256]
                o_norm = ctx.tt(dst, ctx.sb["PS"][0:64, 4 * 512:4 * 512 + 256],
                                s["rbc"][0:64, 0:256], ALU.mult,
                                waits=(("ve", v3), ("mm", getattr(ctx, "_flush_mm", 0))))
        if os.environ.get("KDBG") and ctx.mode == "C" and quad == int(os.environ.get("KDBGQ", "3")):
            ctx.barrier()
            ctx.dma(d["dbg_qlr"].ap(), s["qlr"][:, :])
            ctx.dma(d["dbg_klr"].ap(), klr[:, :])
            ctx.dma(d["dbg_vt4"].ap(), vtm4[:, :])
            ctx.dma(d["dbg_nb1"].ap(), s["negB"][:, :])
            ctx.dma(d["dbg_oa1"].ap(), oall[:, :])
            ctx.barrier()
        _flush_outproj(ctx, wo_dma, wC, oall, npairs=2)
        exp_done = [0, 0]
        pv_done = [0, 0]
        o_norm = 0
    ctx.barrier()
    ctx.pop_scope()


def ph_store(ctx):
    s, d = ctx.sb, ctx.dr
    if ctx.mode == "A":
        g = ctx.dma(d["xn_out"].ap(), s["xn"][:, :])
        ctx.p.op("sync", lambda e: e.nop(), waits=(("ld", g),))
        return
    if ctx.mode == "B":
        ctx.dma(d["x1_out"].ap().rearrange("(c p) t -> p c t", p=128), s["x"][:, :])
        g2 = ctx.dma(d["xn2_out"].ap(), s["xn"][:, :])
        ctx.p.op("sync", lambda e: e.nop(), waits=(("ld", g2),))
        return
    g = ctx.dma(d["y"].ap().rearrange("(c p) t -> p c t", p=128), s["x"][:, :])
    ctx.p.op("sync", lambda e: e.nop(), waits=(("ld", g),))


def emit_all(ctx):
    ph_load(ctx)
    if ctx.mode == "A":
        ctx.sb["xn"] = ctx.alloc("xn", [128, DC * T], BF16)
        emit_ln(ctx)
        ph_store(ctx)
        return
    if ctx.mode == "B":
        upto = int(os.environ.get("KUPTO", "9"))
        ctx.push_scope()
        ctx.sb["xn"] = ctx.alloc("xn", [128, DC * T], BF16)
        ctx.dma(ctx.sb["xn"][:, :], ctx.dr["xn_own_in"].ap())
        ctx.barrier()
        if upto >= 1:
            emit_negB_l0(ctx)
        if upto >= 2:
            emit_ffn0(ctx)
        ctx.pop_scope()
        if upto >= 3:
            emit_l0_attn(ctx)
        ctx.push_scope()
        ctx.sb["xn"] = ctx.alloc("xn", [128, DC * T], BF16)
        ctx.sb["maskfm"] = ctx.alloc("maskfm", [8, T], F32)
        if upto >= 4:
            emit_ln(ctx)
        if upto >= 5:
            emit_router_S(ctx)
        ph_store(ctx)
        ctx.pop_scope()
        return
    if ctx.mode == "C":
        ctx.push_scope()
        ctx.sb["qlr"] = ctx.alloc("qlr", [128, 8 * T], BF16)
        ctx.push_scope()
        ctx.sb["xn"] = ctx.alloc("xn", [128, DC * T], BF16)
        ctx.sb["maskfm"] = ctx.alloc("maskfm", [8, T], F32)
        ctx.sb["maskfmb"] = ctx.alloc("maskfmb", [8, T], BF16)
        ctx.dma(ctx.sb["xn"][:, :], ctx.dr["xn2own"].ap())
        ctx.barrier()
        emit_router_S(ctx)
        emit_mask_final(ctx)
        emit_qlr_prep(ctx)
        if os.environ.get("KC_MOE", "1") == "1":
            emit_moe(ctx)
        ctx.pop_scope()
        if os.environ.get("KC_ATTN", "1") == "1":
            emit_l1_attn(ctx)
        ctx.pop_scope()
        ph_store(ctx)
        return
    up = int(os.environ.get("KF_UP", "9"))
    ctx.push_scope()
    ctx.sb["xn"] = ctx.alloc("xn", [128, DC * T], BF16)
    emit_ln(ctx)
    emit_gather_issue(ctx, 0)
    emit_negB_l0(ctx)
    if up >= 1:
        emit_ffn0(ctx)
    ctx.pop_scope()
    if up >= 2:
        emit_l0_attn(ctx)
    if up >= 3:
        ctx.push_scope()
        ctx.sb["qlr"] = ctx.alloc("qlr", [128, 8 * T], BF16)
        ctx.push_scope()
        ctx.sb["xn"] = ctx.alloc("xn", [128, DC * T], BF16)
        ctx.sb["maskfm"] = ctx.alloc("maskfm", [8, T], F32)
        ctx.sb["maskfmb"] = ctx.alloc("maskfmb", [8, T], BF16)
        emit_ln(ctx)
        emit_router_S(ctx)
        emit_gather_issue(ctx, 1)
        emit_mask_final(ctx)
        if os.environ.get("KDBG"):
            ctx.dma(ctx.dr["dbg_sm"].ap(), ctx.sb["smalls"][:, :])
            ctx.dma(ctx.dr["dbg_mt"].ap(), ctx.sb["masktok"][:, :])
            ctx.barrier()
        emit_qlr_prep(ctx)
        if up >= 4:
            emit_moe(ctx)
        ph_assemble1(ctx)
        ctx.pop_scope()
        if up >= 5:
            emit_l1_attn(ctx)
        ctx.pop_scope()
    ph_store(ctx)


def build_program(mode):
    nc = bass.Bass("TRN2", target_bir_lowering=False)
    ctx = Ctx(nc, mode)
    declare_dram(ctx)
    root = ExitStack()
    ctx.scopes = [root]
    with root:
        declare_sbuf_persistent(ctx)
        sems = {}
        for snm in ("mm", "ve", "ac", "ld", "gd", "cc"):
            sems[snm] = root.enter_context(nc.semaphore(snm))
        block = root.enter_context(nc.Block())
        emit_all(ctx)
        with nc.allow_low_precision(reason="bf16 operands with fp32 psum accumulation by design"):
            ctx.p.emit(block, sems)
    return nc


def kernel(**inputs):
    w = host_weights(inputs)
    shared = host_constants()
    x0 = np.asarray(inputs["L0_attn_x"], np.float32)

    nc = build_program("full")
    maps = []
    for core in range(NCORES):
        dd = dict(shared)
        dd.update(core_constants(core))
        dd["x0"] = _f32(x0[core // 2][own_rows(core)].T)
        for knm in ("Wq", "Wk", "Wv", "Wo", "bq", "bk", "bv", "bo",
                    "Wg_t", "Wu_t", "Wd_t", "RW", "Rb",
                    "Wqkv", "Wo1", "Wqlsr", "Wklsr", "coreT",
                    "w1", "w2", "b1t", "b2e"):
            dd[knm] = w[knm]
        for nm in ("q", "k", "v"):
            for suf in ("s128", "b128", "srow", "brow"):
                dd[f"{nm}{suf}"] = w[f"{nm}{suf}"]
        maps.append(dd)
    res = run_bass_kernel_spmd(nc, maps, core_ids=list(range(NCORES)))

    out = np.empty((B, S, D), dtype=np.float32)
    for core in range(NCORES):
        b = core // 2
        p = core % 2
        y = np.asarray(res.results[core]["y"], np.float32).T
        for j in range(NT):
            out[b, (2 * j + p) * 128:(2 * j + p + 1) * 128, :] = y[j * 128:(j + 1) * 128]
    return out


# version 06
# speedup vs baseline: 1.4777x; 1.1665x over previous
import os
import numpy as np
import ml_dtypes
from contextlib import ExitStack
import concourse.bass as bass
import concourse.mybir as mybir
from concourse.bass_utils import run_bass_kernel_spmd

F32 = mybir.dt.float32
BF16 = mybir.dt.bfloat16
I32 = mybir.dt.int32
AF = mybir.ActivationFunctionType
ALU = mybir.AluOpType
AX = mybir.AxisListType

B, S, D, NH, HD, R = 4, 4096, 1024, 16, 64, 16
T = 2048
NT = 16
DC = 8
HID = 3072
NE, MH = 8, 1024
CAP = 2560.0
EPS_LN = 1e-5
EPS_QK = 1e-6
INV127 = 1.0 / 127.0
NCORES = 8
XN_ELEMS = 128 * DC * T


class Prog:
    ENGINES = ("tensor", "vector", "scalar", "gpsimd", "sync")

    def __init__(self):
        self.ops = []
        self.cnt = {}
        self.pending_barrier = {}

    def sem(self, name):
        self.cnt[name] = 0
        return name

    SELF_SERIAL = {"vector": "ve"}

    def op(self, engine, fn, waits=(), inc=(), inc_amt=1):
        if isinstance(inc, str):
            inc = (inc,)
        w = list(self.pending_barrier.pop(engine, ())) + list(waits)
        ss = self.SELF_SERIAL.get(engine)
        if ss is not None and self.cnt[ss] > 0:
            w.append((ss, self.cnt[ss]))
        incs = []
        for snm in inc:
            self.cnt[snm] += inc_amt
            incs.append((snm, inc_amt))
        self.ops.append((engine, fn, tuple(w), tuple(incs)))
        return self.cnt[inc[0]] if inc else None

    def barrier(self, sems):
        snap = [(snm, self.cnt[snm]) for snm in sems if self.cnt[snm] > 0]
        for e in self.ENGINES:
            cur = self.pending_barrier.get(e, [])
            self.pending_barrier[e] = cur + snap

    def emit(self, block, handles):
        per = {e: [] for e in self.ENGINES}
        for rec in self.ops:
            per[rec[0]].append(rec)

        def mk(engine_name):
            def body(eng):
                for (_, fn, waits, incs) in per[engine_name]:
                    seen = {}
                    for (snm, v) in waits:
                        if seen.get(snm, -1) < v:
                            seen[snm] = v
                    for snm, v in seen.items():
                        eng.wait_ge(handles[snm], v)
                    inst = fn(eng)
                    for (snm, a) in incs:
                        inst.then_inc(handles[snm], a)
            return body

        block.tensor(mk("tensor"))
        block.vector(mk("vector"))
        block.scalar(mk("scalar"))
        block.gpsimd(mk("gpsimd"))
        block.sync(mk("sync"))


class Ctx:
    def __init__(self, nc, mode):
        self.nc = nc
        self.mode = mode
        self.p = Prog()
        for snm in ("mm", "ve", "ac", "ld", "gd", "cc"):
            self.p.sem(snm)
        self.dr = {}
        self.sb = {}

    def mm(self, out, lhsT, rhs, start=True, stop=True, waits=(), tp=None):
        def fn(e, out=out, lhsT=lhsT, rhs=rhs, start=start, stop=stop, tp=tp):
            return e.matmul(out, lhsT, rhs, start=start, stop=stop, tile_position=tp)
        return self.p.op("tensor", fn, waits=waits, inc="mm")

    def trans(self, out, in_, ident, waits=()):
        return self.p.op("tensor",
                         lambda e, o=out, i=in_, d=ident: e.transpose(o, i, d),
                         waits=waits, inc="mm")

    def act(self, out, in_, func, bias=0.0, scale=1.0, waits=()):
        def fn(e, out=out, in_=in_, func=func, bias=bias, scale=scale):
            return e.activation(out, in_, func, bias=bias, scale=scale)
        return self.p.op("scalar", fn, waits=waits, inc="ac")

    def copy(self, out, in_, waits=()):
        return self.p.op("vector", lambda e, o=out, i=in_: e.tensor_copy(o, i),
                         waits=waits, inc="ve")

    def tt(self, out, in0, in1, op, waits=()):
        return self.p.op("vector",
                         lambda e, o=out, a=in0, b=in1, op=op: e.tensor_tensor(o, a, b, op),
                         waits=waits, inc="ve")

    def ts(self, out, in0, s1, s2=None, op0=ALU.mult, op1=ALU.add, waits=()):
        if s2 is None:
            op1 = ALU.bypass
        def fn(e, o=out, a=in0, s1=s1, s2=s2, op0=op0, op1=op1):
            return e.tensor_scalar(o, a, s1, s2, op0, op1)
        return self.p.op("vector", fn, waits=waits, inc="ve")

    def stt(self, out, in0, scalar, in1, op0, op1, waits=()):
        def fn(e, o=out, a=in0, s=scalar, b=in1, op0=op0, op1=op1):
            return e.scalar_tensor_tensor(o, a, s, b, op0, op1)
        return self.p.op("vector", fn, waits=waits, inc="ve")

    def red(self, out, in_, op, negate=False, absv=False, waits=()):
        def fn(e, o=out, i=in_, op=op, neg=negate, ab=absv):
            return e.tensor_reduce(o, i, AX.X, op, apply_absolute_value=ab, negate=neg)
        return self.p.op("vector", fn, waits=waits, inc="ve")

    def recip(self, out, in_, waits=()):
        return self.p.op("vector", lambda e, o=out, i=in_: e.reciprocal(o, i),
                         waits=waits, inc="ve")

    def max8(self, out, in_, waits=()):
        return self.p.op("vector", lambda e, o=out, i=in_: e.max(out=o, in_=i),
                         waits=waits, inc="ve")

    def memset(self, ap, val, waits=()):
        return self.p.op("vector", lambda e, a=ap, v=val: e.memset(a, v),
                         waits=waits, inc="ve")

    def dma(self, dst, src, waits=(), eng="sync"):
        if len(src.shape) == 3 and len(dst.shape) == 2:
            dst = dst.rearrange("p (c w) -> p c w", c=src.shape[1])
        elif len(dst.shape) == 3 and len(src.shape) == 2:
            src = src.rearrange("p (c w) -> p c w", c=dst.shape[1])
        snm = "ld" if eng == "sync" else "gd"
        return self.p.op(eng, lambda e, d=dst, s=src: e.dma_start(out=d, in_=s),
                         waits=waits, inc=snm, inc_amt=16)

    def dma_fn(self, fn, waits=(), eng="gpsimd"):
        snm = "ld" if eng == "sync" else "gd"
        return self.p.op(eng, fn, waits=waits, inc=snm, inc_amt=16)

    def coll(self, kind, op, ins, outs, waits=()):
        def fn(e, kind=kind, op=op, ins=ins, outs=outs):
            return e.collective_compute(kind, op, replica_groups=[list(range(NCORES))],
                                        ins=ins, outs=outs)
        return self.p.op("gpsimd", fn, waits=waits, inc="cc", inc_amt=1)

    def barrier(self):
        self.p.barrier(("mm", "ve", "ac", "ld", "gd", "cc"))

    def now(self, snm):
        return self.p.cnt[snm]

    def bankf(self, i, lo=0, width=512, p0=0, p1=128):
        return self.sb["PS"][p0:p1, i * 512 + lo: i * 512 + lo + width]

    def alloc(self, name, shape, dtype, psum=False):
        fn = self.nc.psum_tensor if psum else self.nc.sbuf_tensor
        self._nalloc = getattr(self, "_nalloc", 0) + 1
        return self.scopes[-1].enter_context(fn(f"sb{self._nalloc}_{name}", shape, dtype))

    def push_scope(self):
        es = ExitStack()
        self.scopes.append(es)
        return es

    def pop_scope(self):
        self.scopes.pop().close()


def _bf(x):
    return np.ascontiguousarray(np.asarray(x), dtype=ml_dtypes.bfloat16)


def _f32(x):
    return np.ascontiguousarray(np.asarray(x), dtype=np.float32)


def feat8(v):
    return _f32(np.asarray(v, np.float32).reshape(8, 128).T)


def host_constants():
    c = {}
    c["identf"] = _f32(np.eye(128, dtype=np.float32))
    c["identb"] = _bf(np.eye(128, dtype=np.float32))
    Bm = np.eye(64, dtype=np.float32) - 1.0 / 64.0
    Z = np.zeros((64, 64), np.float32)
    c["bpair"] = _bf(np.block([[Bm, Z], [Z, Bm]]))
    J = np.ones((64, 64), np.float32)
    c["ones64"] = _bf(np.block([[J, Z], [Z, J]]))
    c["onesrow"] = _bf(np.ones((1, 128), np.float32))
    c["onesrowf"] = _f32(np.ones((1, 128), np.float32))
    c["onescol"] = _bf(np.ones((128, 1), np.float32))
    return c


def core_constants(core):
    p = core % 2
    c = {}
    c["rkall"] = np.array([[core, (core // 2) * 2, (core // 2) * 2 + 1, core ^ 1]],
                          dtype=np.int32)
    row = np.arange(128)[:, None]
    col = np.arange(256)[None, :]
    masks = []
    for i in range(4):
        valid = (i * 128 + row) <= (2 * (col // 128) * 128 + p * 128 + col % 128)
        masks.append(valid.astype(np.float32))
    c["cmask"] = _bf(np.stack(masks))
    c["flagS"] = _f32(np.full((8, 2), 1.0 if core == 0 else 0.0, np.float32))
    return c


def host_weights(inputs):
    w = {}
    w["Wq"] = _f32(inputs["L0_attn_Wq"]); w["Wk"] = _f32(inputs["L0_attn_Wk"])
    w["Wv"] = _f32(inputs["L0_attn_Wv"]); w["Wo"] = _f32(inputs["L0_attn_Wo"])
    w["bq"] = feat8(inputs["L0_attn_bq"]); w["bk"] = feat8(inputs["L0_attn_bk"])
    w["bv"] = feat8(inputs["L0_attn_bv"]); w["bo"] = feat8(inputs["L0_attn_bo"])
    for nm in ("q", "k", "v"):
        sc = np.asarray(inputs[f"L0_attn_{nm}_scale"], np.float32)
        bi = np.asarray(inputs[f"L0_attn_{nm}_bias"], np.float32)
        w[f"{nm}s128"] = _f32(np.tile(sc, 2)[:, None])
        w[f"{nm}b128"] = _f32(np.tile(bi, 2)[:, None])
        w[f"{nm}srow"] = _f32(sc[None, :])
        w[f"{nm}brow"] = _f32(bi[None, :])
    w["Wg_t"] = np.ascontiguousarray(np.asarray(inputs["L0_ffn_w_gate"], np.int32).T)
    w["Wu_t"] = np.ascontiguousarray(np.asarray(inputs["L0_ffn_w_up"], np.int32).T)
    w["Wd_t"] = np.ascontiguousarray(np.asarray(inputs["L0_ffn_w_down"], np.int32).T)
    w["Wqkv"] = _f32(inputs["L1_attn_W_qkv"])
    w["Wo1"] = _f32(inputs["L1_attn_W_o"])
    wq_lsr = np.asarray(inputs["L1_attn_W_q_lsr"], np.float32)
    wk_lsr = np.asarray(inputs["L1_attn_W_k_lsr"], np.float32)
    w["Wqlsr"] = _f32(np.transpose(wq_lsr, (1, 0, 2)).reshape(64, 256))
    w["Wklsr"] = _f32(np.transpose(wk_lsr, (1, 0, 2)).reshape(64, 256))
    w["coreT"] = _f32(np.asarray(inputs["L1_attn_core"], np.float32).T)
    w["RW"] = _f32(inputs["L1_ffn_router_w"])
    w["Rb"] = _f32(np.asarray(inputs["L1_ffn_router_b"], np.float32)[:, None])
    w["w1"] = _f32(inputs["L1_ffn_w1"])
    w["w2"] = _f32(inputs["L1_ffn_w2"])
    b1 = np.asarray(inputs["L1_ffn_b1"], np.float32)
    w["b1t"] = _f32(np.stack([b1[e].reshape(8, 128).T for e in range(NE)]))
    w["b2e"] = _f32(inputs["L1_ffn_b2"])
    return w


def own_rows(core):
    p = core % 2
    return np.concatenate([np.arange(128) + (2 * j + p) * 128 for j in range(NT)])


def declare_dram(ctx):
    nc, mode = ctx.nc, ctx.mode
    t = ctx.dr

    def inp(name, shape, dt=F32):
        t[name] = nc.dram_tensor(name, shape, dt, kind="ExternalInput")

    inp("identf", [128, 128]); inp("identb", [128, 128], BF16)
    inp("bpair", [128, 128], BF16); inp("ones64", [128, 128], BF16)
    inp("onesrow", [1, 128], BF16); inp("onesrowf", [1, 128])
    inp("onescol", [128, 1], BF16)
    inp("rkall", [1, 4], I32)
    inp("cmask", [4, 128, 256], BF16); inp("flagS", [8, 2])

    if mode in ("A", "B", "full"):
        inp("x0", [D, T])
    if mode in ("B", "full"):
        inp("Wq", [D, D]); inp("Wk", [D, D]); inp("Wv", [D, D]); inp("Wo", [D, D])
        inp("bq", [128, 8]); inp("bk", [128, 8]); inp("bv", [128, 8]); inp("bo", [128, 8])
        for nm in ("q", "k", "v"):
            inp(f"{nm}s128", [128, 1]); inp(f"{nm}b128", [128, 1])
            inp(f"{nm}srow", [1, 64]); inp(f"{nm}brow", [1, 64])
        inp("Wg_t", [D, HID], I32); inp("Wu_t", [D, HID], I32); inp("Wd_t", [HID, D], I32)
    if mode in ("B", "C", "full"):
        inp("RW", [D, 8]); inp("Rb", [8, 1])
    if mode in ("C", "full"):
        inp("Wqkv", [D, 3 * D]); inp("Wo1", [D, D])
        inp("Wqlsr", [64, 256]); inp("Wklsr", [64, 256]); inp("coreT", [16, 16])
        inp("w1", [NE, D, MH]); inp("w2", [NE, MH, D])
        inp("b1t", [NE, 128, 8]); inp("b2e", [NE, D])
    if mode == "B":
        inp("xn_own_in", [128, DC * T], BF16)
        inp("kf0_in", [128, DC * 2 * T], BF16)
    if mode == "C":
        inp("x1", [D, T])
        if os.environ.get("KDBG"):
            t["dbg_qlr"] = nc.dram_tensor("dbg_qlr", [128, 8 * T], BF16, kind="ExternalOutput")
            t["dbg_klr"] = nc.dram_tensor("dbg_klr", [128, 2 * 4096], BF16, kind="ExternalOutput")
            t["dbg_vt4"] = nc.dram_tensor("dbg_vt4", [128, 32 * 4 * 65], BF16, kind="ExternalOutput")
            t["dbg_nb1"] = nc.dram_tensor("dbg_nb1", [128, 18], F32, kind="ExternalOutput")
            t["dbg_oa1"] = nc.dram_tensor("dbg_oa1", [128, 2 * T], BF16, kind="ExternalOutput")
        inp("xn2own", [128, DC * T], BF16)
        inp("kf1_in", [128, DC * 2 * T], BF16)
        inp("Sall", [8, 2 * NCORES])

    if mode == "A":
        t["xn_out"] = nc.dram_tensor("xn_out", [128, DC * T], BF16, kind="ExternalOutput")
    if mode == "B":
        t["x1_out"] = nc.dram_tensor("x1_out", [D, T], F32, kind="ExternalOutput")
        t["xn2_out"] = nc.dram_tensor("xn2_out", [128, DC * T], BF16, kind="ExternalOutput")
        t["Sp_out"] = nc.dram_tensor("Sp_out", [8, 2], F32, kind="ExternalOutput")
        if os.environ.get("KDBG"):
            t["dbg_qh"] = nc.dram_tensor("dbg_qh", [128, T], BF16, kind="ExternalOutput")
            t["dbg_kr"] = nc.dram_tensor("dbg_kr", [128, 2 * T], BF16, kind="ExternalOutput")
            t["dbg_vt"] = nc.dram_tensor("dbg_vt", [128, 32 * 2 * 65], BF16, kind="ExternalOutput")
            t["dbg_nb"] = nc.dram_tensor("dbg_nb", [128, 1], F32, kind="ExternalOutput")
            t["dbg_oa"] = nc.dram_tensor("dbg_oa", [128, 2 * T], BF16, kind="ExternalOutput")
            t["dbg_wc"] = nc.dram_tensor("dbg_wc", [128, 2 * 1024], BF16, kind="ExternalOutput")
    if mode in ("C", "full"):
        t["y"] = nc.dram_tensor("y", [D, T], F32, kind="ExternalOutput")

    if mode == "full":
        if os.environ.get("KDBG"):
            t["dbg_sm"] = nc.dram_tensor("dbg_sm", [128, 64], F32, kind="ExternalOutput")
            t["dbg_mt"] = nc.dram_tensor("dbg_mt", [128, 144], F32, kind="ExternalOutput")
        t["cc_xn_in"] = nc.dram_tensor("cc_xn_in", [XN_ELEMS], BF16)
        t["cc_xn0_out"] = nc.dram_tensor("cc_xn0_out", [NCORES * XN_ELEMS], BF16,
                                         addr_space="Shared")
        t["cc_xn1_out"] = nc.dram_tensor("cc_xn1_out", [NCORES * XN_ELEMS], BF16,
                                         addr_space="Shared")
        t["cc_S_in"] = nc.dram_tensor("cc_S_in", [16], F32)
        t["cc_S_out"] = nc.dram_tensor("cc_S_out", [16], F32, addr_space="Shared")
        t["kf1_dram"] = nc.dram_tensor("kf1_dram", [128, DC * 2 * T], BF16)


def declare_sbuf_persistent(ctx):
    s = ctx.sb
    A = ctx.alloc
    s["x"] = A("x", [128, DC * T], F32)
    s["consts"] = A("consts", [128, 384], BF16)
    s["identf"] = A("identf", [128, 128], F32)
    s["onesrow"] = A("onesrow", [1, 128], BF16)
    s["onesrowf"] = A("onesrowf", [1, 128], F32)
    s["onescol"] = A("onescol", [128, 1], BF16)
    s["bias8"] = A("bias8", [128, 40], F32)
    s["qk128"] = A("qk128", [128, 12], F32)
    s["rows64"] = A("rows64", [1, 384], F32)
    s["rowsA"] = A("rowsA", [1, 512], F32)
    s["smalls"] = A("smalls", [128, 64], F32)
    s["negB"] = A("negB", [128, 18], F32)
    s["ints"] = A("ints", [1, 4], I32)
    s["flagS"] = A("flagS", [8, 2], F32)
    s["cmask"] = A("cmask", [128, 4 * 256], BF16)
    s["coreT"] = A("coreT", [16, 16], F32)
    s["cscaled"] = A("cscaled", [16, 16], F32)
    s["lsr"] = A("lsr", [128, 512], BF16)
    s["rbc"] = A("rbc", [128, 1024], F32)
    s["rstdb"] = A("rstdb", [128, 1024], BF16)
    s["bbc"] = A("bbc", [128, 512], BF16)
    s["masktok"] = A("masktok", [128, 16 + 128], F32)
    s["epsc"] = A("epsc", [128, 2], F32)
    s["biasc"] = A("biasc", [128, 24], F32)
    s["sqsl"] = A("sqsl", [128, 1024], BF16)
    s["PS"] = A("PS", [128, 3584], F32, psum=True)
    s["PSB"] = A("PSB", [128, 1024], BF16, psum=True)


def ph_load(ctx):
    s, d, mode = ctx.sb, ctx.dr, ctx.mode
    ctx.dma(s["identf"][:, :], d["identf"].ap())
    ctx.dma(s["consts"][:, 0:128], d["identb"].ap(), eng="gpsimd")
    ctx.dma(s["consts"][:, 128:256], d["bpair"].ap(), eng="gpsimd")
    ctx.dma(s["consts"][:, 256:384], d["ones64"].ap(), eng="gpsimd")
    ctx.dma(s["onesrow"][:, :], d["onesrow"].ap(), eng="gpsimd")
    ctx.dma(s["onesrowf"][:, :], d["onesrowf"].ap())
    ctx.dma(s["onescol"][:, :], d["onescol"].ap(), eng="gpsimd")
    ctx.dma(s["ints"][:, :], d["rkall"].ap())
    ctx.dma(s["cmask"].ap().rearrange("p (m w) -> p m w", m=4),
            d["cmask"].ap().rearrange("m p w -> p m w"))
    ctx.dma(s["flagS"][:, :], d["flagS"].ap())
    ctx.memset(s["epsc"][:, 0:1], EPS_LN)
    ctx.memset(s["epsc"][:, 1:2], EPS_QK)
    if mode in ("A", "B", "full"):
        ctx.dma(s["x"][:, :], d["x0"].ap().rearrange("(c p) t -> p c t", p=128))
    if mode == "C":
        ctx.dma(s["x"][:, :], d["x1"].ap().rearrange("(c p) t -> p c t", p=128))
        ctx.dma(s["bias8"][0:8, 0:16], d["Sall"].ap())
    if mode in ("B", "full"):
        for i, nm in enumerate(("bq", "bk", "bv", "bo")):
            ctx.dma(s["bias8"][:, 8 * i:8 * i + 8], d[nm].ap())
        for i, nm in enumerate(("qs128", "qb128", "ks128", "kb128", "vs128", "vb128")):
            ctx.dma(s["qk128"][:, i:i + 1], d[nm].ap())
        for i, nm in enumerate(("qsrow", "qbrow", "ksrow", "kbrow", "vsrow", "vbrow")):
            ctx.dma(s["rows64"][:, 64 * i:64 * i + 64], d[nm].ap())
    if mode in ("B", "C", "full"):
        ctx.dma(s["bias8"][0:8, 32:33], d["Rb"].ap())
    if mode in ("C", "full"):
        ctx.dma(s["coreT"][:, :], d["coreT"].ap())
        ctx.dma(s["lsr"][0:64, 0:256], d["Wqlsr"].ap(), eng="gpsimd")
        ctx.dma(s["lsr"][0:64, 256:512], d["Wklsr"].ap(), eng="gpsimd")
        ctx.dma(s["lsr"][64:128, 0:256], d["Wqlsr"].ap(), eng="gpsimd")
        ctx.dma(s["lsr"][64:128, 256:512], d["Wklsr"].ap(), eng="gpsimd")
    ctx.barrier()


def emit_ln(ctx):
    s = ctx.sb
    ctx.push_scope()
    rex = ctx.alloc("lnrows", [1, 1536], F32)
    xb = s["xn"]
    casts = []
    for dc in range(DC):
        casts.append(ctx.copy(xb[:, dc * T:(dc + 1) * T], s["x"][:, dc * T:(dc + 1) * T]))
    sq_mm = [0, 0]
    tt_last = 0
    for q in range(4):
        q0 = q * 512
        sums = ctx.bankf(0, p1=1)
        ssq = ctx.bankf(1, p1=1)
        mm_s = mm_q = 0
        for dc in range(DC):
            xc = xb[:, dc * T + q0: dc * T + q0 + 512]
            sq = s["sqsl"][:, (dc % 2) * 512:(dc % 2) * 512 + 512]
            aw = [("ve", casts[dc])]
            if sq_mm[dc % 2]:
                aw.append(("mm", sq_mm[dc % 2]))
            a1 = ctx.act(sq, xc, AF.Square, waits=aw)
            mm_s = ctx.mm(sums, s["onescol"][:, :], xc, start=(dc == 0), stop=(dc == DC - 1),
                          waits=(("ve", casts[dc]), ("ve", tt_last)))
            mm_q = ctx.mm(ssq, s["onescol"][:, :], sq, start=(dc == 0), stop=(dc == DC - 1),
                          waits=(("ac", a1),))
            sq_mm[dc % 2] = mm_q
        rA = s["rowsA"][0:1, 0:512]
        rB = rex[0:1, 0:512]
        rC = rex[0:1, 512:1024]
        rD = rex[0:1, 1024:1536]
        v1 = ctx.ts(rA, sums, 1.0 / D, None, ALU.mult, waits=(("mm", mm_s),))
        a1 = ctx.act(rB, rA, AF.Square, waits=(("ve", v1),))
        v2 = ctx.stt(rB, ssq, 1.0 / D, rB, ALU.mult, ALU.subtract,
                     waits=(("mm", mm_q), ("ac", a1)))
        a2 = ctx.act(rB, rB, AF.Sqrt, bias=s["epsc"][0:1, 0:1], waits=(("ve", v2),))
        v3 = ctx.recip(rC, rB, waits=(("ac", a2),))
        v4 = ctx.tt(rD, rA, rC, ALU.mult)
        bcA = ctx.bankf(2)
        bcB = ctx.bankf(3)
        m1 = ctx.mm(bcA, s["onesrowf"][:, :], rC, waits=(("ve", v3), ("ac", ctx.now("ac"))))
        m2 = ctx.mm(bcB, s["onesrowf"][:, :], rD, waits=(("ve", v4),))
        e1 = ctx.act(s["rstdb"][:, 0:512], bcA, AF.Copy, waits=(("mm", m1), ("ve", tt_last)))
        e2 = ctx.act(s["bbc"][:, :], bcB, AF.Copy, waits=(("mm", m2),))
        for dc in range(DC):
            xc = xb[:, dc * T + q0: dc * T + q0 + 512]
            ctx.tt(xc, xc, s["rstdb"][:, 0:512], ALU.mult, waits=(("ac", e2), ("mm", mm_q)))
            tt_last = ctx.tt(xc, xc, s["bbc"][:, :], ALU.subtract)
    ctx.barrier()
    ctx.pop_scope()


def emit_gather_issue(ctx, layer):
    s, d = ctx.sb, ctx.dr
    if ctx.mode != "full":
        return
    ccin = d["cc_xn_in"].ap().rearrange("(p n) -> p n", p=128)
    ccout = d["cc_xn0_out"] if layer == 0 else d["cc_xn1_out"]
    g1 = ctx.dma(ccin, s["xn"][:, :], eng="gpsimd")
    ctx.cc_xn = ctx.coll("AllGather", ALU.bypass, [d["cc_xn_in"].ap()], [ccout.ap()],
                         waits=(("gd", g1),))


def ph_assemble0(ctx):
    s, d = ctx.sb, ctx.dr
    if ctx.mode == "B":
        ctx.dma(s["kfull"][:, :], d["kf0_in"].ap())
        ctx.barrier()
        return
    agv = d["cc_xn0_out"].ap().rearrange("(r p c w) -> r p c w", r=NCORES, p=128, c=DC)
    kv = s["kfull"].ap().rearrange("p (c g w) -> p c g w", c=DC, g=2)
    cw = ctx.cc_xn

    def fetch(slot, off):
        def fn(e, slot=slot, off=off):
            reg = e.alloc_register()
            e.reg_load(reg, ctx.sb["ints"][0:1, slot:slot + 1])
            rv = e.snap(reg, min_val=0, max_val=7)
            return e.dma_start(out=kv[:, :, off, :], in_=agv[rv, :, :, :])
        return fn

    ctx.p.op("gpsimd", fetch(0, 0), waits=(("cc", cw),), inc="gd", inc_amt=16)
    ctx.p.op("gpsimd", fetch(3, 1), inc="gd", inc_amt=16)
    ctx.barrier()


def ph_assemble1(ctx):
    s, d = ctx.sb, ctx.dr
    if ctx.mode != "full":
        return
    agv = d["cc_xn1_out"].ap().rearrange("(r p c j w) -> r p c j w",
                                         r=NCORES, p=128, c=DC, j=NT)
    kv = d["kf1_dram"].ap().rearrange("p (c g w) -> p c g w", c=DC, g=2 * NT)
    cw = ctx.cc_xn

    _rv = {}

    def fetch(slot, off, dc):
        def fn(e, slot=slot, off=off, dc=dc):
            if slot not in _rv:
                reg = e.alloc_register()
                e.reg_load(reg, ctx.sb["ints"][0:1, slot:slot + 1])
                _rv[slot] = e.snap(reg, min_val=0, max_val=7)
            return e.dma_start(out=kv[:, dc, off:2 * NT:2, :], in_=agv[_rv[slot], :, dc, :, :])
        return fn

    for dc in range(DC):
        ctx.p.op("gpsimd", fetch(1, 0, dc), waits=(("cc", cw),) if dc == 0 else (),
                 inc="gd", inc_amt=16)
        ctx.p.op("gpsimd", fetch(2, 1, dc), inc="gd", inc_amt=16)
    ctx.barrier()


def emit_ffn0(ctx):
    s, d = ctx.sb, ctx.dr
    ctx.push_scope()
    H = ctx.alloc("ffnH", [128, 24 * 1024], BF16)
    wA = ctx.alloc("ffnWA", [128, 8 * 1024], BF16)
    wB = ctx.alloc("ffnWB", [128, 8 * 1024], BF16)
    wC = ctx.alloc("ffnWC", [128, 6 * 1024], BF16)
    sg = ctx.sb["sqsl"]
    wfree = [0, 0]
    dfree = [0, 0]
    h_done = 0
    for th in range(2):
        t0 = th * 1024
        for blk in range(6):
            src_g = d["Wg_t"].ap()[:, blk * 512:(blk + 1) * 512].rearrange(
                "(c p) h -> p c h", p=128)
            src_u = d["Wu_t"].ap()[:, blk * 512:(blk + 1) * 512].rearrange(
                "(c p) h -> p c h", p=128)
            half = (blk % 2) * 4096
            w = (("mm", wfree[blk % 2]),) if wfree[blk % 2] else ()
            gdg = ctx.dma(wA[:, half:half + 4096], src_g, waits=w, eng="gpsimd")
            gdu = ctx.dma(wB[:, half:half + 4096], src_u, waits=w, eng="gpsimd")
            for hq in range(4):
                hc = blk * 4 + hq
                mmg = mmu = 0
                for sub in range(2):
                    gb = ctx.bankf(sub)
                    for dc in range(DC):
                        lhs_g = wA[:, half + dc * 512 + hq * 128:
                                   half + dc * 512 + hq * 128 + 128]
                        rhs = s["xn"][:, dc * T + t0 + sub * 512:
                                      dc * T + t0 + sub * 512 + 512]
                        wt = []
                        if dc == 0 and sub == 0 and hq == 0:
                            wt.append(("gd", gdg))
                        if dc == 0:
                            wt.append(("ac", ctx.now("ac")))
                        mmg = ctx.mm(gb, lhs_g, rhs, start=(dc == 0), stop=(dc == DC - 1),
                                     waits=wt)
                for sub in range(2):
                    ub = ctx.bankf(2 + sub)
                    for dc in range(DC):
                        lhs_u = wB[:, half + dc * 512 + hq * 128:
                                   half + dc * 512 + hq * 128 + 128]
                        rhs = s["xn"][:, dc * T + t0 + sub * 512:
                                      dc * T + t0 + sub * 512 + 512]
                        wt = []
                        if dc == 0 and sub == 0 and hq == 0:
                            wt.append(("gd", gdu))
                        if dc == 0:
                            wt.append(("ve", ctx.now("ve")))
                        mmu = ctx.mm(ub, lhs_u, rhs, start=(dc == 0), stop=(dc == DC - 1),
                                     waits=wt)
                a1 = ctx.act(sg[:, :], ctx.sb["PS"][:, 0:1024], AF.Silu, scale=INV127,
                             waits=(("mm", mmg),))
                h_done = ctx.stt(H[:, hc * 1024:(hc + 1) * 1024],
                                 ctx.sb["PS"][:, 1024:2048], INV127, sg[:, :],
                                 ALU.mult, ALU.mult, waits=(("mm", mmu), ("ac", a1)))
            wfree[blk % 2] = ctx.now("mm")
        dbank_rd = getattr(ctx, "_ffn_dbank", [0, 0])
        for dco in range(DC):
            src = d["Wd_t"].ap()[:, dco * 128:(dco + 1) * 128].rearrange(
                "(c p) o -> p c o", p=128)
            half = (dco % 2) * 3072
            w = (("mm", dfree[dco % 2]),) if dfree[dco % 2] else ()
            gdd = ctx.dma(wC[:, half:half + 3072], src, waits=w, eng="gpsimd")
            for sub in range(2):
                zb = ctx.bankf(4 + sub)
                mmd = 0
                for hc in range(24):
                    lhs = wC[:, half + hc * 128: half + hc * 128 + 128]
                    rhs = H[:, hc * 1024 + sub * 512: hc * 1024 + sub * 512 + 512]
                    wt = []
                    if hc == 0:
                        wt = [("gd", gdd), ("ve", h_done)]
                        if dbank_rd[sub]:
                            wt.append(("ve", dbank_rd[sub]))
                    mmd = ctx.mm(zb, lhs, rhs, start=(hc == 0), stop=(hc == 23), waits=wt)
                xc = s["x"][:, dco * T + t0 + sub * 512: dco * T + t0 + sub * 512 + 512]
                dbank_rd[sub] = ctx.stt(xc, zb, INV127, xc, ALU.mult, ALU.add,
                                        waits=(("mm", mmd),))
            dfree[dco % 2] = ctx.now("mm")
        ctx._ffn_dbank = dbank_rd
    ctx.barrier()
    ctx.pop_scope()


def emit_negB_l0(ctx):
    s = ctx.sb
    r = s["rowsA"]
    outs = []
    for i, (so, bo) in enumerate(((0, 64), (128, 192))):
        srow = s["rows64"][0:1, so:so + 64]
        brow = s["rows64"][0:1, bo:bo + 64]
        ms = r[0:1, 0:1]
        nb = r[0:1, 1:2]
        v1 = ctx.red(ms, srow, ALU.max, absv=True)
        ctx.tt(r[0:1, 4:68], brow, brow, ALU.mult)
        v3 = ctx.red(nb, r[0:1, 4:68], ALU.add)
        a1 = ctx.act(nb, nb, AF.Sqrt, waits=(("ve", v3),))
        out = r[0:1, 2 + i:3 + i]
        outs.append(ctx.stt(out, ms, 8.0, nb, ALU.mult, ALU.add,
                            waits=(("ac", a1), ("ve", v1))))
    ctx.tt(r[0:1, 0:1], r[0:1, 2:3], r[0:1, 3:4], ALU.mult, waits=(("ve", outs[1]),))
    vB = ctx.ts(r[0:1, 0:1], r[0:1, 0:1], -0.125, None, ALU.mult)
    m1 = ctx.mm(ctx.bankf(6, width=1), s["onesrowf"][:, :], r[0:1, 0:1], waits=(("ve", vB),))
    ctx.copy(s["negB"][:, 0:1], ctx.bankf(6, width=1), waits=(("mm", m1),))
    ctx.ts(s["smalls"][:, 0:1], s["qk128"][:, 0:1], 0.125, None, ALU.mult)
    ctx.ts(s["smalls"][:, 1:2], s["qk128"][:, 1:2], 0.125, None, ALU.mult)
    vb0 = ctx.copy(s["sqsl"][:, 0:24], s["bias8"][:, 0:24])
    mb0 = ctx.mm(ctx.bankf(5, width=24), s["consts"][:, 256:384], s["sqsl"][:, 0:24],
                 waits=(("ve", vb0),))
    ctx.stt(s["biasc"][:, 0:24], ctx.bankf(5, width=24), -1.0 / 64.0, s["bias8"][:, 0:24],
            ALU.mult, ALU.add, waits=(("mm", mb0),))
    ctx.barrier()


def _flush_outproj(ctx, wo_dma, wC, oall, npairs=2):
    s = ctx.sb
    oready = ctx.now("ve")
    bank_rd = [0, 0]
    for dco in range(DC):
        for sub in range(4):
            zb = ctx.bankf(sub % 2)
            mmz = 0
            for i in range(npairs):
                lhs = wC[:, i * 1024 + dco * 128: i * 1024 + dco * 128 + 128]
                rhs = oall[:, i * T + sub * 512: i * T + sub * 512 + 512]
                wt = []
                if i == 0:
                    if bank_rd[sub % 2]:
                        wt.append(("ve", bank_rd[sub % 2]))
                    if dco == 0 and sub == 0:
                        wt += [("ve", oready), ("gd", wo_dma), ("ac", ctx.now("ac"))]
                mmz = ctx.mm(zb, lhs, rhs, start=(i == 0), stop=(i == npairs - 1),
                             waits=wt)
            xc = s["x"][:, dco * T + sub * 512: dco * T + sub * 512 + 512]
            bank_rd[sub % 2] = ctx.stt(xc, zb, 1.0, xc, ALU.mult, ALU.add,
                                       waits=(("mm", mmz),))
    ctx.barrier()


def emit_l0_attn(ctx):
    s, d = ctx.sb, ctx.dr
    ctx.push_scope()
    kfull = ctx.alloc("kfull", [128, DC * 2 * T], BF16)
    s["kfull"] = kfull
    ph_assemble0(ctx)
    kraw = ctx.alloc("kraw", [128, 2 * T], BF16)
    vhat = ctx.alloc("vhat", [128, 2 * T], BF16)
    vtm = ctx.alloc("vtm", [128, 32 * 2 * 65], BF16)
    qhat = ctx.alloc("qhat", [128, T], BF16)
    oall = ctx.alloc("oall", [128, 2 * T], BF16)
    wA = ctx.alloc("attnWA", [128, 3 * 1024], BF16)
    wC = ctx.alloc("attnWC", [128, 2 * 1024], BF16)
    psb = ctx.alloc("psb", [128, 2 * 1024], BF16)

    negb = s["negB"][:, 0:1]
    bconst = s["consts"][:, 128:256]
    oconst = s["consts"][:, 256:384]
    evac_tok = [0, 0]
    sqw = [0, 0]
    exp_done = [0, 0]
    pv_done = [0, 0]
    o_norm = 0
    wo_dma = 0
    wo_free = 0

    def pipe(base, raw, dst, n_tok, bcol, scol, b2col, srcf, gwait):
        nonlocal evac_tok, sqw
        last_ve = 0
        prev_vc = 0
        prev_a3 = 0
        prev_ts = [0, 0]
        for ss in range(n_tok // 512):
            pb = ctx.bankf(ss % 2)
            mmp = 0
            for dc in range(DC):
                lhs = wA[:, base + dc * 128: base + dc * 128 + 128]
                wt = []
                if dc == 0 and ss == 0:
                    wt.append(("gd", gwait))
                    wt.append(("ac", ctx.now("ac")))
                    wt.append(("ve", ctx.now("ve")))
                if dc == 0 and evac_tok[ss % 2]:
                    wt.append(("ac", evac_tok[ss % 2]))
                mmp = ctx.mm(pb, lhs, srcf(dc, ss), start=(dc == 0), stop=(dc == DC - 1),
                             waits=wt)
            a1 = ctx.act(raw[:, ss * 512:(ss + 1) * 512], pb, AF.Copy,
                         waits=(("mm", mmp), ("ve", last_ve)))
            evac_tok[ss % 2] = a1
            cb = ctx.bankf(2)
            mc = ctx.mm(cb, bconst, raw[:, ss * 512:(ss + 1) * 512],
                        waits=[("ac", a1)] + ([("ve", prev_vc)] if prev_vc else []))
            vc = ctx.ts(raw[:, ss * 512:(ss + 1) * 512], cb, bcol, None, ALU.add,
                        waits=(("mm", mc),))
            prev_vc = vc
            sq = s["sqsl"][:, (ss % 2) * 512:(ss % 2) * 512 + 512]
            a2 = ctx.act(sq, raw[:, ss * 512:(ss + 1) * 512], AF.Square,
                         waits=[("ve", vc)] + ([("mm", sqw[ss % 2])] if sqw[ss % 2] else []))
            vb = ctx.bankf(3)
            mv = ctx.mm(vb, oconst, sq,
                        waits=[("ac", a2)] + ([("ac", prev_a3)] if prev_a3 else []))
            sqw[ss % 2] = mv
            rbs = s["rbc"][:, (ss % 2) * 512:(ss % 2) * 512 + 512]
            rss = s["rstdb"][:, (ss % 2) * 512:(ss % 2) * 512 + 512]
            a3 = ctx.act(rbs, vb, AF.Ln, scale=1.0 / 64.0,
                         bias=s["epsc"][:, 1:2],
                         waits=(("mm", mv), ("ve", prev_ts[ss % 2])))
            prev_a3 = ctx.act(rss, rbs, AF.Exp, scale=-0.5, waits=(("ac", a3),))
            ctx.tt(raw[:, ss * 512:(ss + 1) * 512], raw[:, ss * 512:(ss + 1) * 512],
                   rss, ALU.mult, waits=(("ac", prev_a3),))
            last_ve = ctx.ts(dst[:, ss * 512:(ss + 1) * 512],
                             raw[:, ss * 512:(ss + 1) * 512],
                             scol, b2col, ALU.mult, ALU.add)
            prev_ts[ss % 2] = last_ve
        return last_ve

    _nhp = int(os.environ.get("KHP", str(NH // 2)))
    for hp in range(_nhp):
        wq = d["Wq"].ap()[:, hp * 128:(hp + 1) * 128].rearrange("(c p) o -> p c o", p=128)
        wk = d["Wk"].ap()[:, hp * 128:(hp + 1) * 128].rearrange("(c p) o -> p c o", p=128)
        wv = d["Wv"].ap()[:, hp * 128:(hp + 1) * 128].rearrange("(c p) o -> p c o", p=128)
        wo = d["Wo"].ap()[hp * 128:(hp + 1) * 128, :]
        wfree = ctx.now("mm")
        g1 = ctx.dma(wA[:, 0:1024], wq, waits=(("mm", wfree),), eng="gpsimd")
        g2 = ctx.dma(wA[:, 1024:2048], wk, eng="gpsimd")
        g3 = ctx.dma(wA[:, 2048:3072], wv, eng="gpsimd")
        wo_w = (("mm", wo_free),) if (hp % 2 == 0 and hp > 0) else ()
        g4 = ctx.dma(wC[:, (hp % 2) * 1024:(hp % 2) * 1024 + 1024], wo,
                     waits=wo_w, eng="gpsimd")
        if hp % 2 == 1:
            wo_dma = g4

        def ksrc(dc, ss):
            return kfull[:, dc * 2 * T + ss * 512: dc * 2 * T + ss * 512 + 512]

        kv_done = pipe(1024, kraw, kraw, 2 * T, s["biasc"][:, 8 + hp:9 + hp],
                       s["qk128"][:, 2:3], s["qk128"][:, 3:4], ksrc, g2)
        q_done = pipe(0, qhat, qhat, T, s["biasc"][:, 0 + hp:1 + hp],
                      s["smalls"][:, 0:1], s["smalls"][:, 1:2], ksrc, g1)
        v_done = pipe(2048, vhat, vhat, 2 * T, s["biasc"][:, 16 + hp:17 + hp],
                      s["qk128"][:, 4:5], s["qk128"][:, 5:6], ksrc, g3)
        ms1 = ctx.memset(vtm[:, :], 1.0, waits=(("mm", ctx.now("mm")),))
        pv8 = s["PSB"].ap().rearrange("p (j w) -> p j w", j=8)
        vv = vtm.ap().rearrange("p (c h w) -> p c h w", c=32, h=2)
        tr_ev = 0
        for kb in range(8):
            tr = 0
            for j in range(4):
                kc = kb * 4 + j
                wt = []
                if kb == 0 and j == 0:
                    wt.append(("ve", v_done))
                if kb >= 1 and j == 0:
                    wt.append(("ve", tr_ev))
                tr = ctx.trans(s["PSB"][:, j * 128:(j + 1) * 128],
                               vhat[:, kc * 128:(kc + 1) * 128],
                               s["consts"][:, 0:128], waits=wt)
            w0 = (("mm", tr), ("ve", ms1))
            ctx.copy(vv[:, kb * 4:kb * 4 + 4, 0, 0:64], pv8[:, 0:4, 0:64], waits=w0)
            tr_ev = ctx.copy(vv[:, kb * 4:kb * 4 + 4, 1, 0:64], pv8[:, 0:4, 64:128])
        vtm_done = tr_ev

        for qg in range(4):
            q0 = qg * 512
            ob = [ctx.bankf(4, p1=65), ctx.bankf(5, p1=65)]
            pend = None

            def emit_pv(kc, ea):
                nonlocal pv_done
                pb = kc % 2
                wv0 = [("ac", ea)]
                if kc == 0:
                    wv0.append(("ve", vtm_done))
                    if o_norm:
                        wv0.append(("ve", o_norm))
                ctx.mm(ob[0], vtm[:, (kc * 2) * 65:(kc * 2) * 65 + 65],
                       psb[:, pb * 1024:pb * 1024 + 512],
                       start=(kc == 0), stop=(kc == 31), waits=wv0)
                pv_done[pb] = ctx.mm(ob[1], vtm[:, (kc * 2 + 1) * 65:(kc * 2 + 1) * 65 + 65],
                                     psb[:, pb * 1024 + 512:pb * 1024 + 1024],
                                     start=(kc == 0), stop=(kc == 31))

            for kc in range(32):
                pb = kc % 2
                wt = []
                if kc == 0 and qg == 0:
                    wt += [("ve", q_done), ("ve", kv_done)]
                if exp_done[pb]:
                    wt.append(("ac", exp_done[pb]))
                ctx.mm(ctx.bankf(2 * pb), kraw[0:64, kc * 128:(kc + 1) * 128],
                       qhat[0:64, q0:q0 + 512], tp=(0, 0), waits=wt)
                m1 = ctx.mm(ctx.bankf(2 * pb + 1), kraw[64:128, kc * 128:(kc + 1) * 128],
                            qhat[64:128, q0:q0 + 512], tp=(64, 0))
                wexp = [("mm", m1)]
                if pv_done[pb]:
                    wexp.append(("mm", pv_done[pb]))
                ea = ctx.act(psb[:, pb * 1024:(pb + 1) * 1024],
                             ctx.sb["PS"][:, pb * 1024:(pb + 1) * 1024],
                             AF.Exp, bias=negb, waits=wexp)
                exp_done[pb] = ea
                if pend is not None:
                    emit_pv(*pend)
                pend = (kc, ea)
            emit_pv(*pend)
            for h in range(2):
                zrow = ctx.sb["PS"][64:65, (4 + h) * 512:(4 + h) * 512 + 512]
                rr = s["rowsA"][0:1, 0:512]
                ctx.ts(rr, zrow, 1e-38, None, ALU.add, waits=(("mm", pv_done[1]),))
                v2 = ctx.recip(rr, rr)
                bc = ctx.bankf(6, p1=64)
                m1 = ctx.mm(bc, s["onesrowf"][0:1, 0:64], rr, waits=(("ve", v2),))
                v3 = ctx.copy(s["rbc"][0:64, 0:512], bc, waits=(("mm", m1),))
                dst = oall[h * 64:h * 64 + 64, (hp % 2) * T + q0:(hp % 2) * T + q0 + 512]
                o_norm = ctx.tt(dst, ctx.sb["PS"][0:64, (4 + h) * 512:(4 + h) * 512 + 512],
                                s["rbc"][0:64, 0:512], ALU.mult,
                                waits=(("ve", v3), ("mm", getattr(ctx, "_flush_mm", 0))))
        if os.environ.get("KDBG") and ctx.mode == "B" and hp == _nhp - 1:
            ctx.barrier()
            ctx.dma(d["dbg_qh"].ap(), qhat[:, :])
            ctx.dma(d["dbg_kr"].ap(), kraw[:, :])
            ctx.dma(d["dbg_vt"].ap(), vtm[:, :])
            ctx.dma(d["dbg_nb"].ap(), s["negB"][:, 0:1])
            ctx.dma(d["dbg_oa"].ap(), oall[:, :])
            ctx.dma(d["dbg_wc"].ap(), wC[:, :])
            ctx.barrier()
        if hp % 2 == 1:
            _flush_outproj(ctx, wo_dma, wC, oall, npairs=2)
            exp_done = [0, 0]
            pv_done = [0, 0]
            o_norm = 0
            evac_tok = [0, 0]
            sqw = [0, 0]
        wo_free = ctx.now("mm")
    for dco in range(DC):
        xc = s["x"][:, dco * T:(dco + 1) * T]
        ctx.ts(xc, xc, 1.0, s["bias8"][:, 24 + dco:25 + dco], ALU.mult, ALU.add)
    ctx.barrier()
    ctx.pop_scope()


def emit_router_S(ctx):
    s, d = ctx.sb, ctx.dr
    ctx.push_scope()
    wR = ctx.alloc("wR", [128, 64], BF16)
    t12 = ctx.alloc("t12", [128, 16], BF16)
    logfm = ctx.alloc("logfm", [8, T], F32)
    g1 = ctx.dma(wR[:, :], d["RW"].ap().rearrange("(c p) e -> p c e", p=128),
                 eng="gpsimd")
    lg_done = 0
    for sub in range(4):
        lb = ctx.bankf(0, p1=8)
        mml = 0
        for dc in range(DC):
            lhs = wR[:, dc * 8:(dc + 1) * 8]
            rhs = s["xn"][:, dc * T + sub * 512: dc * T + sub * 512 + 512]
            wt = []
            if dc == 0 and sub == 0:
                wt.append(("gd", g1))
            if dc == 0 and sub >= 1:
                wt.append(("ve", lg_done))
            mml = ctx.mm(lb, lhs, rhs, start=(dc == 0), stop=(dc == DC - 1), waits=wt)
        lg_done = ctx.ts(logfm[0:8, sub * 512:(sub + 1) * 512], lb,
                         s["bias8"][0:8, 32:33], None, ALU.add, waits=(("mm", mml),))
    s0_done = s1_done = 0
    prev_ev = 0
    for ti in range(NT):
        tb = ctx.sb["PS"][0:128, 512:512 + 8]
        wt = [("ve", lg_done)] if ti == 0 else [("ve", prev_ev)]
        mt = ctx.trans(tb, logfm[0:8, ti * 128:(ti + 1) * 128],
                       s["identf"][0:8, 0:8], waits=wt)
        lt = s["masktok"][:, 0:8]
        ctx.copy(lt, tb, waits=(("mm", mt),))
        m8 = s["masktok"][:, 8:16]
        ctx.max8(m8, lt)
        oh1 = s["smalls"][:, 8:16]
        ctx.ts(oh1, lt, m8[:, 0:1], None, ALU.is_ge)
        l2 = s["smalls"][:, 16:24]
        ctx.stt(l2, oh1, -1e9, lt, ALU.mult, ALU.add)
        oh2 = s["smalls"][:, 24:32]
        ctx.ts(oh2, l2, m8[:, 1:2], None, ALU.is_ge)
        dd = s["smalls"][:, 32:33]
        v5 = ctx.tt(dd, m8[:, 1:2], m8[:, 0:1], ALU.subtract)
        e2 = s["smalls"][:, 33:34]
        a1 = ctx.act(e2, dd, AF.Exp, waits=(("ve", v5),))
        p1 = s["smalls"][:, 34:35]
        ctx.ts(p1, e2, 1.0, None, ALU.add, waits=(("ac", a1),))
        ctx.recip(p1, p1)
        p2 = s["smalls"][:, 35:36]
        ctx.ts(p2, p1, -1.0, 1.0, ALU.mult, ALU.add)
        t1 = s["smalls"][:, 40:48]
        t2 = s["smalls"][:, 48:56]
        ctx.ts(t1, oh1, p1[:, 0:1], None, ALU.mult)
        ctx.ts(t2, oh2, p2[:, 0:1], None, ALU.mult)
        mrow = s["masktok"][:, 16 + ti * 8: 16 + ti * 8 + 8]
        ctx.tt(mrow, t1, t2, ALU.add)
        wb = (("mm", s1_done),) if ti > 0 else ()
        v12 = ctx.copy(t12[:, 0:8], t1, waits=wb)
        prev_ev = ctx.copy(t12[:, 8:16], t2)
        s0_done = ctx.mm(ctx.sb["PS"][0:8, 2 * 512:2 * 512 + 1], t12[:, 0:8],
                         s["onescol"][:, :], start=(ti == 0), stop=(ti == NT - 1),
                         waits=(("ve", v12),))
        s1_done = ctx.mm(ctx.sb["PS"][0:8, 3 * 512:3 * 512 + 1], t12[:, 8:16],
                         s["onescol"][:, :], start=(ti == 0), stop=(ti == NT - 1),
                         waits=(("ve", prev_ev),))
    ctx.copy(s["smalls"][0:8, 60:61], ctx.sb["PS"][0:8, 2 * 512:2 * 512 + 1],
             waits=(("mm", s1_done),))
    vS1 = ctx.copy(s["smalls"][0:8, 61:62], ctx.sb["PS"][0:8, 3 * 512:3 * 512 + 1])
    if ctx.mode == "full":
        g2 = ctx.dma(d["cc_S_in"].ap().rearrange("(e k) -> e k", e=8),
                     s["smalls"][0:8, 60:62], waits=(("ve", vS1),), eng="gpsimd")
        ctx.cc_S = ctx.coll("AllReduce", ALU.add, [d["cc_S_in"].ap()],
                            [d["cc_S_out"].ap()], waits=(("gd", g2),))
    if ctx.mode == "B":
        ctx.dma(d["Sp_out"].ap(), s["smalls"][0:8, 60:62], waits=(("ve", vS1),))
    ctx.barrier()
    ctx.pop_scope()


def emit_mask_final(ctx):
    s, d = ctx.sb, ctx.dr
    if ctx.mode == "full":
        gl = ctx.dma(s["smalls"][0:8, 62:64],
                     d["cc_S_out"].ap().rearrange("(e k) -> e k", e=8),
                     waits=(("cc", ctx.cc_S),))
        wS = (("ld", gl),)
    else:
        ctx.red(s["smalls"][0:8, 62:63], s["bias8"][0:8, 0:8], ALU.add)
        v2 = ctx.red(s["smalls"][0:8, 63:64], s["bias8"][0:8, 8:16], ALU.add)
        wS = (("ve", v2),)
    vf = ctx.tt(s["smalls"][0:8, 62:64], s["smalls"][0:8, 62:64], s["flagS"][:, :],
                ALU.mult, waits=wS)
    va = ctx.tt(s["masktok"][0:8, 16:18], s["masktok"][0:8, 16:18],
                s["smalls"][0:8, 62:64], ALU.add, waits=(("ve", vf),))
    vc = ctx.ts(s["masktok"][:, 16:24], s["masktok"][:, 16:24], CAP, None, ALU.min,
                waits=(("ve", va),))
    tr = vc
    for ti in range(NT):
        tb = ctx.sb["PS"][0:8, 512:512 + 128]
        mt = ctx.trans(tb, s["masktok"][:, 16 + ti * 8:16 + ti * 8 + 8],
                       s["identf"][:, :], waits=(("ve", tr),))
        tr = ctx.copy(s["maskfm"][0:8, ti * 128:(ti + 1) * 128], tb, waits=(("mm", mt),))
    ctx.barrier()


def emit_moe(ctx):
    s, d = ctx.sb, ctx.dr
    ctx.push_scope()
    Hm = ctx.alloc("moeH", [128, 8 * 1024], BF16)
    wA = ctx.alloc("moeWA", [128, 8 * 1024], BF16)
    wB = ctx.alloc("moeWB", [128, 8 * 1024], BF16)
    mbcb = ctx.alloc("moeMBCB", [128, 1024], BF16)
    ttmp = ctx.sb["sqsl"]
    b1sb = ctx.alloc("b1sb", [128, 8], F32)
    b2sb = ctx.alloc("b2sb", [8, D], BF16)
    mr0 = ctx.alloc("mrow0", [1, 1024], BF16)
    mmlast = 0
    bcfree = 0
    bc_ve = 0
    b1_free = 0
    ctx._moe_wfree = [0, 0]
    ctx._moe_w2free = [0, 0]
    ctx._moe_vt = [0, 0]
    for ei in range(NE):
        gb1 = ctx.dma(b1sb[:, :], d["b1t"].ap()[ei, :, :],
                      waits=(("ac", b1_free),) if b1_free else (), eng="gpsimd")
        for th in range(2):
            t0 = th * 1024
            mb_done = 0
            gmr = ctx.dma(mr0[:, :], s["maskfm"][ei:ei + 1, t0:t0 + 1024],
                          waits=(("mm", bcfree),) if bcfree else (), eng="gpsimd")
            for sub in range(2):
                bc = ctx.bankf(6)
                wt = [("gd", gmr)]
                if bc_ve:
                    wt.append(("ve", bc_ve))
                m1 = ctx.mm(bc, s["onesrow"][:, :],
                            mr0[0:1, sub * 512: sub * 512 + 512], waits=wt)
                bc_ve = ctx.copy(mbcb[:, sub * 512:(sub + 1) * 512], bc, waits=(("mm", m1),))
                mb_done = bc_ve
            bcfree = ctx.now("mm")
            blkidx = ei * 2 + th
            for wh in range(2):
                bi = (blkidx * 2 + wh) % 2
                w1s = d["w1"].ap()[ei, :, wh * 512:(wh + 1) * 512].rearrange(
                    "(c p) h -> p c h", p=128)
                wt = (("mm", ctx._moe_wfree[bi]),) if ctx._moe_wfree[bi] else ()
                gw1 = ctx.dma(wA[:, bi * 4096:bi * 4096 + 4096], w1s, waits=wt,
                              eng="gpsimd")
                for hq in range(4):
                    hc = wh * 4 + hq
                    gset = 2 * (hc % 2)
                    mmg = 0
                    for sub in range(2):
                        gb = ctx.bankf(gset + sub)
                        for dc in range(DC):
                            lhs = wA[:, bi * 4096 + dc * 512 + hq * 128:
                                      bi * 4096 + dc * 512 + hq * 128 + 128]
                            rhs = s["xn"][:, dc * T + t0 + sub * 512:
                                          dc * T + t0 + sub * 512 + 512]
                            wtm = []
                            if dc == 0 and sub == 0 and hq == 0:
                                wtm.append(("gd", gw1))
                            if dc == 0 and ctx._moe_vt[hc % 2]:
                                wtm.append(("ve", ctx._moe_vt[hc % 2]))
                            mmg = ctx.mm(gb, lhs, rhs, start=(dc == 0),
                                         stop=(dc == DC - 1), waits=wtm)
                    vt = ctx.tt(ttmp[:, :], ctx.sb["PS"][:, gset * 512:gset * 512 + 1024],
                                mbcb[:, :], ALU.mult, waits=(("mm", mmg), ("ve", mb_done)))
                    ctx._moe_vt[hc % 2] = vt
                    wg = [("ve", vt)]
                    if ei == 0 and th == 0 and hc == 0:
                        wg.append(("gd", gb1))
                    a1 = ctx.act(Hm[:, hc * 1024:(hc + 1) * 1024], ttmp[:, :], AF.Gelu,
                                 bias=b1sb[:, hc:hc + 1], waits=wg)
                    b1_free = a1
                    ctx.tt(Hm[:, hc * 1024:(hc + 1) * 1024],
                           Hm[:, hc * 1024:(hc + 1) * 1024], mbcb[:, :], ALU.mult,
                           waits=(("ac", a1),))
                ctx._moe_wfree[bi] = ctx.now("mm")
                mmlast = ctx.now("mm")
            h_done = ctx.now("ve")
            for zh in range(2):
                bi = (blkidx * 2 + zh) % 2
                w2s = d["w2"].ap()[ei, :, zh * 512:(zh + 1) * 512].rearrange(
                    "(c p) o -> p c o", p=128)
                wt2 = (("mm", ctx._moe_w2free[bi]),) if ctx._moe_w2free[bi] else ()
                gw2 = ctx.dma(wB[:, bi * 4096:bi * 4096 + 4096], w2s, waits=wt2,
                              eng="gpsimd")
                for dq in range(4):
                    dco = zh * 4 + dq
                    for sub in range(2):
                        zb = ctx.bankf(4 + (sub % 2))
                        mmz = 0
                        for hc in range(NE):
                            lhs = wB[:, bi * 4096 + hc * 512 + dq * 128:
                                      bi * 4096 + hc * 512 + dq * 128 + 128]
                            rhs = Hm[:, hc * 1024 + sub * 512: hc * 1024 + sub * 512 + 512]
                            wtm = []
                            if hc == 0:
                                wtm.append(("ve", ctx.now("ve")))
                            if hc == 0 and dq == 0 and sub == 0:
                                wtm += [("gd", gw2), ("ve", h_done)]
                            mmz = ctx.mm(zb, lhs, rhs, start=(hc == 0), stop=(hc == NE - 1),
                                         waits=wtm)
                        xc = s["x"][:, dco * T + t0 + sub * 512:
                                    dco * T + t0 + sub * 512 + 512]
                        ctx.stt(xc, zb, 1.0, xc, ALU.mult, ALU.add, waits=(("mm", mmz),))
                ctx._moe_w2free[bi] = ctx.now("mm")
                mmlast = ctx.now("mm")
    gb2 = ctx.dma(b2sb[:, :], d["b2e"].ap(), eng="gpsimd")
    bank_rd2 = [0, 0]
    for dco in range(DC):
        for sub in range(4):
            zb = ctx.bankf(sub % 2)
            wt = []
            if dco == 0 and sub == 0:
                wt = [("gd", gb2), ("mm", mmlast), ("ve", ctx.now("ve"))]
            if bank_rd2[sub % 2]:
                wt.append(("ve", bank_rd2[sub % 2]))
            mmz = ctx.mm(zb, b2sb[0:8, dco * 128:(dco + 1) * 128],
                         s["maskfm"][0:8, sub * 512:(sub + 1) * 512], waits=wt)
            xc = s["x"][:, dco * T + sub * 512: dco * T + sub * 512 + 512]
            bank_rd2[sub % 2] = ctx.stt(xc, zb, 1.0, xc, ALU.mult, ALU.add,
                                        waits=(("mm", mmz),))
    ctx.barrier()
    ctx.pop_scope()


def emit_qlr_prep(ctx):
    s, d = ctx.sb, ctx.dr
    ctx.push_scope()
    qraw = ctx.alloc("qraw", [128, T], BF16)
    wA = ctx.alloc("qlrWA", [128, 1024], BF16)
    ctx.ts(s["cscaled"][0:16, :], s["coreT"][0:16, :], 0.25, None, ALU.mult)
    evac = [0, 0]
    for hp in range(NH // 2):
        wqc = d["Wqkv"].ap()[:, hp * 128:(hp + 1) * 128].rearrange("(c p) o -> p c o", p=128)
        g1 = ctx.dma(wA[:, :], wqc, waits=(("mm", ctx.now("mm")),), eng="gpsimd")
        ap_done = 0
        for ss in range(4):
            pb = ctx.bankf(ss % 2)
            mmp = 0
            for dc in range(DC):
                lhs = wA[:, dc * 128:(dc + 1) * 128]
                rhs = s["xn"][:, dc * T + ss * 512: dc * T + ss * 512 + 512]
                wt = []
                if dc == 0 and ss == 0:
                    wt.append(("gd", g1))
                if dc == 0 and evac[ss % 2]:
                    wt.append(("ac", evac[ss % 2]))
                mmp = ctx.mm(pb, lhs, rhs, start=(dc == 0), stop=(dc == DC - 1), waits=wt)
            ap_done = ctx.act(qraw[:, ss * 512:(ss + 1) * 512], pb, AF.Copy,
                              waits=(("mm", mmp), ("ve", ctx.now("ve"))))
            evac[ss % 2] = ap_done
        for h01 in range(2):
            h = hp * 2 + h01
            rb = h01 * 64
            for ss in range(4):
                lb = ctx.sb["PS"][0:16, 2 * 512: 2 * 512 + 512]
                mq = ctx.mm(lb, s["lsr"][rb:rb + 64, h * 16: h * 16 + 16],
                            qraw[rb:rb + 64, ss * 512:(ss + 1) * 512],
                            waits=(("ac", ap_done), ("ve", ctx.now("ve"))),
                            tp=(rb, 0) if rb else None)
                dst = s["qlr"][rb:rb + 16, hp * T + ss * 512: hp * T + ss * 512 + 512]
                ctx.ts(dst, lb, s["cscaled"][0:16, h:h + 1] if rb == 0 else
                       s["cscaled"][0:16, h:h + 1], None, ALU.mult, waits=(("mm", mq),))
    ctx.barrier()
    ctx.pop_scope()


def emit_l1_attn(ctx):
    s, d = ctx.sb, ctx.dr
    ctx.push_scope()
    slab = ctx.alloc("slab", [128, 2 * 4096], BF16)
    klr = ctx.alloc("klr", [128, 2 * 4096], BF16)
    kraw = ctx.alloc("kraw1", [128, 4096], BF16)
    vtm4 = ctx.alloc("vtm4", [128, 32 * 4 * 65], BF16)
    oall = ctx.alloc("oall1", [128, 2 * T], BF16)
    wAk = ctx.alloc("wAk", [128, 2 * 1024], BF16)
    wAv = ctx.alloc("wAv", [128, 2 * 1024], BF16)
    wC = ctx.alloc("l1WC", [128, 2 * 1024], BF16)
    psb = ctx.alloc("psb1", [128, 1024], BF16)

    if ctx.mode == "C":
        kf_src = d["kf1_in"].ap()
    else:
        kf_src = d["kf1_dram"].ap()
    kfv = kf_src.rearrange("p (c g w) -> p c g w", c=DC, g=2 * NT)

    exp_done = [0, 0]
    pv_done = [0, 0]
    o_norm = 0
    ctx._l1nb = [0, 0]
    ctx._l1pend = None
    for quad in range(4):
        wv_src = d["Wqkv"].ap()[:, 2 * D + quad * 256: 2 * D + (quad + 1) * 256].rearrange(
            "(c p) o -> p c o", p=128)
        gv = ctx.dma(wAv[:, :], wv_src, waits=(("mm", ctx.now("mm")),), eng="gpsimd")
        for pr in range(2):
            hp = quad * 2 + pr
            wk_src = d["Wqkv"].ap()[:, D + hp * 128: D + hp * 128 + 128].rearrange(
                "(c p) o -> p c o", p=128)
            ctx.dma(wAk[:, pr * 1024:(pr + 1) * 1024], wk_src,
                    waits=(("mm", ctx.now("mm")),), eng="gpsimd")
        gwk = ctx.now("gd")
        ms1 = ctx.memset(vtm4[:, :], 1.0, waits=(("mm", ctx.now("mm")),))
        slab_free = [0, 0]
        kevac = [0, 0]
        klr_done = 0
        vdone = 0
        for ss in range(8):
            sl = slab[:, (ss % 2) * 4096:(ss % 2) * 4096 + 4096]
            wt = [("mm", slab_free[ss % 2])] if slab_free[ss % 2] else []
            gsl = ctx.dma(sl, kfv[:, :, 4 * ss:4 * ss + 4, :], waits=wt, eng="gpsimd")
            for pr in range(2):
                pb = ctx.bankf(pr)
                mmp = 0
                for dc in range(DC):
                    lhs = wAk[:, pr * 1024 + dc * 128: pr * 1024 + dc * 128 + 128]
                    rhs = sl[:, dc * 512:(dc + 1) * 512]
                    wtm = []
                    if dc == 0:
                        wtm.append(("gd", gsl))
                        if ss == 0:
                            wtm.append(("gd", gwk))
                            wtm.append(("ac", ctx.now("ac")))
                        if kevac[pr]:
                            wtm.append(("ac", kevac[pr]))
                    mmp = ctx.mm(pb, lhs, rhs, start=(dc == 0), stop=(dc == DC - 1),
                                 waits=wtm)
                ke = ctx.act(kraw[:, ss * 512:(ss + 1) * 512], pb, AF.Copy,
                             waits=(("mm", mmp), ("ve", klr_done)))
                kevac[pr] = ke
                for h01 in range(2):
                    rb = h01 * 64
                    lb = ctx.sb["PS"][0:16, 2 * 512: 2 * 512 + 512]
                    hh = (quad * 2 + pr) * 2 + h01
                    mq = ctx.mm(lb, s["lsr"][rb:rb + 64, 256 + hh * 16:
                                             256 + hh * 16 + 16],
                                kraw[rb:rb + 64, ss * 512:(ss + 1) * 512],
                                waits=(("ac", ke), ("ve", ctx.now("ve"))),
                                tp=(rb, 0) if rb else None)
                    dst = klr[rb:rb + 16, pr * 4096 + ss * 512: pr * 4096 + ss * 512 + 512]
                    klr_done = ctx.copy(dst, lb, waits=(("mm", mq),))
            for j in range(4):
                kc = ss * 4 + j
                vb = ctx.bankf(3, width=256)
                mmv = 0
                for dc in range(DC):
                    lhs = sl[:, dc * 512 + j * 128: dc * 512 + j * 128 + 128]
                    rhs = wAv[:, dc * 256:(dc + 1) * 256]
                    wtm = []
                    if dc == 0:
                        if kc == 0:
                            wtm += [("gd", gv), ("ve", ms1)]
                        if vdone:
                            wtm.append(("ve", vdone))
                    mmv = ctx.mm(vb, lhs, rhs, start=(dc == 0), stop=(dc == DC - 1),
                                 waits=wtm)
                pvv = ctx.sb["PS"].ap()[:, 3 * 512:3 * 512 + 256].rearrange(
                    "p (h w) -> p h w", h=4)
                vv = vtm4.ap().rearrange("p (c h w) -> p c h w", c=32, h=4)
                vdone = ctx.copy(vv[:, kc, :, 0:64], pvv, waits=(("mm", mmv),))
            slab_free[ss % 2] = ctx.now("mm")
        for h4 in range(4):
            h = quad * 4 + h4
            hp = quad * 2 + h4 // 2
            rb = (h4 % 2) * 64
            outs = []
            for idx, (buf, off, n_tok) in enumerate(
                    ((s["qlr"], (quad * 2 + h4 // 2) * T, T),
                     (klr, (h4 // 2) * 4096, 2 * T))):
                sqb = kraw[rb:rb + 16, 0:2048]
                nss = n_tok // 512
                mx = s["rowsA"][0:1, 64 + h4 * 2 + idx: 64 + h4 * 2 + idx + 1]
                first = True
                for half in range((n_tok + 2047) // 2048):
                    a1 = ctx.act(sqb, buf[rb:rb + 16, off + half * 2048: off + half * 2048 + 2048],
                                 AF.Square, waits=(("ve", ctx.now("ve")),))
                    for ssq in range(4):
                        bnk = ctx.bankf(2 + (ssq % 2), p1=1)
                        mmn = ctx.mm(bnk, s["onescol"][rb:rb + 16, :],
                                     sqb[:, ssq * 512:(ssq + 1) * 512],
                                     waits=(("ac", a1), ("ve", ctx.now("ve"))),
                                     tp=(rb, 0) if rb else None)
                        tmp = s["rowsA"][0:1, 60:61]
                        ctx.red(tmp, bnk, ALU.max, waits=(("mm", mmn),))
                        if first:
                            ctx.copy(mx, tmp)
                            first = False
                        else:
                            ctx.tt(mx, mx, tmp, ALU.max)
                outs.append(mx)
            pr_ = s["rowsA"][0:1, 61:62]
            v1 = ctx.tt(pr_, outs[0], outs[1], ALU.mult)
            a2 = ctx.act(pr_, pr_, AF.Sqrt, waits=(("ve", v1),))
            v2 = ctx.ts(pr_, pr_, -1.0, None, ALU.mult, waits=(("ac", a2),))
            m1 = ctx.mm(ctx.bankf(6, width=1), s["onesrowf"][:, :], pr_,
                        waits=(("ve", v2), ("ve", ctx.now("ve"))))
            ctx.copy(s["negB"][:, 2 + h:3 + h], ctx.bankf(6, width=1), waits=(("mm", m1),))
        wo1 = d["Wo1"].ap()[quad * 256:(quad + 1) * 256, :].rearrange(
            "(i p) o -> p i o", p=128)
        wo_dma = ctx.dma(wC[:, :], wo1, waits=(("mm", ctx.now("mm")),), eng="gpsimd")
        for h4 in range(4):
            h = quad * 4 + h4
            rb = (h4 % 2) * 64
            pr = h4 // 2
            negb1 = s["negB"][:, 2 + h:3 + h]
            for qg in range(8):
                q0 = qg * 256
                kmax = 4 * qg + 4
                obk = 4 + (qg % 2)
                ob = ctx.bankf(obk, p1=65, width=256)
                pend = None

                def emit_pv1(kc2, ea, mask_done):
                    nonlocal pv_done
                    g2i = (kc2 // 2) % 2
                    for j in range(2):
                        kc = kc2 + j
                        wv0 = [("ac", ea)]
                        if mask_done:
                            wv0.append(("ve", mask_done))
                        if kc == 0 and ctx._l1nb[qg % 2]:
                            wv0.append(("ve", ctx._l1nb[qg % 2]))
                        pv_done[g2i] = ctx.mm(
                            ob, vtm4[:, (kc * 4 + h4) * 65:(kc * 4 + h4) * 65 + 65],
                            psb[:, g2i * 512 + j * 256: g2i * 512 + j * 256 + 256],
                            start=(kc == 0), stop=(kc == kmax - 1), waits=wv0)

                for kc2 in range(0, kmax, 2):
                    g2i = (kc2 // 2) % 2
                    wt = []
                    if exp_done[g2i]:
                        wt.append(("ac", exp_done[g2i]))
                    if kc2 == 0 and qg == 0 and h4 == 0:
                        wt.append(("ve", vdone))
                        wt.append(("ve", ctx.now("ve")))
                    ctx.mm(ctx.bankf(g2i, lo=0, width=256),
                           klr[rb:rb + 16, pr * 4096 + kc2 * 128: pr * 4096 + kc2 * 128 + 128],
                           s["qlr"][rb:rb + 16, (quad * 2 + pr) * T + q0:
                                    (quad * 2 + pr) * T + q0 + 256],
                           tp=(rb, 0) if rb else None, waits=wt)
                    m1 = ctx.mm(ctx.bankf(g2i, lo=256, width=256),
                                klr[rb:rb + 16, pr * 4096 + kc2 * 128 + 128:
                                    pr * 4096 + kc2 * 128 + 256],
                                s["qlr"][rb:rb + 16, (quad * 2 + pr) * T + q0:
                                         (quad * 2 + pr) * T + q0 + 256],
                                tp=(rb, 0) if rb else None)
                    wexp = [("mm", m1)]
                    if pv_done[g2i]:
                        wexp.append(("mm", pv_done[g2i]))
                    psl = psb[:, g2i * 512:(g2i + 1) * 512]
                    ea = ctx.act(psl, ctx.bankf(g2i), AF.Exp, bias=negb1, waits=wexp)
                    exp_done[g2i] = ea
                    mask_done = 0
                    for j in range(2):
                        kc = kc2 + j
                        i = kc - (kmax - 4)
                        if i >= 0:
                            msl = psb[:, g2i * 512 + j * 256: g2i * 512 + j * 256 + 256]
                            mask_done = ctx.tt(msl, msl,
                                               s["cmask"][:, i * 256:(i + 1) * 256],
                                               ALU.mult, waits=(("ac", ea),))
                    if pend is not None:
                        emit_pv1(*pend)
                    pend = (kc2, ea, mask_done)
                emit_pv1(*pend)
                pva, pvb = pv_done[0], pv_done[1]

                def emit_norm(obk=obk, qg=qg, rb=rb, pr=pr, q0=q0, pva=pva, pvb=pvb):
                    zrow = ctx.sb["PS"][64:65, obk * 512: obk * 512 + 256]
                    rr = s["rowsA"][0:1, 0:256]
                    ctx.ts(rr, zrow, 1e-38, None, ALU.add,
                           waits=(("mm", pva or pvb), ("mm", pvb or pva)))
                    v2 = ctx.recip(rr, rr)
                    bc = ctx.bankf(6, p1=64, width=256)
                    m1 = ctx.mm(bc, s["onesrowf"][0:1, 0:64], rr, waits=(("ve", v2),))
                    v3 = ctx.copy(s["rbc"][0:64, 0:256], bc, waits=(("mm", m1),))
                    dst = oall[rb:rb + 64, pr * T + q0: pr * T + q0 + 256]
                    ctx._l1nb[qg % 2] = ctx.tt(
                        dst, ctx.sb["PS"][0:64, obk * 512:obk * 512 + 256],
                        s["rbc"][0:64, 0:256], ALU.mult,
                        waits=(("ve", v3), ("mm", getattr(ctx, "_flush_mm", 0))))

                if ctx._l1pend is not None:
                    ctx._l1pend()
                ctx._l1pend = emit_norm
        if ctx._l1pend is not None:
            ctx._l1pend()
            ctx._l1pend = None
        if os.environ.get("KDBG") and ctx.mode == "C" and quad == int(os.environ.get("KDBGQ", "3")):
            ctx.barrier()
            ctx.dma(d["dbg_qlr"].ap(), s["qlr"][:, :])
            ctx.dma(d["dbg_klr"].ap(), klr[:, :])
            ctx.dma(d["dbg_vt4"].ap(), vtm4[:, :])
            ctx.dma(d["dbg_nb1"].ap(), s["negB"][:, :])
            ctx.dma(d["dbg_oa1"].ap(), oall[:, :])
            ctx.barrier()
        _flush_outproj(ctx, wo_dma, wC, oall, npairs=2)
        exp_done = [0, 0]
        pv_done = [0, 0]
        o_norm = 0
    ctx.barrier()
    ctx.pop_scope()


def ph_store(ctx):
    s, d = ctx.sb, ctx.dr
    if ctx.mode == "A":
        g = ctx.dma(d["xn_out"].ap(), s["xn"][:, :])
        ctx.p.op("sync", lambda e: e.nop(), waits=(("ld", g),))
        return
    if ctx.mode == "B":
        ctx.dma(d["x1_out"].ap().rearrange("(c p) t -> p c t", p=128), s["x"][:, :])
        g2 = ctx.dma(d["xn2_out"].ap(), s["xn"][:, :])
        ctx.p.op("sync", lambda e: e.nop(), waits=(("ld", g2),))
        return
    g = ctx.dma(d["y"].ap().rearrange("(c p) t -> p c t", p=128), s["x"][:, :])
    ctx.p.op("sync", lambda e: e.nop(), waits=(("ld", g),))


def emit_all(ctx):
    ph_load(ctx)
    if ctx.mode == "A":
        ctx.sb["xn"] = ctx.alloc("xn", [128, DC * T], BF16)
        emit_ln(ctx)
        ph_store(ctx)
        return
    if ctx.mode == "B":
        upto = int(os.environ.get("KUPTO", "9"))
        ctx.push_scope()
        ctx.sb["xn"] = ctx.alloc("xn", [128, DC * T], BF16)
        ctx.dma(ctx.sb["xn"][:, :], ctx.dr["xn_own_in"].ap())
        ctx.barrier()
        if upto >= 1:
            emit_negB_l0(ctx)
        if upto >= 2:
            emit_ffn0(ctx)
        ctx.pop_scope()
        if upto >= 3:
            emit_l0_attn(ctx)
        ctx.push_scope()
        ctx.sb["xn"] = ctx.alloc("xn", [128, DC * T], BF16)
        if upto >= 4:
            emit_ln(ctx)
        if upto >= 5:
            emit_router_S(ctx)
        ph_store(ctx)
        ctx.pop_scope()
        return
    if ctx.mode == "C":
        ctx.push_scope()
        ctx.sb["qlr"] = ctx.alloc("qlr", [128, 8 * T], BF16)
        ctx.push_scope()
        ctx.sb["xn"] = ctx.alloc("xn", [128, DC * T], BF16)
        ctx.sb["maskfm"] = ctx.alloc("maskfm", [8, T], BF16)
        ctx.dma(ctx.sb["xn"][:, :], ctx.dr["xn2own"].ap())
        ctx.barrier()
        emit_router_S(ctx)
        emit_mask_final(ctx)
        emit_qlr_prep(ctx)
        if os.environ.get("KC_MOE", "1") == "1":
            emit_moe(ctx)
        ctx.pop_scope()
        if os.environ.get("KC_ATTN", "1") == "1":
            emit_l1_attn(ctx)
        ctx.pop_scope()
        ph_store(ctx)
        return
    up = int(os.environ.get("KF_UP", "9"))
    ctx.push_scope()
    ctx.sb["xn"] = ctx.alloc("xn", [128, DC * T], BF16)
    emit_ln(ctx)
    emit_gather_issue(ctx, 0)
    emit_negB_l0(ctx)
    if up >= 1:
        emit_ffn0(ctx)
    ctx.pop_scope()
    if up >= 2:
        emit_l0_attn(ctx)
    if up >= 3:
        ctx.push_scope()
        ctx.sb["qlr"] = ctx.alloc("qlr", [128, 8 * T], BF16)
        ctx.push_scope()
        ctx.sb["xn"] = ctx.alloc("xn", [128, DC * T], BF16)
        ctx.sb["maskfm"] = ctx.alloc("maskfm", [8, T], BF16)
        emit_ln(ctx)
        emit_router_S(ctx)
        emit_gather_issue(ctx, 1)
        emit_mask_final(ctx)
        if os.environ.get("KDBG"):
            ctx.dma(ctx.dr["dbg_sm"].ap(), ctx.sb["smalls"][:, :])
            ctx.dma(ctx.dr["dbg_mt"].ap(), ctx.sb["masktok"][:, :])
            ctx.barrier()
        emit_qlr_prep(ctx)
        if up >= 4:
            emit_moe(ctx)
        ph_assemble1(ctx)
        ctx.pop_scope()
        if up >= 5:
            emit_l1_attn(ctx)
        ctx.pop_scope()
    ph_store(ctx)


def build_program(mode):
    nc = bass.Bass("TRN2", target_bir_lowering=False)
    ctx = Ctx(nc, mode)
    declare_dram(ctx)
    root = ExitStack()
    ctx.scopes = [root]
    with root:
        declare_sbuf_persistent(ctx)
        sems = {}
        for snm in ("mm", "ve", "ac", "ld", "gd", "cc"):
            sems[snm] = root.enter_context(nc.semaphore(snm))
        block = root.enter_context(nc.Block())
        emit_all(ctx)
        with nc.allow_low_precision(reason="bf16 operands with fp32 psum accumulation by design"):
            ctx.p.emit(block, sems)
    return nc


def kernel(**inputs):
    w = host_weights(inputs)
    shared = host_constants()
    x0 = np.asarray(inputs["L0_attn_x"], np.float32)

    nc = build_program("full")
    maps = []
    for core in range(NCORES):
        dd = dict(shared)
        dd.update(core_constants(core))
        dd["x0"] = _f32(x0[core // 2][own_rows(core)].T)
        for knm in ("Wq", "Wk", "Wv", "Wo", "bq", "bk", "bv", "bo",
                    "Wg_t", "Wu_t", "Wd_t", "RW", "Rb",
                    "Wqkv", "Wo1", "Wqlsr", "Wklsr", "coreT",
                    "w1", "w2", "b1t", "b2e"):
            dd[knm] = w[knm]
        for nm in ("q", "k", "v"):
            for suf in ("s128", "b128", "srow", "brow"):
                dd[f"{nm}{suf}"] = w[f"{nm}{suf}"]
        maps.append(dd)
    res = run_bass_kernel_spmd(nc, maps, core_ids=list(range(NCORES)))

    out = np.empty((B, S, D), dtype=np.float32)
    for core in range(NCORES):
        b = core // 2
        p = core % 2
        y = np.asarray(res.results[core]["y"], np.float32).T
        for j in range(NT):
            out[b, (2 * j + p) * 128:(2 * j + p + 1) * 128, :] = y[j * 128:(j + 1) * 128]
    return out


# version 07
# speedup vs baseline: 1.5167x; 1.0264x over previous
import os
import numpy as np
import ml_dtypes
from contextlib import ExitStack
import concourse.bass as bass
import concourse.mybir as mybir
from concourse.bass_utils import run_bass_kernel_spmd

F32 = mybir.dt.float32
BF16 = mybir.dt.bfloat16
I32 = mybir.dt.int32
AF = mybir.ActivationFunctionType
ALU = mybir.AluOpType
AX = mybir.AxisListType

B, S, D, NH, HD, R = 4, 4096, 1024, 16, 64, 16
T = 2048
NT = 16
DC = 8
HID = 3072
NE, MH = 8, 1024
CAP = 2560.0
EPS_LN = 1e-5
EPS_QK = 1e-6
INV127 = 1.0 / 127.0
NCORES = 8
XN_ELEMS = 128 * DC * T


class Prog:
    ENGINES = ("tensor", "vector", "scalar", "gpsimd", "sync")

    def __init__(self):
        self.ops = []
        self.cnt = {}
        self.pending_barrier = {}

    def sem(self, name):
        self.cnt[name] = 0
        return name

    SELF_SERIAL = {"vector": "ve"}

    def op(self, engine, fn, waits=(), inc=(), inc_amt=1):
        if isinstance(inc, str):
            inc = (inc,)
        w = list(self.pending_barrier.pop(engine, ())) + list(waits)
        ss = self.SELF_SERIAL.get(engine)
        if ss is not None and self.cnt[ss] > 0:
            w.append((ss, self.cnt[ss]))
        incs = []
        for snm in inc:
            self.cnt[snm] += inc_amt
            incs.append((snm, inc_amt))
        self.ops.append((engine, fn, tuple(w), tuple(incs)))
        return self.cnt[inc[0]] if inc else None

    def barrier(self, sems):
        snap = [(snm, self.cnt[snm]) for snm in sems if self.cnt[snm] > 0]
        for e in self.ENGINES:
            cur = self.pending_barrier.get(e, [])
            self.pending_barrier[e] = cur + snap

    def emit(self, block, handles):
        per = {e: [] for e in self.ENGINES}
        for rec in self.ops:
            per[rec[0]].append(rec)

        def mk(engine_name):
            def body(eng):
                for (_, fn, waits, incs) in per[engine_name]:
                    seen = {}
                    for (snm, v) in waits:
                        if seen.get(snm, -1) < v:
                            seen[snm] = v
                    for snm, v in seen.items():
                        eng.wait_ge(handles[snm], v)
                    inst = fn(eng)
                    for (snm, a) in incs:
                        inst.then_inc(handles[snm], a)
            return body

        block.tensor(mk("tensor"))
        block.vector(mk("vector"))
        block.scalar(mk("scalar"))
        block.gpsimd(mk("gpsimd"))
        block.sync(mk("sync"))


class Ctx:
    def __init__(self, nc, mode):
        self.nc = nc
        self.mode = mode
        self.p = Prog()
        for snm in ("mm", "ve", "ac", "ld", "gd", "cc"):
            self.p.sem(snm)
        self.dr = {}
        self.sb = {}

    def mm(self, out, lhsT, rhs, start=True, stop=True, waits=(), tp=None):
        def fn(e, out=out, lhsT=lhsT, rhs=rhs, start=start, stop=stop, tp=tp):
            return e.matmul(out, lhsT, rhs, start=start, stop=stop, tile_position=tp)
        return self.p.op("tensor", fn, waits=waits, inc="mm")

    def trans(self, out, in_, ident, waits=()):
        return self.p.op("tensor",
                         lambda e, o=out, i=in_, d=ident: e.transpose(o, i, d),
                         waits=waits, inc="mm")

    def act(self, out, in_, func, bias=0.0, scale=1.0, waits=()):
        def fn(e, out=out, in_=in_, func=func, bias=bias, scale=scale):
            return e.activation(out, in_, func, bias=bias, scale=scale)
        return self.p.op("scalar", fn, waits=waits, inc="ac")

    def copy(self, out, in_, waits=()):
        return self.p.op("vector", lambda e, o=out, i=in_: e.tensor_copy(o, i),
                         waits=waits, inc="ve")

    def tt(self, out, in0, in1, op, waits=()):
        return self.p.op("vector",
                         lambda e, o=out, a=in0, b=in1, op=op: e.tensor_tensor(o, a, b, op),
                         waits=waits, inc="ve")

    def ts(self, out, in0, s1, s2=None, op0=ALU.mult, op1=ALU.add, waits=()):
        if s2 is None:
            op1 = ALU.bypass
        def fn(e, o=out, a=in0, s1=s1, s2=s2, op0=op0, op1=op1):
            return e.tensor_scalar(o, a, s1, s2, op0, op1)
        return self.p.op("vector", fn, waits=waits, inc="ve")

    def stt(self, out, in0, scalar, in1, op0, op1, waits=()):
        def fn(e, o=out, a=in0, s=scalar, b=in1, op0=op0, op1=op1):
            return e.scalar_tensor_tensor(o, a, s, b, op0, op1)
        return self.p.op("vector", fn, waits=waits, inc="ve")

    def red(self, out, in_, op, negate=False, absv=False, waits=()):
        def fn(e, o=out, i=in_, op=op, neg=negate, ab=absv):
            return e.tensor_reduce(o, i, AX.X, op, apply_absolute_value=ab, negate=neg)
        return self.p.op("vector", fn, waits=waits, inc="ve")

    def recip(self, out, in_, waits=()):
        return self.p.op("vector", lambda e, o=out, i=in_: e.reciprocal(o, i),
                         waits=waits, inc="ve")

    def max8(self, out, in_, waits=()):
        return self.p.op("vector", lambda e, o=out, i=in_: e.max(out=o, in_=i),
                         waits=waits, inc="ve")

    def memset(self, ap, val, waits=()):
        return self.p.op("vector", lambda e, a=ap, v=val: e.memset(a, v),
                         waits=waits, inc="ve")

    def dma(self, dst, src, waits=(), eng="sync"):
        if len(src.shape) == 3 and len(dst.shape) == 2:
            dst = dst.rearrange("p (c w) -> p c w", c=src.shape[1])
        elif len(dst.shape) == 3 and len(src.shape) == 2:
            src = src.rearrange("p (c w) -> p c w", c=dst.shape[1])
        snm = "ld" if eng == "sync" else "gd"
        return self.p.op(eng, lambda e, d=dst, s=src: e.dma_start(out=d, in_=s),
                         waits=waits, inc=snm, inc_amt=16)

    def dma_fn(self, fn, waits=(), eng="gpsimd"):
        snm = "ld" if eng == "sync" else "gd"
        return self.p.op(eng, fn, waits=waits, inc=snm, inc_amt=16)

    def coll(self, kind, op, ins, outs, waits=()):
        def fn(e, kind=kind, op=op, ins=ins, outs=outs):
            return e.collective_compute(kind, op, replica_groups=[list(range(NCORES))],
                                        ins=ins, outs=outs)
        return self.p.op("gpsimd", fn, waits=waits, inc="cc", inc_amt=1)

    def barrier(self):
        self.p.barrier(("mm", "ve", "ac", "ld", "gd", "cc"))

    def now(self, snm):
        return self.p.cnt[snm]

    def bankf(self, i, lo=0, width=512, p0=0, p1=128):
        return self.sb["PS"][p0:p1, i * 512 + lo: i * 512 + lo + width]

    def alloc(self, name, shape, dtype, psum=False):
        fn = self.nc.psum_tensor if psum else self.nc.sbuf_tensor
        self._nalloc = getattr(self, "_nalloc", 0) + 1
        return self.scopes[-1].enter_context(fn(f"sb{self._nalloc}_{name}", shape, dtype))

    def push_scope(self):
        es = ExitStack()
        self.scopes.append(es)
        return es

    def pop_scope(self):
        self.scopes.pop().close()


def _bf(x):
    return np.ascontiguousarray(np.asarray(x), dtype=ml_dtypes.bfloat16)


def _f32(x):
    return np.ascontiguousarray(np.asarray(x), dtype=np.float32)


def feat8(v):
    return _f32(np.asarray(v, np.float32).reshape(8, 128).T)


def host_constants():
    c = {}
    c["identf"] = _f32(np.eye(128, dtype=np.float32))
    c["identb"] = _bf(np.eye(128, dtype=np.float32))
    Bm = np.eye(64, dtype=np.float32) - 1.0 / 64.0
    Z = np.zeros((64, 64), np.float32)
    c["bpair"] = _bf(np.block([[Bm, Z], [Z, Bm]]))
    J = np.ones((64, 64), np.float32)
    c["ones64"] = _bf(np.block([[J, Z], [Z, J]]))
    c["onesrow"] = _bf(np.ones((1, 128), np.float32))
    c["onesrowf"] = _f32(np.ones((1, 128), np.float32))
    c["onescol"] = _bf(np.ones((128, 1), np.float32))
    return c


def core_constants(core):
    p = core % 2
    c = {}
    c["rkall"] = np.array([[core, (core // 2) * 2, (core // 2) * 2 + 1, core ^ 1]],
                          dtype=np.int32)
    row = np.arange(128)[:, None]
    col = np.arange(256)[None, :]
    masks = []
    for i in range(4):
        valid = (i * 128 + row) <= (2 * (col // 128) * 128 + p * 128 + col % 128)
        masks.append(valid.astype(np.float32))
    c["cmask"] = _bf(np.stack(masks))
    c["flagS"] = _f32(np.full((8, 2), 1.0 if core == 0 else 0.0, np.float32))
    return c


def host_weights(inputs):
    w = {}
    w["Wq"] = _f32(inputs["L0_attn_Wq"]); w["Wk"] = _f32(inputs["L0_attn_Wk"])
    w["Wv"] = _f32(inputs["L0_attn_Wv"]); w["Wo"] = _f32(inputs["L0_attn_Wo"])
    w["bq"] = feat8(inputs["L0_attn_bq"]); w["bk"] = feat8(inputs["L0_attn_bk"])
    w["bv"] = feat8(inputs["L0_attn_bv"]); w["bo"] = feat8(inputs["L0_attn_bo"])
    for nm in ("q", "k", "v"):
        sc = np.asarray(inputs[f"L0_attn_{nm}_scale"], np.float32)
        bi = np.asarray(inputs[f"L0_attn_{nm}_bias"], np.float32)
        w[f"{nm}s128"] = _f32(np.tile(sc, 2)[:, None])
        w[f"{nm}b128"] = _f32(np.tile(bi, 2)[:, None])
        w[f"{nm}srow"] = _f32(sc[None, :])
        w[f"{nm}brow"] = _f32(bi[None, :])
    w["Wg_t"] = np.ascontiguousarray(np.asarray(inputs["L0_ffn_w_gate"], np.int32).T)
    w["Wu_t"] = np.ascontiguousarray(np.asarray(inputs["L0_ffn_w_up"], np.int32).T)
    w["Wd_t"] = np.ascontiguousarray(np.asarray(inputs["L0_ffn_w_down"], np.int32).T)
    w["Wqkv"] = _f32(inputs["L1_attn_W_qkv"])
    w["Wo1"] = _f32(inputs["L1_attn_W_o"])
    wq_lsr = np.asarray(inputs["L1_attn_W_q_lsr"], np.float32)
    wk_lsr = np.asarray(inputs["L1_attn_W_k_lsr"], np.float32)
    w["Wqlsr"] = _f32(np.transpose(wq_lsr, (1, 0, 2)).reshape(64, 256))
    w["Wklsr"] = _f32(np.transpose(wk_lsr, (1, 0, 2)).reshape(64, 256))
    w["coreT"] = _f32(np.asarray(inputs["L1_attn_core"], np.float32).T)
    w["RW"] = _f32(inputs["L1_ffn_router_w"])
    w["Rb"] = _f32(np.asarray(inputs["L1_ffn_router_b"], np.float32)[:, None])
    w["w1"] = _f32(inputs["L1_ffn_w1"])
    w["w2"] = _f32(inputs["L1_ffn_w2"])
    b1 = np.asarray(inputs["L1_ffn_b1"], np.float32)
    w["b1t"] = _f32(np.stack([b1[e].reshape(8, 128).T for e in range(NE)]))
    w["b2e"] = _f32(inputs["L1_ffn_b2"])
    return w


def own_rows(core):
    p = core % 2
    return np.concatenate([np.arange(128) + (2 * j + p) * 128 for j in range(NT)])


def declare_dram(ctx):
    nc, mode = ctx.nc, ctx.mode
    t = ctx.dr

    def inp(name, shape, dt=F32):
        t[name] = nc.dram_tensor(name, shape, dt, kind="ExternalInput")

    inp("identf", [128, 128]); inp("identb", [128, 128], BF16)
    inp("bpair", [128, 128], BF16); inp("ones64", [128, 128], BF16)
    inp("onesrow", [1, 128], BF16); inp("onesrowf", [1, 128])
    inp("onescol", [128, 1], BF16)
    inp("rkall", [1, 4], I32)
    inp("cmask", [4, 128, 256], BF16); inp("flagS", [8, 2])

    if mode in ("A", "B", "full"):
        inp("x0", [D, T])
    if mode in ("B", "full"):
        inp("Wq", [D, D]); inp("Wk", [D, D]); inp("Wv", [D, D]); inp("Wo", [D, D])
        inp("bq", [128, 8]); inp("bk", [128, 8]); inp("bv", [128, 8]); inp("bo", [128, 8])
        for nm in ("q", "k", "v"):
            inp(f"{nm}s128", [128, 1]); inp(f"{nm}b128", [128, 1])
            inp(f"{nm}srow", [1, 64]); inp(f"{nm}brow", [1, 64])
        inp("Wg_t", [D, HID], I32); inp("Wu_t", [D, HID], I32); inp("Wd_t", [HID, D], I32)
    if mode in ("B", "C", "full"):
        inp("RW", [D, 8]); inp("Rb", [8, 1])
    if mode in ("C", "full"):
        inp("Wqkv", [D, 3 * D]); inp("Wo1", [D, D])
        inp("Wqlsr", [64, 256]); inp("Wklsr", [64, 256]); inp("coreT", [16, 16])
        inp("w1", [NE, D, MH]); inp("w2", [NE, MH, D])
        inp("b1t", [NE, 128, 8]); inp("b2e", [NE, D])
    if mode == "B":
        inp("xn_own_in", [128, DC * T], BF16)
        inp("kf0_in", [128, DC * 2 * T], BF16)
    if mode == "C":
        inp("x1", [D, T])
        if os.environ.get("KDBG"):
            t["dbg_qlr"] = nc.dram_tensor("dbg_qlr", [128, 8 * T], BF16, kind="ExternalOutput")
            t["dbg_klr"] = nc.dram_tensor("dbg_klr", [128, 2 * 4096], BF16, kind="ExternalOutput")
            t["dbg_vt4"] = nc.dram_tensor("dbg_vt4", [128, 32 * 4 * 65], BF16, kind="ExternalOutput")
            t["dbg_nb1"] = nc.dram_tensor("dbg_nb1", [128, 18], F32, kind="ExternalOutput")
            t["dbg_oa1"] = nc.dram_tensor("dbg_oa1", [128, 2 * T], BF16, kind="ExternalOutput")
        inp("xn2own", [128, DC * T], BF16)
        inp("kf1_in", [128, DC * 2 * T], BF16)
        inp("Sall", [8, 2 * NCORES])

    if mode == "A":
        t["xn_out"] = nc.dram_tensor("xn_out", [128, DC * T], BF16, kind="ExternalOutput")
    if mode == "B":
        t["x1_out"] = nc.dram_tensor("x1_out", [D, T], F32, kind="ExternalOutput")
        t["xn2_out"] = nc.dram_tensor("xn2_out", [128, DC * T], BF16, kind="ExternalOutput")
        t["Sp_out"] = nc.dram_tensor("Sp_out", [8, 2], F32, kind="ExternalOutput")
        if os.environ.get("KDBG"):
            t["dbg_qh"] = nc.dram_tensor("dbg_qh", [128, T], BF16, kind="ExternalOutput")
            t["dbg_kr"] = nc.dram_tensor("dbg_kr", [128, 2 * T], BF16, kind="ExternalOutput")
            t["dbg_vt"] = nc.dram_tensor("dbg_vt", [128, 32 * 2 * 65], BF16, kind="ExternalOutput")
            t["dbg_nb"] = nc.dram_tensor("dbg_nb", [128, 1], F32, kind="ExternalOutput")
            t["dbg_oa"] = nc.dram_tensor("dbg_oa", [128, 2 * T], BF16, kind="ExternalOutput")
            t["dbg_wc"] = nc.dram_tensor("dbg_wc", [128, 2 * 1024], BF16, kind="ExternalOutput")
    if mode in ("C", "full"):
        t["y"] = nc.dram_tensor("y", [D, T], F32, kind="ExternalOutput")

    if mode == "full":
        if os.environ.get("KDBG"):
            t["dbg_sm"] = nc.dram_tensor("dbg_sm", [128, 64], F32, kind="ExternalOutput")
            t["dbg_mt"] = nc.dram_tensor("dbg_mt", [128, 144], F32, kind="ExternalOutput")
        t["cc_xn_in"] = nc.dram_tensor("cc_xn_in", [XN_ELEMS], BF16)
        t["cc_xn0_out"] = nc.dram_tensor("cc_xn0_out", [NCORES * XN_ELEMS], BF16,
                                         addr_space="Shared")
        t["cc_xn1_out"] = nc.dram_tensor("cc_xn1_out", [NCORES * XN_ELEMS], BF16,
                                         addr_space="Shared")
        t["cc_S_in"] = nc.dram_tensor("cc_S_in", [16], F32)
        t["cc_S_out"] = nc.dram_tensor("cc_S_out", [16], F32, addr_space="Shared")
        t["kf1_dram"] = nc.dram_tensor("kf1_dram", [128, DC * 2 * T], BF16)


def declare_sbuf_persistent(ctx):
    s = ctx.sb
    A = ctx.alloc
    s["x"] = A("x", [128, DC * T], F32)
    s["consts"] = A("consts", [128, 384], BF16)
    s["identf"] = A("identf", [128, 128], F32)
    s["onesrow"] = A("onesrow", [1, 128], BF16)
    s["onesrowf"] = A("onesrowf", [1, 128], F32)
    s["onescol"] = A("onescol", [128, 1], BF16)
    s["bias8"] = A("bias8", [128, 40], F32)
    s["qk128"] = A("qk128", [128, 12], F32)
    s["rows64"] = A("rows64", [1, 384], F32)
    s["rowsA"] = A("rowsA", [1, 512], F32)
    s["smalls"] = A("smalls", [128, 64], F32)
    s["negB"] = A("negB", [128, 18], F32)
    s["ints"] = A("ints", [1, 4], I32)
    s["flagS"] = A("flagS", [8, 2], F32)
    s["cmask"] = A("cmask", [128, 4 * 256], BF16)
    s["coreT"] = A("coreT", [16, 16], F32)
    s["cscaled"] = A("cscaled", [16, 16], F32)
    s["lsr"] = A("lsr", [128, 512], BF16)
    s["rbc"] = A("rbc", [128, 1024], F32)
    s["rstdb"] = A("rstdb", [128, 1024], BF16)
    s["bbc"] = A("bbc", [128, 512], BF16)
    s["masktok"] = A("masktok", [128, 16 + 128], F32)
    s["epsc"] = A("epsc", [128, 2], F32)
    s["biasc"] = A("biasc", [128, 24], F32)
    s["sqsl"] = A("sqsl", [128, 1024], BF16)
    s["PS"] = A("PS", [128, 3584], F32, psum=True)
    s["PSB"] = A("PSB", [128, 1024], BF16, psum=True)


def ph_load(ctx):
    s, d, mode = ctx.sb, ctx.dr, ctx.mode
    ctx.dma(s["identf"][:, :], d["identf"].ap())
    ctx.dma(s["consts"][:, 0:128], d["identb"].ap(), eng="gpsimd")
    ctx.dma(s["consts"][:, 128:256], d["bpair"].ap(), eng="gpsimd")
    ctx.dma(s["consts"][:, 256:384], d["ones64"].ap(), eng="gpsimd")
    ctx.dma(s["onesrow"][:, :], d["onesrow"].ap(), eng="gpsimd")
    ctx.dma(s["onesrowf"][:, :], d["onesrowf"].ap())
    ctx.dma(s["onescol"][:, :], d["onescol"].ap(), eng="gpsimd")
    ctx.dma(s["ints"][:, :], d["rkall"].ap())
    ctx.dma(s["cmask"].ap().rearrange("p (m w) -> p m w", m=4),
            d["cmask"].ap().rearrange("m p w -> p m w"))
    ctx.dma(s["flagS"][:, :], d["flagS"].ap())
    ctx.memset(s["epsc"][:, 0:1], EPS_LN)
    ctx.memset(s["epsc"][:, 1:2], EPS_QK)
    if mode in ("A", "B", "full"):
        ctx.dma(s["x"][:, :], d["x0"].ap().rearrange("(c p) t -> p c t", p=128))
    if mode == "C":
        ctx.dma(s["x"][:, :], d["x1"].ap().rearrange("(c p) t -> p c t", p=128))
        ctx.dma(s["bias8"][0:8, 0:16], d["Sall"].ap())
    if mode in ("B", "full"):
        for i, nm in enumerate(("bq", "bk", "bv", "bo")):
            ctx.dma(s["bias8"][:, 8 * i:8 * i + 8], d[nm].ap())
        for i, nm in enumerate(("qs128", "qb128", "ks128", "kb128", "vs128", "vb128")):
            ctx.dma(s["qk128"][:, i:i + 1], d[nm].ap())
        for i, nm in enumerate(("qsrow", "qbrow", "ksrow", "kbrow", "vsrow", "vbrow")):
            ctx.dma(s["rows64"][:, 64 * i:64 * i + 64], d[nm].ap())
    if mode in ("B", "C", "full"):
        ctx.dma(s["bias8"][0:8, 32:33], d["Rb"].ap())
    if mode in ("C", "full"):
        ctx.dma(s["coreT"][:, :], d["coreT"].ap())
        ctx.dma(s["lsr"][0:64, 0:256], d["Wqlsr"].ap(), eng="gpsimd")
        ctx.dma(s["lsr"][0:64, 256:512], d["Wklsr"].ap(), eng="gpsimd")
        ctx.dma(s["lsr"][64:128, 0:256], d["Wqlsr"].ap(), eng="gpsimd")
        ctx.dma(s["lsr"][64:128, 256:512], d["Wklsr"].ap(), eng="gpsimd")
    ctx.barrier()


def emit_ln(ctx):
    s = ctx.sb
    ctx.push_scope()
    rex = ctx.alloc("lnrows", [1, 1536], F32)
    xb = s["xn"]
    casts = []
    for dc in range(DC):
        casts.append(ctx.copy(xb[:, dc * T:(dc + 1) * T], s["x"][:, dc * T:(dc + 1) * T]))
    sq_mm = [0, 0]
    tt_last = 0
    for q in range(4):
        q0 = q * 512
        sums = ctx.bankf(0, p1=1)
        ssq = ctx.bankf(1, p1=1)
        mm_s = mm_q = 0
        for dc in range(DC):
            xc = xb[:, dc * T + q0: dc * T + q0 + 512]
            sq = s["sqsl"][:, (dc % 2) * 512:(dc % 2) * 512 + 512]
            aw = [("ve", casts[dc])]
            if sq_mm[dc % 2]:
                aw.append(("mm", sq_mm[dc % 2]))
            a1 = ctx.act(sq, xc, AF.Square, waits=aw)
            mm_s = ctx.mm(sums, s["onescol"][:, :], xc, start=(dc == 0), stop=(dc == DC - 1),
                          waits=(("ve", casts[dc]), ("ve", tt_last)))
            mm_q = ctx.mm(ssq, s["onescol"][:, :], sq, start=(dc == 0), stop=(dc == DC - 1),
                          waits=(("ac", a1),))
            sq_mm[dc % 2] = mm_q
        rA = s["rowsA"][0:1, 0:512]
        rB = rex[0:1, 0:512]
        rC = rex[0:1, 512:1024]
        rD = rex[0:1, 1024:1536]
        v1 = ctx.ts(rA, sums, 1.0 / D, None, ALU.mult, waits=(("mm", mm_s),))
        a1 = ctx.act(rB, rA, AF.Square, waits=(("ve", v1),))
        v2 = ctx.stt(rB, ssq, 1.0 / D, rB, ALU.mult, ALU.subtract,
                     waits=(("mm", mm_q), ("ac", a1)))
        a2 = ctx.act(rB, rB, AF.Sqrt, bias=s["epsc"][0:1, 0:1], waits=(("ve", v2),))
        v3 = ctx.recip(rC, rB, waits=(("ac", a2),))
        v4 = ctx.tt(rD, rA, rC, ALU.mult)
        bcA = ctx.bankf(2)
        bcB = ctx.bankf(3)
        m1 = ctx.mm(bcA, s["onesrowf"][:, :], rC, waits=(("ve", v3), ("ac", ctx.now("ac"))))
        m2 = ctx.mm(bcB, s["onesrowf"][:, :], rD, waits=(("ve", v4),))
        e1 = ctx.act(s["rstdb"][:, 0:512], bcA, AF.Copy, waits=(("mm", m1), ("ve", tt_last)))
        e2 = ctx.act(s["bbc"][:, :], bcB, AF.Copy, waits=(("mm", m2),))
        for dc in range(DC):
            xc = xb[:, dc * T + q0: dc * T + q0 + 512]
            ctx.tt(xc, xc, s["rstdb"][:, 0:512], ALU.mult, waits=(("ac", e2), ("mm", mm_q)))
            tt_last = ctx.tt(xc, xc, s["bbc"][:, :], ALU.subtract)
    ctx.barrier()
    ctx.pop_scope()


def emit_gather_issue(ctx, layer):
    s, d = ctx.sb, ctx.dr
    if ctx.mode != "full":
        return
    ccin = d["cc_xn_in"].ap().rearrange("(p n) -> p n", p=128)
    ccout = d["cc_xn0_out"] if layer == 0 else d["cc_xn1_out"]
    g1 = ctx.dma(ccin, s["xn"][:, :], eng="gpsimd")
    ctx.cc_xn = ctx.coll("AllGather", ALU.bypass, [d["cc_xn_in"].ap()], [ccout.ap()],
                         waits=(("gd", g1),))


def ph_assemble0(ctx):
    s, d = ctx.sb, ctx.dr
    if ctx.mode == "B":
        ctx.dma(s["kfull"][:, :], d["kf0_in"].ap())
        ctx.barrier()
        return
    agv = d["cc_xn0_out"].ap().rearrange("(r p c w) -> r p c w", r=NCORES, p=128, c=DC)
    kv = s["kfull"].ap().rearrange("p (c g w) -> p c g w", c=DC, g=2)
    cw = ctx.cc_xn

    def fetch(slot, off):
        def fn(e, slot=slot, off=off):
            reg = e.alloc_register()
            e.reg_load(reg, ctx.sb["ints"][0:1, slot:slot + 1])
            rv = e.snap(reg, min_val=0, max_val=7)
            return e.dma_start(out=kv[:, :, off, :], in_=agv[rv, :, :, :])
        return fn

    ctx.p.op("gpsimd", fetch(0, 0), waits=(("cc", cw),), inc="gd", inc_amt=16)
    ctx.p.op("gpsimd", fetch(3, 1), inc="gd", inc_amt=16)
    ctx.barrier()


def ph_assemble1(ctx):
    s, d = ctx.sb, ctx.dr
    if ctx.mode != "full":
        return
    agv = d["cc_xn1_out"].ap().rearrange("(r p c j w) -> r p c j w",
                                         r=NCORES, p=128, c=DC, j=NT)
    kv = d["kf1_dram"].ap().rearrange("p (c g w) -> p c g w", c=DC, g=2 * NT)
    cw = ctx.cc_xn

    _rv = {}

    def fetch(slot, off, dc):
        def fn(e, slot=slot, off=off, dc=dc):
            if slot not in _rv:
                reg = e.alloc_register()
                e.reg_load(reg, ctx.sb["ints"][0:1, slot:slot + 1])
                _rv[slot] = e.snap(reg, min_val=0, max_val=7)
            return e.dma_start(out=kv[:, dc, off:2 * NT:2, :], in_=agv[_rv[slot], :, dc, :, :])
        return fn

    for dc in range(DC):
        ctx.p.op("gpsimd", fetch(1, 0, dc), waits=(("cc", cw),) if dc == 0 else (),
                 inc="gd", inc_amt=16)
        ctx.p.op("gpsimd", fetch(2, 1, dc), inc="gd", inc_amt=16)
    ctx.barrier()


def emit_ffn0(ctx):
    s, d = ctx.sb, ctx.dr
    ctx.push_scope()
    H = ctx.alloc("ffnH", [128, 24 * 1024], BF16)
    wA = ctx.alloc("ffnWA", [128, 8 * 1024], BF16)
    wB = ctx.alloc("ffnWB", [128, 8 * 1024], BF16)
    wC = ctx.alloc("ffnWC", [128, 6 * 1024], BF16)
    sg = ctx.sb["sqsl"]
    wfree = [0, 0]
    dfree = [0, 0]
    h_done = 0
    for th in range(2):
        t0 = th * 1024
        for blk in range(6):
            src_g = d["Wg_t"].ap()[:, blk * 512:(blk + 1) * 512].rearrange(
                "(c p) h -> p c h", p=128)
            src_u = d["Wu_t"].ap()[:, blk * 512:(blk + 1) * 512].rearrange(
                "(c p) h -> p c h", p=128)
            half = (blk % 2) * 4096
            w = (("mm", wfree[blk % 2]),) if wfree[blk % 2] else ()
            gdg = ctx.dma(wA[:, half:half + 4096], src_g, waits=w, eng="gpsimd")
            gdu = ctx.dma(wB[:, half:half + 4096], src_u, waits=w, eng="gpsimd")
            for hq in range(4):
                hc = blk * 4 + hq
                mmg = mmu = 0
                for sub in range(2):
                    gb = ctx.bankf(sub)
                    for dc in range(DC):
                        lhs_g = wA[:, half + dc * 512 + hq * 128:
                                   half + dc * 512 + hq * 128 + 128]
                        rhs = s["xn"][:, dc * T + t0 + sub * 512:
                                      dc * T + t0 + sub * 512 + 512]
                        wt = []
                        if dc == 0 and sub == 0 and hq == 0:
                            wt.append(("gd", gdg))
                        if dc == 0:
                            wt.append(("ac", ctx.now("ac")))
                        mmg = ctx.mm(gb, lhs_g, rhs, start=(dc == 0), stop=(dc == DC - 1),
                                     waits=wt)
                for sub in range(2):
                    ub = ctx.bankf(2 + sub)
                    for dc in range(DC):
                        lhs_u = wB[:, half + dc * 512 + hq * 128:
                                   half + dc * 512 + hq * 128 + 128]
                        rhs = s["xn"][:, dc * T + t0 + sub * 512:
                                      dc * T + t0 + sub * 512 + 512]
                        wt = []
                        if dc == 0 and sub == 0 and hq == 0:
                            wt.append(("gd", gdu))
                        if dc == 0:
                            wt.append(("ve", ctx.now("ve")))
                        mmu = ctx.mm(ub, lhs_u, rhs, start=(dc == 0), stop=(dc == DC - 1),
                                     waits=wt)
                a1 = ctx.act(sg[:, :], ctx.sb["PS"][:, 0:1024], AF.Silu, scale=INV127,
                             waits=(("mm", mmg),))
                h_done = ctx.stt(H[:, hc * 1024:(hc + 1) * 1024],
                                 ctx.sb["PS"][:, 1024:2048], INV127, sg[:, :],
                                 ALU.mult, ALU.mult, waits=(("mm", mmu), ("ac", a1)))
            wfree[blk % 2] = ctx.now("mm")
        dbank_rd = getattr(ctx, "_ffn_dbank", [0, 0])
        for dco in range(DC):
            src = d["Wd_t"].ap()[:, dco * 128:(dco + 1) * 128].rearrange(
                "(c p) o -> p c o", p=128)
            half = (dco % 2) * 3072
            w = (("mm", dfree[dco % 2]),) if dfree[dco % 2] else ()
            gdd = ctx.dma(wC[:, half:half + 3072], src, waits=w, eng="gpsimd")
            for sub in range(2):
                zb = ctx.bankf(4 + sub)
                mmd = 0
                for hc in range(24):
                    lhs = wC[:, half + hc * 128: half + hc * 128 + 128]
                    rhs = H[:, hc * 1024 + sub * 512: hc * 1024 + sub * 512 + 512]
                    wt = []
                    if hc == 0:
                        wt = [("gd", gdd), ("ve", h_done)]
                        if dbank_rd[sub]:
                            wt.append(("ve", dbank_rd[sub]))
                    mmd = ctx.mm(zb, lhs, rhs, start=(hc == 0), stop=(hc == 23), waits=wt)
                xc = s["x"][:, dco * T + t0 + sub * 512: dco * T + t0 + sub * 512 + 512]
                dbank_rd[sub] = ctx.stt(xc, zb, INV127, xc, ALU.mult, ALU.add,
                                        waits=(("mm", mmd),))
            dfree[dco % 2] = ctx.now("mm")
        ctx._ffn_dbank = dbank_rd
    ctx.barrier()
    ctx.pop_scope()


def emit_negB_l0(ctx):
    s = ctx.sb
    r = s["rowsA"]
    outs = []
    for i, (so, bo) in enumerate(((0, 64), (128, 192))):
        srow = s["rows64"][0:1, so:so + 64]
        brow = s["rows64"][0:1, bo:bo + 64]
        ms = r[0:1, 0:1]
        nb = r[0:1, 1:2]
        v1 = ctx.red(ms, srow, ALU.max, absv=True)
        ctx.tt(r[0:1, 4:68], brow, brow, ALU.mult)
        v3 = ctx.red(nb, r[0:1, 4:68], ALU.add)
        a1 = ctx.act(nb, nb, AF.Sqrt, waits=(("ve", v3),))
        out = r[0:1, 2 + i:3 + i]
        outs.append(ctx.stt(out, ms, 8.0, nb, ALU.mult, ALU.add,
                            waits=(("ac", a1), ("ve", v1))))
    ctx.tt(r[0:1, 0:1], r[0:1, 2:3], r[0:1, 3:4], ALU.mult, waits=(("ve", outs[1]),))
    vB = ctx.ts(r[0:1, 0:1], r[0:1, 0:1], -0.125, None, ALU.mult)
    m1 = ctx.mm(ctx.bankf(6, width=1), s["onesrowf"][:, :], r[0:1, 0:1], waits=(("ve", vB),))
    ctx.copy(s["negB"][:, 0:1], ctx.bankf(6, width=1), waits=(("mm", m1),))
    ctx.ts(s["smalls"][:, 0:1], s["qk128"][:, 0:1], 0.125, None, ALU.mult)
    ctx.ts(s["smalls"][:, 1:2], s["qk128"][:, 1:2], 0.125, None, ALU.mult)
    vb0 = ctx.copy(s["sqsl"][:, 0:24], s["bias8"][:, 0:24])
    mb0 = ctx.mm(ctx.bankf(5, width=24), s["consts"][:, 256:384], s["sqsl"][:, 0:24],
                 waits=(("ve", vb0),))
    ctx.stt(s["biasc"][:, 0:24], ctx.bankf(5, width=24), -1.0 / 64.0, s["bias8"][:, 0:24],
            ALU.mult, ALU.add, waits=(("mm", mb0),))
    ctx.barrier()


def _flush_outproj(ctx, wo_dma, wC, oall, npairs=2):
    s = ctx.sb
    oready = ctx.now("ve")
    bank_rd = [0, 0]
    for dco in range(DC):
        for sub in range(4):
            zb = ctx.bankf(sub % 2)
            mmz = 0
            for i in range(npairs):
                lhs = wC[:, i * 1024 + dco * 128: i * 1024 + dco * 128 + 128]
                rhs = oall[:, i * T + sub * 512: i * T + sub * 512 + 512]
                wt = []
                if i == 0:
                    if bank_rd[sub % 2]:
                        wt.append(("ve", bank_rd[sub % 2]))
                    if dco == 0 and sub == 0:
                        wt += [("ve", oready), ("gd", wo_dma), ("ac", ctx.now("ac"))]
                mmz = ctx.mm(zb, lhs, rhs, start=(i == 0), stop=(i == npairs - 1),
                             waits=wt)
            xc = s["x"][:, dco * T + sub * 512: dco * T + sub * 512 + 512]
            bank_rd[sub % 2] = ctx.stt(xc, zb, 1.0, xc, ALU.mult, ALU.add,
                                       waits=(("mm", mmz),))
    ctx.barrier()


def emit_l0_attn(ctx):
    s, d = ctx.sb, ctx.dr
    ctx.push_scope()
    kfull = ctx.alloc("kfull", [128, DC * 2 * T], BF16)
    s["kfull"] = kfull
    ph_assemble0(ctx)
    kraw = ctx.alloc("kraw", [128, 2 * T], BF16)
    vhat = ctx.alloc("vhat", [128, 2 * T], BF16)
    vtm = ctx.alloc("vtm", [128, 32 * 2 * 65], BF16)
    qhat = ctx.alloc("qhat", [128, T], BF16)
    oall = ctx.alloc("oall", [128, 2 * T], BF16)
    wA = ctx.alloc("attnWA", [128, 3 * 1024], BF16)
    wC = ctx.alloc("attnWC", [128, 2 * 1024], BF16)
    psb = ctx.alloc("psb", [128, 2 * 1024], BF16)

    negb = s["negB"][:, 0:1]
    bconst = s["consts"][:, 128:256]
    oconst = s["consts"][:, 256:384]
    evac_tok = [0, 0]
    sqw = [0, 0]
    exp_done = [0, 0]
    pv_done = [0, 0]
    o_norm = 0
    wo_dma = 0
    wo_free = 0

    def pipe(base, raw, dst, n_tok, bcol, scol, b2col, srcf, gwait):
        nonlocal evac_tok, sqw
        last_ve = 0
        prev_vc = 0
        prev_a3 = 0
        prev_ts = [0, 0]
        a1s = {}
        a2s = {}

        def stage_c(ss):
            nonlocal prev_vc
            blk = raw[:, ss * 512:(ss + 1) * 512]
            cb = ctx.bankf(2)
            mc = ctx.mm(cb, bconst, blk,
                        waits=[("ac", a1s[ss])] + ([("ve", prev_vc)] if prev_vc else []))
            vc = ctx.ts(blk, cb, bcol, None, ALU.add, waits=(("mm", mc),))
            prev_vc = vc
            sq = s["sqsl"][:, (ss % 2) * 512:(ss % 2) * 512 + 512]
            a2s[ss] = ctx.act(sq, blk, AF.Square,
                              waits=[("ve", vc)] +
                              ([("mm", sqw[ss % 2])] if sqw[ss % 2] else []))

        def stage_v(ss):
            nonlocal prev_a3, last_ve
            blk = raw[:, ss * 512:(ss + 1) * 512]
            sq = s["sqsl"][:, (ss % 2) * 512:(ss % 2) * 512 + 512]
            vb = ctx.bankf(3)
            mv = ctx.mm(vb, oconst, sq,
                        waits=[("ac", a2s[ss])] + ([("ac", prev_a3)] if prev_a3 else []))
            sqw[ss % 2] = mv
            rbs = s["rbc"][:, (ss % 2) * 512:(ss % 2) * 512 + 512]
            rss = s["rstdb"][:, (ss % 2) * 512:(ss % 2) * 512 + 512]
            a3 = ctx.act(rbs, vb, AF.Ln, scale=1.0 / 64.0,
                         bias=s["epsc"][:, 1:2],
                         waits=(("mm", mv), ("ve", prev_ts[ss % 2])))
            prev_a3 = ctx.act(rss, rbs, AF.Exp, scale=-0.5, waits=(("ac", a3),))
            ctx.tt(blk, blk, rss, ALU.mult, waits=(("ac", prev_a3),))
            last_ve = ctx.ts(dst[:, ss * 512:(ss + 1) * 512], blk,
                             scol, b2col, ALU.mult, ALU.add)
            prev_ts[ss % 2] = last_ve

        nss = n_tok // 512
        for ss in range(nss):
            pb = ctx.bankf(ss % 2)
            mmp = 0
            for dc in range(DC):
                lhs = wA[:, base + dc * 128: base + dc * 128 + 128]
                wt = []
                if dc == 0 and ss == 0:
                    wt.append(("gd", gwait))
                    wt.append(("ac", ctx.now("ac")))
                    wt.append(("ve", ctx.now("ve")))
                if dc == 0 and evac_tok[ss % 2]:
                    wt.append(("ac", evac_tok[ss % 2]))
                mmp = ctx.mm(pb, lhs, srcf(dc, ss), start=(dc == 0), stop=(dc == DC - 1),
                             waits=wt)
            a1s[ss] = ctx.act(raw[:, ss * 512:(ss + 1) * 512], pb, AF.Copy,
                              waits=(("mm", mmp),))
            evac_tok[ss % 2] = a1s[ss]
            if ss >= 1:
                stage_c(ss - 1)
            if ss >= 2:
                stage_v(ss - 2)
        stage_c(nss - 1)
        stage_v(nss - 2)
        stage_v(nss - 1)
        return last_ve

    _nhp = int(os.environ.get("KHP", str(NH // 2)))
    for hp in range(_nhp):
        wq = d["Wq"].ap()[:, hp * 128:(hp + 1) * 128].rearrange("(c p) o -> p c o", p=128)
        wk = d["Wk"].ap()[:, hp * 128:(hp + 1) * 128].rearrange("(c p) o -> p c o", p=128)
        wv = d["Wv"].ap()[:, hp * 128:(hp + 1) * 128].rearrange("(c p) o -> p c o", p=128)
        wo = d["Wo"].ap()[hp * 128:(hp + 1) * 128, :]
        wfree = ctx.now("mm")
        g1 = ctx.dma(wA[:, 0:1024], wq, waits=(("mm", wfree),), eng="gpsimd")
        g2 = ctx.dma(wA[:, 1024:2048], wk, eng="gpsimd")
        g3 = ctx.dma(wA[:, 2048:3072], wv, eng="gpsimd")
        wo_w = (("mm", wo_free),) if (hp % 2 == 0 and hp > 0) else ()
        g4 = ctx.dma(wC[:, (hp % 2) * 1024:(hp % 2) * 1024 + 1024], wo,
                     waits=wo_w, eng="gpsimd")
        if hp % 2 == 1:
            wo_dma = g4

        def ksrc(dc, ss):
            return kfull[:, dc * 2 * T + ss * 512: dc * 2 * T + ss * 512 + 512]

        kv_done = pipe(1024, kraw, kraw, 2 * T, s["biasc"][:, 8 + hp:9 + hp],
                       s["qk128"][:, 2:3], s["qk128"][:, 3:4], ksrc, g2)
        q_done = pipe(0, qhat, qhat, T, s["biasc"][:, 0 + hp:1 + hp],
                      s["smalls"][:, 0:1], s["smalls"][:, 1:2], ksrc, g1)
        v_done = pipe(2048, vhat, vhat, 2 * T, s["biasc"][:, 16 + hp:17 + hp],
                      s["qk128"][:, 4:5], s["qk128"][:, 5:6], ksrc, g3)
        ms1 = ctx.memset(vtm[:, :], 1.0, waits=(("mm", ctx.now("mm")),))
        pv8 = s["PSB"].ap().rearrange("p (j w) -> p j w", j=8)
        vv = vtm.ap().rearrange("p (c h w) -> p c h w", c=32, h=2)
        tr_ev = 0
        for kb in range(8):
            tr = 0
            for j in range(4):
                kc = kb * 4 + j
                wt = []
                if kb == 0 and j == 0:
                    wt.append(("ve", v_done))
                if kb >= 1 and j == 0:
                    wt.append(("ve", tr_ev))
                tr = ctx.trans(s["PSB"][:, j * 128:(j + 1) * 128],
                               vhat[:, kc * 128:(kc + 1) * 128],
                               s["consts"][:, 0:128], waits=wt)
            w0 = (("mm", tr), ("ve", ms1))
            ctx.copy(vv[:, kb * 4:kb * 4 + 4, 0, 0:64], pv8[:, 0:4, 0:64], waits=w0)
            tr_ev = ctx.copy(vv[:, kb * 4:kb * 4 + 4, 1, 0:64], pv8[:, 0:4, 64:128])
        vtm_done = tr_ev

        for qg in range(4):
            q0 = qg * 512
            ob = [ctx.bankf(4, p1=65), ctx.bankf(5, p1=65)]
            pend = None

            def emit_pv(kc, ea):
                nonlocal pv_done
                pb = kc % 2
                wv0 = [("ac", ea)]
                if kc == 0:
                    wv0.append(("ve", vtm_done))
                    if o_norm:
                        wv0.append(("ve", o_norm))
                ctx.mm(ob[0], vtm[:, (kc * 2) * 65:(kc * 2) * 65 + 65],
                       psb[:, pb * 1024:pb * 1024 + 512],
                       start=(kc == 0), stop=(kc == 31), waits=wv0)
                pv_done[pb] = ctx.mm(ob[1], vtm[:, (kc * 2 + 1) * 65:(kc * 2 + 1) * 65 + 65],
                                     psb[:, pb * 1024 + 512:pb * 1024 + 1024],
                                     start=(kc == 0), stop=(kc == 31))

            for kc in range(32):
                pb = kc % 2
                wt = []
                if kc == 0 and qg == 0:
                    wt += [("ve", q_done), ("ve", kv_done)]
                if exp_done[pb]:
                    wt.append(("ac", exp_done[pb]))
                ctx.mm(ctx.bankf(2 * pb), kraw[0:64, kc * 128:(kc + 1) * 128],
                       qhat[0:64, q0:q0 + 512], tp=(0, 0), waits=wt)
                m1 = ctx.mm(ctx.bankf(2 * pb + 1), kraw[64:128, kc * 128:(kc + 1) * 128],
                            qhat[64:128, q0:q0 + 512], tp=(64, 0))
                wexp = [("mm", m1)]
                if pv_done[pb]:
                    wexp.append(("mm", pv_done[pb]))
                ea = ctx.act(psb[:, pb * 1024:(pb + 1) * 1024],
                             ctx.sb["PS"][:, pb * 1024:(pb + 1) * 1024],
                             AF.Exp, bias=negb, waits=wexp)
                exp_done[pb] = ea
                if pend is not None:
                    emit_pv(*pend)
                pend = (kc, ea)
            emit_pv(*pend)
            for h in range(2):
                zrow = ctx.sb["PS"][64:65, (4 + h) * 512:(4 + h) * 512 + 512]
                rr = s["rowsA"][0:1, 0:512]
                ctx.ts(rr, zrow, 1e-38, None, ALU.add, waits=(("mm", pv_done[1]),))
                v2 = ctx.recip(rr, rr)
                bc = ctx.bankf(6, p1=64)
                m1 = ctx.mm(bc, s["onesrowf"][0:1, 0:64], rr, waits=(("ve", v2),))
                v3 = ctx.copy(s["rbc"][0:64, 0:512], bc, waits=(("mm", m1),))
                dst = oall[h * 64:h * 64 + 64, (hp % 2) * T + q0:(hp % 2) * T + q0 + 512]
                o_norm = ctx.tt(dst, ctx.sb["PS"][0:64, (4 + h) * 512:(4 + h) * 512 + 512],
                                s["rbc"][0:64, 0:512], ALU.mult,
                                waits=(("ve", v3), ("mm", getattr(ctx, "_flush_mm", 0))))
        if os.environ.get("KDBG") and ctx.mode == "B" and hp == _nhp - 1:
            ctx.barrier()
            ctx.dma(d["dbg_qh"].ap(), qhat[:, :])
            ctx.dma(d["dbg_kr"].ap(), kraw[:, :])
            ctx.dma(d["dbg_vt"].ap(), vtm[:, :])
            ctx.dma(d["dbg_nb"].ap(), s["negB"][:, 0:1])
            ctx.dma(d["dbg_oa"].ap(), oall[:, :])
            ctx.dma(d["dbg_wc"].ap(), wC[:, :])
            ctx.barrier()
        if hp % 2 == 1:
            _flush_outproj(ctx, wo_dma, wC, oall, npairs=2)
            exp_done = [0, 0]
            pv_done = [0, 0]
            o_norm = 0
            evac_tok = [0, 0]
            sqw = [0, 0]
        wo_free = ctx.now("mm")
    for dco in range(DC):
        xc = s["x"][:, dco * T:(dco + 1) * T]
        ctx.ts(xc, xc, 1.0, s["bias8"][:, 24 + dco:25 + dco], ALU.mult, ALU.add)
    ctx.barrier()
    ctx.pop_scope()


def emit_router_S(ctx):
    s, d = ctx.sb, ctx.dr
    ctx.push_scope()
    wR = ctx.alloc("wR", [128, 64], BF16)
    t12 = ctx.alloc("t12", [128, 16], BF16)
    logfm = ctx.alloc("logfm", [8, T], F32)
    g1 = ctx.dma(wR[:, :], d["RW"].ap().rearrange("(c p) e -> p c e", p=128),
                 eng="gpsimd")
    lg_done = 0
    for sub in range(4):
        lb = ctx.bankf(0, p1=8)
        mml = 0
        for dc in range(DC):
            lhs = wR[:, dc * 8:(dc + 1) * 8]
            rhs = s["xn"][:, dc * T + sub * 512: dc * T + sub * 512 + 512]
            wt = []
            if dc == 0 and sub == 0:
                wt.append(("gd", g1))
            if dc == 0 and sub >= 1:
                wt.append(("ve", lg_done))
            mml = ctx.mm(lb, lhs, rhs, start=(dc == 0), stop=(dc == DC - 1), waits=wt)
        lg_done = ctx.ts(logfm[0:8, sub * 512:(sub + 1) * 512], lb,
                         s["bias8"][0:8, 32:33], None, ALU.add, waits=(("mm", mml),))
    s0_done = s1_done = 0
    prev_ev = 0
    for ti in range(NT):
        tb = ctx.sb["PS"][0:128, 512:512 + 8]
        wt = [("ve", lg_done)] if ti == 0 else [("ve", prev_ev)]
        mt = ctx.trans(tb, logfm[0:8, ti * 128:(ti + 1) * 128],
                       s["identf"][0:8, 0:8], waits=wt)
        lt = s["masktok"][:, 0:8]
        ctx.copy(lt, tb, waits=(("mm", mt),))
        m8 = s["masktok"][:, 8:16]
        ctx.max8(m8, lt)
        oh1 = s["smalls"][:, 8:16]
        ctx.ts(oh1, lt, m8[:, 0:1], None, ALU.is_ge)
        l2 = s["smalls"][:, 16:24]
        ctx.stt(l2, oh1, -1e9, lt, ALU.mult, ALU.add)
        oh2 = s["smalls"][:, 24:32]
        ctx.ts(oh2, l2, m8[:, 1:2], None, ALU.is_ge)
        dd = s["smalls"][:, 32:33]
        v5 = ctx.tt(dd, m8[:, 1:2], m8[:, 0:1], ALU.subtract)
        e2 = s["smalls"][:, 33:34]
        a1 = ctx.act(e2, dd, AF.Exp, waits=(("ve", v5),))
        p1 = s["smalls"][:, 34:35]
        ctx.ts(p1, e2, 1.0, None, ALU.add, waits=(("ac", a1),))
        ctx.recip(p1, p1)
        p2 = s["smalls"][:, 35:36]
        ctx.ts(p2, p1, -1.0, 1.0, ALU.mult, ALU.add)
        t1 = s["smalls"][:, 40:48]
        t2 = s["smalls"][:, 48:56]
        ctx.ts(t1, oh1, p1[:, 0:1], None, ALU.mult)
        ctx.ts(t2, oh2, p2[:, 0:1], None, ALU.mult)
        mrow = s["masktok"][:, 16 + ti * 8: 16 + ti * 8 + 8]
        ctx.tt(mrow, t1, t2, ALU.add)
        wb = (("mm", s1_done),) if ti > 0 else ()
        v12 = ctx.copy(t12[:, 0:8], t1, waits=wb)
        prev_ev = ctx.copy(t12[:, 8:16], t2)
        s0_done = ctx.mm(ctx.sb["PS"][0:8, 2 * 512:2 * 512 + 1], t12[:, 0:8],
                         s["onescol"][:, :], start=(ti == 0), stop=(ti == NT - 1),
                         waits=(("ve", v12),))
        s1_done = ctx.mm(ctx.sb["PS"][0:8, 3 * 512:3 * 512 + 1], t12[:, 8:16],
                         s["onescol"][:, :], start=(ti == 0), stop=(ti == NT - 1),
                         waits=(("ve", prev_ev),))
    ctx.copy(s["smalls"][0:8, 60:61], ctx.sb["PS"][0:8, 2 * 512:2 * 512 + 1],
             waits=(("mm", s1_done),))
    vS1 = ctx.copy(s["smalls"][0:8, 61:62], ctx.sb["PS"][0:8, 3 * 512:3 * 512 + 1])
    if ctx.mode == "full":
        g2 = ctx.dma(d["cc_S_in"].ap().rearrange("(e k) -> e k", e=8),
                     s["smalls"][0:8, 60:62], waits=(("ve", vS1),), eng="gpsimd")
        ctx.cc_S = ctx.coll("AllReduce", ALU.add, [d["cc_S_in"].ap()],
                            [d["cc_S_out"].ap()], waits=(("gd", g2),))
    if ctx.mode == "B":
        ctx.dma(d["Sp_out"].ap(), s["smalls"][0:8, 60:62], waits=(("ve", vS1),))
    ctx.barrier()
    ctx.pop_scope()


def emit_mask_final(ctx):
    s, d = ctx.sb, ctx.dr
    if ctx.mode == "full":
        gl = ctx.dma(s["smalls"][0:8, 62:64],
                     d["cc_S_out"].ap().rearrange("(e k) -> e k", e=8),
                     waits=(("cc", ctx.cc_S),))
        wS = (("ld", gl),)
    else:
        ctx.red(s["smalls"][0:8, 62:63], s["bias8"][0:8, 0:8], ALU.add)
        v2 = ctx.red(s["smalls"][0:8, 63:64], s["bias8"][0:8, 8:16], ALU.add)
        wS = (("ve", v2),)
    vf = ctx.tt(s["smalls"][0:8, 62:64], s["smalls"][0:8, 62:64], s["flagS"][:, :],
                ALU.mult, waits=wS)
    va = ctx.tt(s["masktok"][0:8, 16:18], s["masktok"][0:8, 16:18],
                s["smalls"][0:8, 62:64], ALU.add, waits=(("ve", vf),))
    vc = ctx.ts(s["masktok"][:, 16:24], s["masktok"][:, 16:24], CAP, None, ALU.min,
                waits=(("ve", va),))
    tr = vc
    for ti in range(NT):
        tb = ctx.sb["PS"][0:8, 512:512 + 128]
        mt = ctx.trans(tb, s["masktok"][:, 16 + ti * 8:16 + ti * 8 + 8],
                       s["identf"][:, :], waits=(("ve", tr),))
        tr = ctx.copy(s["maskfm"][0:8, ti * 128:(ti + 1) * 128], tb, waits=(("mm", mt),))
    ctx.barrier()


def emit_moe(ctx):
    s, d = ctx.sb, ctx.dr
    ctx.push_scope()
    Hm = ctx.alloc("moeH", [128, 8 * 1024], BF16)
    wA = ctx.alloc("moeWA", [128, 8 * 1024], BF16)
    wB = ctx.alloc("moeWB", [128, 8 * 1024], BF16)
    mbcb = ctx.alloc("moeMBCB", [128, 1024], BF16)
    ttmp = ctx.sb["sqsl"]
    b1sb = ctx.alloc("b1sb", [128, 8], F32)
    b2sb = ctx.alloc("b2sb", [8, D], BF16)
    mr0 = ctx.alloc("mrow0", [1, 1024], BF16)
    mmlast = 0
    bcfree = 0
    bc_ve = 0
    b1_free = 0
    ctx._moe_wfree = [0, 0]
    ctx._moe_w2free = [0, 0]
    ctx._moe_vt = [0, 0]
    for ei in range(NE):
        gb1 = ctx.dma(b1sb[:, :], d["b1t"].ap()[ei, :, :],
                      waits=(("ac", b1_free),) if b1_free else (), eng="gpsimd")
        for th in range(2):
            t0 = th * 1024
            mb_done = 0
            gmr = ctx.dma(mr0[:, :], s["maskfm"][ei:ei + 1, t0:t0 + 1024],
                          waits=(("mm", bcfree),) if bcfree else (), eng="gpsimd")
            for sub in range(2):
                bc = ctx.bankf(6)
                wt = [("gd", gmr)]
                if bc_ve:
                    wt.append(("ve", bc_ve))
                m1 = ctx.mm(bc, s["onesrow"][:, :],
                            mr0[0:1, sub * 512: sub * 512 + 512], waits=wt)
                bc_ve = ctx.copy(mbcb[:, sub * 512:(sub + 1) * 512], bc, waits=(("mm", m1),))
                mb_done = bc_ve
            bcfree = ctx.now("mm")
            blkidx = ei * 2 + th
            for wh in range(2):
                bi = (blkidx * 2 + wh) % 2
                w1s = d["w1"].ap()[ei, :, wh * 512:(wh + 1) * 512].rearrange(
                    "(c p) h -> p c h", p=128)
                wt = (("mm", ctx._moe_wfree[bi]),) if ctx._moe_wfree[bi] else ()
                gw1 = ctx.dma(wA[:, bi * 4096:bi * 4096 + 4096], w1s, waits=wt,
                              eng="gpsimd")
                for hq in range(4):
                    hc = wh * 4 + hq
                    gset = 2 * (hc % 2)
                    mmg = 0
                    for sub in range(2):
                        gb = ctx.bankf(gset + sub)
                        for dc in range(DC):
                            lhs = wA[:, bi * 4096 + dc * 512 + hq * 128:
                                      bi * 4096 + dc * 512 + hq * 128 + 128]
                            rhs = s["xn"][:, dc * T + t0 + sub * 512:
                                          dc * T + t0 + sub * 512 + 512]
                            wtm = []
                            if dc == 0 and sub == 0 and hq == 0:
                                wtm.append(("gd", gw1))
                            if dc == 0 and ctx._moe_vt[hc % 2]:
                                wtm.append(("ve", ctx._moe_vt[hc % 2]))
                            mmg = ctx.mm(gb, lhs, rhs, start=(dc == 0),
                                         stop=(dc == DC - 1), waits=wtm)
                    vt = ctx.tt(ttmp[:, :], ctx.sb["PS"][:, gset * 512:gset * 512 + 1024],
                                mbcb[:, :], ALU.mult, waits=(("mm", mmg), ("ve", mb_done)))
                    ctx._moe_vt[hc % 2] = vt
                    wg = [("ve", vt)]
                    if ei == 0 and th == 0 and hc == 0:
                        wg.append(("gd", gb1))
                    a1 = ctx.act(Hm[:, hc * 1024:(hc + 1) * 1024], ttmp[:, :], AF.Gelu,
                                 bias=b1sb[:, hc:hc + 1], waits=wg)
                    b1_free = a1
                    ctx.tt(Hm[:, hc * 1024:(hc + 1) * 1024],
                           Hm[:, hc * 1024:(hc + 1) * 1024], mbcb[:, :], ALU.mult,
                           waits=(("ac", a1),))
                ctx._moe_wfree[bi] = ctx.now("mm")
                mmlast = ctx.now("mm")
            h_done = ctx.now("ve")
            for zh in range(2):
                bi = (blkidx * 2 + zh) % 2
                w2s = d["w2"].ap()[ei, :, zh * 512:(zh + 1) * 512].rearrange(
                    "(c p) o -> p c o", p=128)
                wt2 = (("mm", ctx._moe_w2free[bi]),) if ctx._moe_w2free[bi] else ()
                gw2 = ctx.dma(wB[:, bi * 4096:bi * 4096 + 4096], w2s, waits=wt2,
                              eng="gpsimd")
                for dq in range(4):
                    dco = zh * 4 + dq
                    for sub in range(2):
                        zb = ctx.bankf(4 + (sub % 2))
                        mmz = 0
                        for hc in range(NE):
                            lhs = wB[:, bi * 4096 + hc * 512 + dq * 128:
                                      bi * 4096 + hc * 512 + dq * 128 + 128]
                            rhs = Hm[:, hc * 1024 + sub * 512: hc * 1024 + sub * 512 + 512]
                            wtm = []
                            if hc == 0:
                                wtm.append(("ve", ctx.now("ve")))
                            if hc == 0 and dq == 0 and sub == 0:
                                wtm += [("gd", gw2), ("ve", h_done)]
                            mmz = ctx.mm(zb, lhs, rhs, start=(hc == 0), stop=(hc == NE - 1),
                                         waits=wtm)
                        xc = s["x"][:, dco * T + t0 + sub * 512:
                                    dco * T + t0 + sub * 512 + 512]
                        ctx.stt(xc, zb, 1.0, xc, ALU.mult, ALU.add, waits=(("mm", mmz),))
                ctx._moe_w2free[bi] = ctx.now("mm")
                mmlast = ctx.now("mm")
    gb2 = ctx.dma(b2sb[:, :], d["b2e"].ap(), eng="gpsimd")
    bank_rd2 = [0, 0]
    for dco in range(DC):
        for sub in range(4):
            zb = ctx.bankf(sub % 2)
            wt = []
            if dco == 0 and sub == 0:
                wt = [("gd", gb2), ("mm", mmlast), ("ve", ctx.now("ve"))]
            if bank_rd2[sub % 2]:
                wt.append(("ve", bank_rd2[sub % 2]))
            mmz = ctx.mm(zb, b2sb[0:8, dco * 128:(dco + 1) * 128],
                         s["maskfm"][0:8, sub * 512:(sub + 1) * 512], waits=wt)
            xc = s["x"][:, dco * T + sub * 512: dco * T + sub * 512 + 512]
            bank_rd2[sub % 2] = ctx.stt(xc, zb, 1.0, xc, ALU.mult, ALU.add,
                                        waits=(("mm", mmz),))
    ctx.barrier()
    ctx.pop_scope()


def emit_qlr_prep(ctx):
    s, d = ctx.sb, ctx.dr
    ctx.push_scope()
    qraw = ctx.alloc("qraw", [128, T], BF16)
    wA = ctx.alloc("qlrWA", [128, 1024], BF16)
    ctx.ts(s["cscaled"][0:16, :], s["coreT"][0:16, :], 0.25, None, ALU.mult)
    evac = [0, 0]
    for hp in range(NH // 2):
        wqc = d["Wqkv"].ap()[:, hp * 128:(hp + 1) * 128].rearrange("(c p) o -> p c o", p=128)
        g1 = ctx.dma(wA[:, :], wqc, waits=(("mm", ctx.now("mm")),), eng="gpsimd")
        ap_done = 0
        for ss in range(4):
            pb = ctx.bankf(ss % 2)
            mmp = 0
            for dc in range(DC):
                lhs = wA[:, dc * 128:(dc + 1) * 128]
                rhs = s["xn"][:, dc * T + ss * 512: dc * T + ss * 512 + 512]
                wt = []
                if dc == 0 and ss == 0:
                    wt.append(("gd", g1))
                if dc == 0 and evac[ss % 2]:
                    wt.append(("ac", evac[ss % 2]))
                mmp = ctx.mm(pb, lhs, rhs, start=(dc == 0), stop=(dc == DC - 1), waits=wt)
            ap_done = ctx.act(qraw[:, ss * 512:(ss + 1) * 512], pb, AF.Copy,
                              waits=(("mm", mmp), ("ve", ctx.now("ve"))))
            evac[ss % 2] = ap_done
        for h01 in range(2):
            h = hp * 2 + h01
            rb = h01 * 64
            for ss in range(4):
                lb = ctx.sb["PS"][0:16, 2 * 512: 2 * 512 + 512]
                mq = ctx.mm(lb, s["lsr"][rb:rb + 64, h * 16: h * 16 + 16],
                            qraw[rb:rb + 64, ss * 512:(ss + 1) * 512],
                            waits=(("ac", ap_done), ("ve", ctx.now("ve"))),
                            tp=(rb, 0) if rb else None)
                dst = s["qlr"][rb:rb + 16, hp * T + ss * 512: hp * T + ss * 512 + 512]
                ctx.ts(dst, lb, s["cscaled"][0:16, h:h + 1] if rb == 0 else
                       s["cscaled"][0:16, h:h + 1], None, ALU.mult, waits=(("mm", mq),))
    ctx.barrier()
    ctx.pop_scope()


def emit_l1_attn(ctx):
    s, d = ctx.sb, ctx.dr
    ctx.push_scope()
    slab = ctx.alloc("slab", [128, 2 * 4096], BF16)
    klr = ctx.alloc("klr", [128, 2 * 4096], BF16)
    kraw = ctx.alloc("kraw1", [128, 4096], BF16)
    vtm4 = ctx.alloc("vtm4", [128, 32 * 4 * 65], BF16)
    oall = ctx.alloc("oall1", [128, 2 * T], BF16)
    wAk = ctx.alloc("wAk", [128, 2 * 1024], BF16)
    wAv = ctx.alloc("wAv", [128, 2 * 1024], BF16)
    wC = ctx.alloc("l1WC", [128, 2 * 1024], BF16)
    psb = ctx.alloc("psb1", [128, 1024], BF16)

    if ctx.mode == "C":
        kf_src = d["kf1_in"].ap()
    else:
        kf_src = d["kf1_dram"].ap()
    kfv = kf_src.rearrange("p (c g w) -> p c g w", c=DC, g=2 * NT)

    exp_done = [0, 0]
    pv_done = [0, 0]
    o_norm = 0
    for quad in range(4):
        wv_src = d["Wqkv"].ap()[:, 2 * D + quad * 256: 2 * D + (quad + 1) * 256].rearrange(
            "(c p) o -> p c o", p=128)
        gv = ctx.dma(wAv[:, :], wv_src, waits=(("mm", ctx.now("mm")),), eng="gpsimd")
        for pr in range(2):
            hp = quad * 2 + pr
            wk_src = d["Wqkv"].ap()[:, D + hp * 128: D + hp * 128 + 128].rearrange(
                "(c p) o -> p c o", p=128)
            ctx.dma(wAk[:, pr * 1024:(pr + 1) * 1024], wk_src,
                    waits=(("mm", ctx.now("mm")),), eng="gpsimd")
        gwk = ctx.now("gd")
        ms1 = ctx.memset(vtm4[:, :], 1.0, waits=(("mm", ctx.now("mm")),))
        slab_free = [0, 0]
        kevac = [0, 0]
        klr_done = 0
        vdone = 0
        for ss in range(8):
            sl = slab[:, (ss % 2) * 4096:(ss % 2) * 4096 + 4096]
            wt = [("mm", slab_free[ss % 2])] if slab_free[ss % 2] else []
            gsl = ctx.dma(sl, kfv[:, :, 4 * ss:4 * ss + 4, :], waits=wt, eng="gpsimd")
            for pr in range(2):
                pb = ctx.bankf(pr)
                mmp = 0
                for dc in range(DC):
                    lhs = wAk[:, pr * 1024 + dc * 128: pr * 1024 + dc * 128 + 128]
                    rhs = sl[:, dc * 512:(dc + 1) * 512]
                    wtm = []
                    if dc == 0:
                        wtm.append(("gd", gsl))
                        if ss == 0:
                            wtm.append(("gd", gwk))
                            wtm.append(("ac", ctx.now("ac")))
                        if kevac[pr]:
                            wtm.append(("ac", kevac[pr]))
                    mmp = ctx.mm(pb, lhs, rhs, start=(dc == 0), stop=(dc == DC - 1),
                                 waits=wtm)
                ke = ctx.act(kraw[:, ss * 512:(ss + 1) * 512], pb, AF.Copy,
                             waits=(("mm", mmp), ("ve", klr_done)))
                kevac[pr] = ke
                for h01 in range(2):
                    rb = h01 * 64
                    lb = ctx.sb["PS"][0:16, 2 * 512: 2 * 512 + 512]
                    hh = (quad * 2 + pr) * 2 + h01
                    mq = ctx.mm(lb, s["lsr"][rb:rb + 64, 256 + hh * 16:
                                             256 + hh * 16 + 16],
                                kraw[rb:rb + 64, ss * 512:(ss + 1) * 512],
                                waits=(("ac", ke), ("ve", ctx.now("ve"))),
                                tp=(rb, 0) if rb else None)
                    dst = klr[rb:rb + 16, pr * 4096 + ss * 512: pr * 4096 + ss * 512 + 512]
                    klr_done = ctx.copy(dst, lb, waits=(("mm", mq),))
            for j in range(4):
                kc = ss * 4 + j
                vb = ctx.bankf(3, width=256)
                mmv = 0
                for dc in range(DC):
                    lhs = sl[:, dc * 512 + j * 128: dc * 512 + j * 128 + 128]
                    rhs = wAv[:, dc * 256:(dc + 1) * 256]
                    wtm = []
                    if dc == 0:
                        if kc == 0:
                            wtm += [("gd", gv), ("ve", ms1)]
                        if vdone:
                            wtm.append(("ve", vdone))
                    mmv = ctx.mm(vb, lhs, rhs, start=(dc == 0), stop=(dc == DC - 1),
                                 waits=wtm)
                pvv = ctx.sb["PS"].ap()[:, 3 * 512:3 * 512 + 256].rearrange(
                    "p (h w) -> p h w", h=4)
                vv = vtm4.ap().rearrange("p (c h w) -> p c h w", c=32, h=4)
                vdone = ctx.copy(vv[:, kc, :, 0:64], pvv, waits=(("mm", mmv),))
            slab_free[ss % 2] = ctx.now("mm")
        for h4 in range(4):
            h = quad * 4 + h4
            hp = quad * 2 + h4 // 2
            rb = (h4 % 2) * 64
            outs = []
            for idx, (buf, off, n_tok) in enumerate(
                    ((s["qlr"], (quad * 2 + h4 // 2) * T, T),
                     (klr, (h4 // 2) * 4096, 2 * T))):
                sqb = kraw[rb:rb + 16, 0:2048]
                nss = n_tok // 512
                mx = s["rowsA"][0:1, 64 + h4 * 2 + idx: 64 + h4 * 2 + idx + 1]
                first = True
                for half in range((n_tok + 2047) // 2048):
                    a1 = ctx.act(sqb, buf[rb:rb + 16, off + half * 2048: off + half * 2048 + 2048],
                                 AF.Square, waits=(("ve", ctx.now("ve")),))
                    for ssq in range(4):
                        bnk = ctx.bankf(2 + (ssq % 2), p1=1)
                        mmn = ctx.mm(bnk, s["onescol"][rb:rb + 16, :],
                                     sqb[:, ssq * 512:(ssq + 1) * 512],
                                     waits=(("ac", a1), ("ve", ctx.now("ve"))),
                                     tp=(rb, 0) if rb else None)
                        tmp = s["rowsA"][0:1, 60:61]
                        ctx.red(tmp, bnk, ALU.max, waits=(("mm", mmn),))
                        if first:
                            ctx.copy(mx, tmp)
                            first = False
                        else:
                            ctx.tt(mx, mx, tmp, ALU.max)
                outs.append(mx)
            pr_ = s["rowsA"][0:1, 61:62]
            v1 = ctx.tt(pr_, outs[0], outs[1], ALU.mult)
            a2 = ctx.act(pr_, pr_, AF.Sqrt, waits=(("ve", v1),))
            v2 = ctx.ts(pr_, pr_, -1.0, None, ALU.mult, waits=(("ac", a2),))
            m1 = ctx.mm(ctx.bankf(6, width=1), s["onesrowf"][:, :], pr_,
                        waits=(("ve", v2), ("ve", ctx.now("ve"))))
            ctx.copy(s["negB"][:, 2 + h:3 + h], ctx.bankf(6, width=1), waits=(("mm", m1),))
        wo1 = d["Wo1"].ap()[quad * 256:(quad + 1) * 256, :].rearrange(
            "(i p) o -> p i o", p=128)
        wo_dma = ctx.dma(wC[:, :], wo1, waits=(("mm", ctx.now("mm")),), eng="gpsimd")
        for h4 in range(4):
            h = quad * 4 + h4
            rb = (h4 % 2) * 64
            pr = h4 // 2
            negb1 = s["negB"][:, 2 + h:3 + h]
            for qg in range(8):
                q0 = qg * 256
                kmax = 4 * qg + 4
                ob = ctx.bankf(4, p1=65, width=256)
                pend = None

                def emit_pv1(kc2, ea, mask_done):
                    nonlocal pv_done
                    g2i = (kc2 // 2) % 2
                    for j in range(2):
                        kc = kc2 + j
                        wv0 = [("ac", ea)]
                        if mask_done:
                            wv0.append(("ve", mask_done))
                        if kc == 0 and o_norm:
                            wv0.append(("ve", o_norm))
                        pv_done[g2i] = ctx.mm(
                            ob, vtm4[:, (kc * 4 + h4) * 65:(kc * 4 + h4) * 65 + 65],
                            psb[:, g2i * 512 + j * 256: g2i * 512 + j * 256 + 256],
                            start=(kc == 0), stop=(kc == kmax - 1), waits=wv0)

                for kc2 in range(0, kmax, 2):
                    g2i = (kc2 // 2) % 2
                    wt = []
                    if exp_done[g2i]:
                        wt.append(("ac", exp_done[g2i]))
                    if kc2 == 0 and qg == 0 and h4 == 0:
                        wt.append(("ve", vdone))
                        wt.append(("ve", ctx.now("ve")))
                    ctx.mm(ctx.bankf(g2i, lo=0, width=256),
                           klr[rb:rb + 16, pr * 4096 + kc2 * 128: pr * 4096 + kc2 * 128 + 128],
                           s["qlr"][rb:rb + 16, (quad * 2 + pr) * T + q0:
                                    (quad * 2 + pr) * T + q0 + 256],
                           tp=(rb, 0) if rb else None, waits=wt)
                    m1 = ctx.mm(ctx.bankf(g2i, lo=256, width=256),
                                klr[rb:rb + 16, pr * 4096 + kc2 * 128 + 128:
                                    pr * 4096 + kc2 * 128 + 256],
                                s["qlr"][rb:rb + 16, (quad * 2 + pr) * T + q0:
                                         (quad * 2 + pr) * T + q0 + 256],
                                tp=(rb, 0) if rb else None)
                    wexp = [("mm", m1)]
                    if pv_done[g2i]:
                        wexp.append(("mm", pv_done[g2i]))
                    psl = psb[:, g2i * 512:(g2i + 1) * 512]
                    ea = ctx.act(psl, ctx.bankf(g2i), AF.Exp, bias=negb1, waits=wexp)
                    exp_done[g2i] = ea
                    mask_done = 0
                    for j in range(2):
                        kc = kc2 + j
                        i = kc - (kmax - 4)
                        if i >= 0:
                            msl = psb[:, g2i * 512 + j * 256: g2i * 512 + j * 256 + 256]
                            mask_done = ctx.tt(msl, msl,
                                               s["cmask"][:, i * 256:(i + 1) * 256],
                                               ALU.mult, waits=(("ac", ea),))
                    if pend is not None:
                        emit_pv1(*pend)
                    pend = (kc2, ea, mask_done)
                emit_pv1(*pend)
                zrow = ctx.sb["PS"][64:65, 4 * 512: 4 * 512 + 256]
                rr = s["rowsA"][0:1, 0:256]
                ctx.ts(rr, zrow, 1e-38, None, ALU.add,
                       waits=(("mm", pv_done[0] or pv_done[1]),
                              ("mm", pv_done[1] or pv_done[0])))
                v2 = ctx.recip(rr, rr)
                bc = ctx.bankf(6, p1=64, width=256)
                m1 = ctx.mm(bc, s["onesrowf"][0:1, 0:64], rr, waits=(("ve", v2),))
                v3 = ctx.copy(s["rbc"][0:64, 0:256], bc, waits=(("mm", m1),))
                dst = oall[rb:rb + 64, pr * T + q0: pr * T + q0 + 256]
                o_norm = ctx.tt(dst, ctx.sb["PS"][0:64, 4 * 512:4 * 512 + 256],
                                s["rbc"][0:64, 0:256], ALU.mult,
                                waits=(("ve", v3), ("mm", getattr(ctx, "_flush_mm", 0))))
        if os.environ.get("KDBG") and ctx.mode == "C" and quad == int(os.environ.get("KDBGQ", "3")):
            ctx.barrier()
            ctx.dma(d["dbg_qlr"].ap(), s["qlr"][:, :])
            ctx.dma(d["dbg_klr"].ap(), klr[:, :])
            ctx.dma(d["dbg_vt4"].ap(), vtm4[:, :])
            ctx.dma(d["dbg_nb1"].ap(), s["negB"][:, :])
            ctx.dma(d["dbg_oa1"].ap(), oall[:, :])
            ctx.barrier()
        _flush_outproj(ctx, wo_dma, wC, oall, npairs=2)
        exp_done = [0, 0]
        pv_done = [0, 0]
        o_norm = 0
    ctx.barrier()
    ctx.pop_scope()


def ph_store(ctx):
    s, d = ctx.sb, ctx.dr
    if ctx.mode == "A":
        g = ctx.dma(d["xn_out"].ap(), s["xn"][:, :])
        ctx.p.op("sync", lambda e: e.nop(), waits=(("ld", g),))
        return
    if ctx.mode == "B":
        ctx.dma(d["x1_out"].ap().rearrange("(c p) t -> p c t", p=128), s["x"][:, :])
        g2 = ctx.dma(d["xn2_out"].ap(), s["xn"][:, :])
        ctx.p.op("sync", lambda e: e.nop(), waits=(("ld", g2),))
        return
    g = ctx.dma(d["y"].ap().rearrange("(c p) t -> p c t", p=128), s["x"][:, :])
    ctx.p.op("sync", lambda e: e.nop(), waits=(("ld", g),))


def emit_all(ctx):
    ph_load(ctx)
    if ctx.mode == "A":
        ctx.sb["xn"] = ctx.alloc("xn", [128, DC * T], BF16)
        emit_ln(ctx)
        ph_store(ctx)
        return
    if ctx.mode == "B":
        upto = int(os.environ.get("KUPTO", "9"))
        ctx.push_scope()
        ctx.sb["xn"] = ctx.alloc("xn", [128, DC * T], BF16)
        ctx.dma(ctx.sb["xn"][:, :], ctx.dr["xn_own_in"].ap())
        ctx.barrier()
        if upto >= 1:
            emit_negB_l0(ctx)
        if upto >= 2:
            emit_ffn0(ctx)
        ctx.pop_scope()
        if upto >= 3:
            emit_l0_attn(ctx)
        ctx.push_scope()
        ctx.sb["xn"] = ctx.alloc("xn", [128, DC * T], BF16)
        if upto >= 4:
            emit_ln(ctx)
        if upto >= 5:
            emit_router_S(ctx)
        ph_store(ctx)
        ctx.pop_scope()
        return
    if ctx.mode == "C":
        ctx.push_scope()
        ctx.sb["qlr"] = ctx.alloc("qlr", [128, 8 * T], BF16)
        ctx.push_scope()
        ctx.sb["xn"] = ctx.alloc("xn", [128, DC * T], BF16)
        ctx.sb["maskfm"] = ctx.alloc("maskfm", [8, T], BF16)
        ctx.dma(ctx.sb["xn"][:, :], ctx.dr["xn2own"].ap())
        ctx.barrier()
        emit_router_S(ctx)
        emit_mask_final(ctx)
        emit_qlr_prep(ctx)
        if os.environ.get("KC_MOE", "1") == "1":
            emit_moe(ctx)
        ctx.pop_scope()
        if os.environ.get("KC_ATTN", "1") == "1":
            emit_l1_attn(ctx)
        ctx.pop_scope()
        ph_store(ctx)
        return
    up = int(os.environ.get("KF_UP", "9"))
    ctx.push_scope()
    ctx.sb["xn"] = ctx.alloc("xn", [128, DC * T], BF16)
    emit_ln(ctx)
    emit_gather_issue(ctx, 0)
    emit_negB_l0(ctx)
    if up >= 1:
        emit_ffn0(ctx)
    ctx.pop_scope()
    if up >= 2:
        emit_l0_attn(ctx)
    if up >= 3:
        ctx.push_scope()
        ctx.sb["qlr"] = ctx.alloc("qlr", [128, 8 * T], BF16)
        ctx.push_scope()
        ctx.sb["xn"] = ctx.alloc("xn", [128, DC * T], BF16)
        ctx.sb["maskfm"] = ctx.alloc("maskfm", [8, T], BF16)
        emit_ln(ctx)
        emit_router_S(ctx)
        emit_gather_issue(ctx, 1)
        emit_mask_final(ctx)
        if os.environ.get("KDBG"):
            ctx.dma(ctx.dr["dbg_sm"].ap(), ctx.sb["smalls"][:, :])
            ctx.dma(ctx.dr["dbg_mt"].ap(), ctx.sb["masktok"][:, :])
            ctx.barrier()
        emit_qlr_prep(ctx)
        if up >= 4:
            emit_moe(ctx)
        ph_assemble1(ctx)
        ctx.pop_scope()
        if up >= 5:
            emit_l1_attn(ctx)
        ctx.pop_scope()
    ph_store(ctx)


def build_program(mode):
    nc = bass.Bass("TRN2", target_bir_lowering=False)
    ctx = Ctx(nc, mode)
    declare_dram(ctx)
    root = ExitStack()
    ctx.scopes = [root]
    with root:
        declare_sbuf_persistent(ctx)
        sems = {}
        for snm in ("mm", "ve", "ac", "ld", "gd", "cc"):
            sems[snm] = root.enter_context(nc.semaphore(snm))
        block = root.enter_context(nc.Block())
        emit_all(ctx)
        with nc.allow_low_precision(reason="bf16 operands with fp32 psum accumulation by design"):
            ctx.p.emit(block, sems)
    return nc


def kernel(**inputs):
    w = host_weights(inputs)
    shared = host_constants()
    x0 = np.asarray(inputs["L0_attn_x"], np.float32)

    nc = build_program("full")
    maps = []
    for core in range(NCORES):
        dd = dict(shared)
        dd.update(core_constants(core))
        dd["x0"] = _f32(x0[core // 2][own_rows(core)].T)
        for knm in ("Wq", "Wk", "Wv", "Wo", "bq", "bk", "bv", "bo",
                    "Wg_t", "Wu_t", "Wd_t", "RW", "Rb",
                    "Wqkv", "Wo1", "Wqlsr", "Wklsr", "coreT",
                    "w1", "w2", "b1t", "b2e"):
            dd[knm] = w[knm]
        for nm in ("q", "k", "v"):
            for suf in ("s128", "b128", "srow", "brow"):
                dd[f"{nm}{suf}"] = w[f"{nm}{suf}"]
        maps.append(dd)
    res = run_bass_kernel_spmd(nc, maps, core_ids=list(range(NCORES)))

    out = np.empty((B, S, D), dtype=np.float32)
    for core in range(NCORES):
        b = core // 2
        p = core % 2
        y = np.asarray(res.results[core]["y"], np.float32).T
        for j in range(NT):
            out[b, (2 * j + p) * 128:(2 * j + p + 1) * 128, :] = y[j * 128:(j + 1) * 128]
    return out
